# version 11
# speedup vs baseline: 1.4928x; 1.0164x over previous
import numpy as np
import concourse.bacc as bacc
import concourse.mybir as mybir
from concourse.alu_op_type import AluOpType as OP
from concourse.bass_utils import run_bass_kernel_spmd

F32 = mybir.dt.float32
BF16 = mybir.dt.bfloat16
I32 = mybir.dt.int32
AF = mybir.ActivationFunctionType
AX = mybir.AxisListType.X

NCORE = 8
D = 2048
T = 1024
NT = 8
SEQ = 8192
DIN = 8512
DSSM = 4096
NH = 64
HD = 64
DS = 128
HID = 8192
CH = 128
NCH = 8
EPS = 1e-5
NOB_IN = 67
NIB_D = 16
LN127 = float(np.log(np.float32(127.0)))


class Sem:
    def __init__(self, nc, name):
        self._cm = nc.semaphore(name)
        self.h = self._cm.__enter__()
        self.n = 0

    def inc(self, instr, k=1):
        instr.then_inc(self.h, k)
        self.n += k
        return instr


def bc64(col_ap, nh):
    return col_ap[:, :, None].broadcast_to([128, nh, HD])


def dma(eng, sem, out, in_, **kw):
    eng.wait_ge(sem.h, sem.n)
    return sem.inc(eng.dma_start(out=out, in_=in_, **kw), 16)


def block_w(w):
    O, I = w.shape
    Opad = ((O + 127) // 128) * 128
    wp = np.zeros((Opad, I), np.float32)
    wp[:O] = w
    r = wp.reshape(Opad // 128, 128, I // 128, 128)
    return np.ascontiguousarray(r.transpose(0, 3, 2, 1))


def prep_in_maps(inputs):
    u_full = np.asarray(inputs["L0_attn_u"], np.float32).reshape(SEQ, D)
    w_in_b = block_w(np.asarray(inputs["L0_attn_W_in"], np.float32))
    w_out_b = block_w(np.asarray(inputs["L0_attn_W_out"], np.float32))
    w_g_b = block_w(np.asarray(inputs["L0_ffn_gate_w"], np.float32))
    w_v_b = block_w(np.asarray(inputs["L0_ffn_val_w"], np.float32))
    w_o_b = block_w(np.asarray(inputs["L0_ffn_out_w"], np.float32))
    conv_w = np.asarray(inputs["L0_attn_conv_w"], np.float32).reshape(4352, 4)
    conv_w_p = np.ascontiguousarray(conv_w.reshape(34, 128, 4).transpose(1, 0, 2))
    conv_b_p = np.ascontiguousarray(
        np.asarray(inputs["L0_attn_conv_b"], np.float32).reshape(34, 128).T)
    dtb = np.asarray(inputs["L0_attn_dt_bias"], np.float32).reshape(NH, 1)
    alog = np.asarray(inputs["L0_attn_A_log"], np.float32).reshape(NH, 1)
    dflat = np.asarray(inputs["L0_attn_D"], np.float32).reshape(1, NH)
    nwflat = np.asarray(inputs["L0_attn_norm_w"], np.float32).reshape(1, DSSM)
    norm_w_p = np.ascontiguousarray(
        np.asarray(inputs["L0_attn_norm_w"], np.float32).reshape(32, 128).T)
    gb_p = np.ascontiguousarray(
        np.asarray(inputs["L0_ffn_gate_b"], np.float32).reshape(64, 128).T)
    vb_p = np.ascontiguousarray(
        np.asarray(inputs["L0_ffn_val_b"], np.float32).reshape(64, 128).T)
    ob_p = np.ascontiguousarray(
        np.asarray(inputs["L0_ffn_out_b"], np.float32).reshape(16, 128).T)

    in_maps = []
    for c in range(NCORE):
        t0 = c * T
        halo = np.zeros((128, D), np.float32)
        if c > 0:
            halo[0:3] = u_full[t0 - 3:t0]
        mask = np.zeros((8, 1), np.float32)
        mask[:c] = 1.0
        in_maps.append({
            "u": np.ascontiguousarray(u_full[t0:t0 + T]),
            "u_halo": halo,
            "gam_g": np.ascontiguousarray(w_g_b[8 * c:8 * c + 8]),
            "gam_v": np.ascontiguousarray(w_v_b[8 * c:8 * c + 8]),
            "gam_o": np.ascontiguousarray(w_o_b[2 * c:2 * c + 2]),
            "w_in": w_in_b, "w_out": w_out_b,
            "w_g": w_g_b, "w_v": w_v_b, "w_o": w_o_b,
            "conv_w": conv_w_p, "conv_b": conv_b_p,
            "dtb": dtb, "alog": alog, "dflat": dflat,
            "nwflat": nwflat, "norm_w_p": norm_w_p,
            "gb": gb_p, "vb": vb_p, "obias": ob_p,
            "mask_lt": mask,
        })
    return in_maps


def kernel(**inputs):
    in_maps = prep_in_maps(inputs)
    nc, out_names = build(debug=False)
    res = run_bass_kernel_spmd(nc, in_maps, core_ids=list(range(NCORE)))
    out = np.concatenate([res.results[c]["out"] for c in range(NCORE)], axis=0)
    return out.reshape(1, SEQ, D)


def build(debug=False):
    nc = bacc.Bacc("TRN2", target_bir_lowering=False, debug=False)
    dbg_outs = []

    d_u = nc.dram_tensor("u", [T, D], F32, kind="ExternalInput").ap()
    d_uh = nc.dram_tensor("u_halo", [128, D], F32, kind="ExternalInput").ap()
    d_win = nc.dram_tensor("w_in", [NOB_IN, 128, NIB_D, 128], F32, kind="ExternalInput").ap()
    d_wout = nc.dram_tensor("w_out", [16, 128, 32, 128], F32, kind="ExternalInput").ap()
    d_wg = nc.dram_tensor("w_g", [64, 128, 16, 128], F32, kind="ExternalInput").ap()
    d_wv = nc.dram_tensor("w_v", [64, 128, 16, 128], F32, kind="ExternalInput").ap()
    d_wo = nc.dram_tensor("w_o", [16, 128, 64, 128], F32, kind="ExternalInput").ap()
    d_cw = nc.dram_tensor("conv_w", [128, 34, 4], F32, kind="ExternalInput").ap()
    d_cb = nc.dram_tensor("conv_b", [128, 34], F32, kind="ExternalInput").ap()
    d_dtb = nc.dram_tensor("dtb", [NH, 1], F32, kind="ExternalInput").ap()
    d_alog = nc.dram_tensor("alog", [NH, 1], F32, kind="ExternalInput").ap()
    d_dflat = nc.dram_tensor("dflat", [1, NH], F32, kind="ExternalInput").ap()
    d_nwflat = nc.dram_tensor("nwflat", [1, DSSM], F32, kind="ExternalInput").ap()
    d_nwp = nc.dram_tensor("norm_w_p", [128, 32], F32, kind="ExternalInput").ap()
    d_gb = nc.dram_tensor("gb", [128, 64], F32, kind="ExternalInput").ap()
    d_vb = nc.dram_tensor("vb", [128, 64], F32, kind="ExternalInput").ap()
    d_obias = nc.dram_tensor("obias", [128, 16], F32, kind="ExternalInput").ap()
    d_mask = nc.dram_tensor("mask_lt", [8, 1], F32, kind="ExternalInput").ap()
    d_gg = nc.dram_tensor("gam_g", [8, 128, 16, 128], F32, kind="ExternalInput").ap()
    d_gv = nc.dram_tensor("gam_v", [8, 128, 16, 128], F32, kind="ExternalInput").ap()
    d_go = nc.dram_tensor("gam_o", [2, 128, 64, 128], F32, kind="ExternalInput").ap()
    d_out = nc.dram_tensor("out", [T, D], F32, kind="ExternalOutput").ap()

    d_ztm = nc.dram_tensor("sp_ztm", [32, NT, 128, 128], F32).ap()
    d_xtm = nc.dram_tensor("sp_xtm", [32, NT, 128, 128], F32).ap()
    d_chain = nc.dram_tensor("sp_chain", [NCH, 128, DSSM], F32).ap()
    d_ynsp = nc.dram_tensor("sp_yn", [NCH, 128, 2, 32, 128], BF16).ap()
    d_vsp = nc.dram_tensor("sp_v", [64, 128, T], F32).ap()
    d_ssp = nc.dram_tensor("sp_s", [16, 128, T], F32).ap()
    d_qxn = nc.dram_tensor("sp_qxn", [128, NIB_D, T], BF16).ap()
    d_small = nc.dram_tensor("sp_small", [2 * NCH, NH], F32).ap()
    t_ccin = nc.dram_tensor("cc_in", [129, DSSM], F32)
    t_ccout = nc.dram_tensor("cc_out", [129 * NCORE, DSSM], F32,
                             addr_space="Shared")

    def dbg(name, shape, dtype=F32):
        h = nc.dram_tensor(name, list(shape), dtype, kind="ExternalOutput").ap()
        dbg_outs.append(name)
        return h

    sems = {}

    def S(name):
        if name not in sems:
            sems[name] = Sem(nc, name)
        return sems[name]

    s_ld = S("ld")
    s_st = S("st")
    s_v = S("v")
    s_sc = S("sc")
    s_mm = S("mm")
    s_gp = S("gp")
    s_cc = S("cc")

    ctx = []

    def sb(name, shape, dtype):
        cm = nc.sbuf_tensor(name, list(shape), dtype)
        t = cm.__enter__()
        ctx.append((cm, t))
        return t

    def ps(name, shape, dtype=F32):
        cm = nc.psum_tensor(name, list(shape), dtype)
        t = cm.__enter__()
        ctx.append((cm, t))
        return t

    blk_cm = nc.Block()
    block = blk_cm.__enter__()

    _DRAIN_OPS = {
        "tensor_scalar", "tensor_tensor", "tensor_copy", "tensor_reduce",
        "scalar_tensor_tensor", "tensor_tensor_scan", "reciprocal", "memset",
        "activation", "copy", "add", "mul", "select", "affine_select",
        "partition_broadcast", "iota", "transpose", "reduce_sum", "reduce_max",
        "tensor_add", "tensor_sub", "tensor_mul", "tensor_max",
        "tensor_scalar_add", "tensor_scalar_mul", "bn_stats", "bn_aggr",
        "square", "sign", "sqrt",
    }

    def _ap_span(x):
        t = getattr(x, "tensor", None)
        if t is None or not hasattr(x, "offset"):
            return None
        try:
            lo = int(x.offset)
            span = 1
            for st, ct in list(x.ap):
                span += (int(ct) - 1) * abs(int(st))
            return (t.name, lo, lo + span)
        except Exception:
            return (t.name, 0, 1 << 60)

    _eng_state = {}

    class EW:
        def __init__(self, e):
            object.__setattr__(self, "_e", e)
            key = id(type(e)), getattr(e, "engine", None).__repr__() if hasattr(e, "engine") else type(e).__name__
            st = _eng_state.setdefault(type(e).__name__, {"w": [], "r": []})
            object.__setattr__(self, "_st", st)

        def __getattr__(self, n):
            a = getattr(self._e, n)
            if n not in _DRAIN_OPS:
                return a
            st = self._st

            def wrapped(*args, **kw):
                outs, ins = [], []
                if n == "memset" and args:
                    s = _ap_span(args[0])
                    if s: outs.append(s)
                else:
                    for x in args:
                        s = _ap_span(x)
                        if s: ins.append(s)
                for k2, x in kw.items():
                    s = _ap_span(x)
                    if s is None:
                        continue
                    (outs if ("out" in k2) else ins).append(s)

                def ovl(a_, lst):
                    for b_ in lst:
                        if a_[0] == b_[0] and a_[1] < b_[2] and b_[1] < a_[2]:
                            return True
                    return False

                hazard = any(ovl(i, st["w"]) for i in ins) or \
                    any(ovl(o, st["w"]) or ovl(o, st["r"]) for o in outs)
                if hazard:
                    self._e.drain()
                    st["w"].clear()
                    st["r"].clear()
                r = a(*args, **kw)
                st["w"].extend(outs)
                st["r"].extend(ins)
                return r
            return wrapped

    def BV(f):
        block.vector(lambda e: f(EW(e)))

    def BS(f):
        block.scalar(lambda e: f(EW(e)))

    def BG(f):
        block.gpsimd(lambda e: f(EW(e)))

    ident = sb("ident", [128, 128], F32)
    zscratch = sb("zscratch", [128, 128], F32)
    mask01 = sb("mask01", [128, 128], F32)
    nw_sb = sb("nw_sb", [128, 32], F32)
    identb = sb("identb", [128, 128], BF16)
    maskneg = sb("maskneg", [128, 128], F32)
    cw_sb = sb("cw_sb", [128, 34 * 4], F32)
    cb_sb = sb("cb_sb", [128, 34], F32)
    dtb_sb = sb("dtb_sb", [NH, 1], F32)
    alog_sb = sb("alog_sb", [NH, 1], F32)
    negA = sb("negA", [NH, 1], F32)
    drow = sb("drow", [1, NH], F32)
    drep = sb("drep", [128, NH], F32)
    gb_sb = sb("gb_sb", [128, 64], F32)
    vb_sb = sb("vb_sb", [128, 64], F32)
    obias_sb = sb("obias_sb", [128, 16], F32)
    mask_sb = sb("mask_sb", [8, 1], F32)
    constcol = sb("constcol", [128, 8], F32)
    _constvals = {}

    def ccol(val):
        if val not in _constvals:
            idx = len(_constvals)
            assert idx < 8
            _constvals[val] = idx
        i = _constvals[val]
        return constcol[:, i:i + 1]

    for _v_ in (EPS, LN127, 1.0, 0.5, 30.0):
        ccol(_v_)

    @BV
    def _(v):
        items = list(_constvals.items())
        for val, idx in items[:-1]:
            v.memset(constcol[:, idx:idx + 1], float(val))
        val, idx = items[-1]
        s_v.inc(v.memset(constcol[:, idx:idx + 1], float(val)))
    consts_done = s_v.n

    gam_part = sb("gam_part", [128, 4], F32)
    gx1_cols = sb("gx1_cols", [128, NT], F32)
    gsc = sb("gsc", [128, 8], F32)
    scan_cms = []

    def sbsc(name, shape, dtype):
        cm = nc.sbuf_tensor(name, list(shape), dtype)
        t = cm.__enter__()
        scan_cms.append(cm)
        return t

    B_sb = sbsc("B_sb", [128, T], F32)
    C_sb = sbsc("C_sb", [128, T], F32)
    cs_ext = sbsc("cs_ext", [NH, T + 8], F32)
    logdt = sbsc("logdt", [NH, T], F32)
    bb64 = sbsc("bb64", [NH, T], F32)
    ones64 = sbsc("ones64", [NH, T], F32)
    dt_s = sbsc("dt_s", [NH, T], F32)
    bbT = sbsc("bbT", [128, NCH * NH], F32)
    sclT = sbsc("sclT", [128, NCH * NH], F32)
    exprelT = sbsc("exprelT", [128, NCH * NH], F32)
    aend_rep = sbsc("aend_rep", [128, NCH * NH], F32)
    rels_rep = sbsc("rels_rep", [128, NCH * NH], F32)



    s_pld = S("pld")

    @block.sync
    def _(sy):
        for dst, src_ in [(cw_sb, d_cw), (cb_sb, d_cb), (dtb_sb, d_dtb),
                          (alog_sb, d_alog), (drow, d_dflat), (nw_sb, d_nwp),
                          (gb_sb, d_gb), (vb_sb, d_vb), (obias_sb, d_obias),
                          (mask_sb, d_mask)]:
            dma(sy, s_pld, dst[:], src_)
    n_param_ld = s_pld.n

    @BV
    def _(v):
        v.memset(zscratch[:], 0.0)
        v.memset(mask01[:], 1.0)
        s_v.inc(v.memset(ones64[:], 1.0))
    zt_done = s_v.n

    @BG
    def _(g):
        g.wait_ge(s_v.h, zt_done)
        g.affine_select(out=ident[:], in_=zscratch[:], compare_op=OP.not_equal,
                        fill=1.0, base=0, pattern=[[-1, 128]], channel_multiplier=1)
        g.affine_select(out=identb[:], in_=zscratch[:], compare_op=OP.not_equal,
                        fill=1.0, base=0, pattern=[[-1, 128]], channel_multiplier=1)
        g.affine_select(out=maskneg[:], in_=zscratch[:], compare_op=OP.is_ge,
                        fill=-1e30, base=0, pattern=[[1, 128]], channel_multiplier=-1)
        s_gp.inc(g.affine_select(out=mask01[:], in_=mask01[:], compare_op=OP.is_ge,
                        fill=0.0, base=0, pattern=[[1, 128]], channel_multiplier=-1))
        g.wait_ge(s_pld.h, n_param_ld)
        s_gp.inc(g.partition_broadcast(out_ap=drep[:], in_ap=drow[0:1, :]))
    setup_gp = s_gp.n

    @BS
    def _(sc):
        sc.preload_act_table(AF.Exp)
        sc.wait_ge(s_pld.h, n_param_ld)
        sc.wait_ge(s_v.h, consts_done)
        s_sc.inc(sc.activation(out=negA[:], in_=alog_sb[:], func=AF.Exp, scale=1.0))
    negA_sc = s_sc.n

    @BV
    def _(v):
        v.wait_ge(s_sc.h, negA_sc)
        s_v.inc(v.tensor_scalar(out=negA[:], in0=negA[:], scalar1=-1.0,
                                scalar2=None, op0=OP.mult))

    xn_cms = [nc.sbuf_tensor("xnhi", [128, NIB_D, T + 4], BF16),
              nc.sbuf_tensor("xnlo", [128, NIB_D, T + 4], BF16)]
    xnhi = xn_cms[0].__enter__()
    xnlo = xn_cms[1].__enter__()
    THAL = T

    p0_ctx = []

    def sb0(name, shape, dtype):
        cm = nc.sbuf_tensor(name, list(shape), dtype)
        t = cm.__enter__()
        p0_ctx.append(cm)
        return t

    @BV
    def _(v):
        v.memset(xnhi[:], 0.0)
        s_v.inc(v.memset(xnlo[:], 0.0))

    ut = [sb0("ut0", [128, D], F32)]
    xnt = sb0("xnt", [128, D], F32)
    xct = sb0("xct", [128, D], F32)
    qi = sb0("qi", [128, D], I32)
    qc = sb0("qc", [128, D], F32)
    qout = sb0("qout", [128, D], BF16)
    hit = sb0("hit", [128, D], BF16)
    lot = sb0("lot", [128, D], BF16)
    qxn_fm = sb0("qxn_fm", [128, NIB_D, T], BF16)
    stat = sb0("stat", [128, 8], F32)
    pb_cm = [nc.psum_tensor(f"p0tp{i}", [128, 1024], BF16) for i in range(2)]
    pbank = [cm.__enter__() for cm in pb_cm]
    p0_ctx.extend(pb_cm)
    bank_free = [0, 0]

    tile_v_done = [0]

    for tt in range(NT + 1):
        buf = ut[0]
        is_halo = tt == NT
        need_v = tile_v_done[0]

        @block.sync
        def _(sy, buf=buf, is_halo=is_halo, tt=tt, need_v=need_v):
            sy.wait_ge(s_v.h, need_v)
            src_ = d_uh if is_halo else d_u[tt * 128:(tt + 1) * 128, :]
            dma(sy, s_ld, buf[:], src_)
        ld_done = s_ld.n

        @BV
        def _(v, buf=buf, ld_done=ld_done):
            v.wait_ge(s_ld.h, ld_done)
            v.tensor_reduce(out=stat[:, 0:1], in_=buf[:], axis=AX, op=OP.add)
            v.tensor_scalar(out=stat[:, 0:1], in0=stat[:, 0:1], scalar1=1.0 / D,
                            scalar2=None, op0=OP.mult)
            v.tensor_scalar(out=xct[:], in0=buf[:], scalar1=stat[:, 0:1],
                            scalar2=None, op0=OP.subtract)
            v.tensor_tensor(out=qc[:], in0=xct[:], in1=xct[:], op=OP.mult)
            v.tensor_reduce(out=stat[:, 1:2], in_=qc[:], axis=AX, op=OP.add)
            s_v.inc(v.tensor_scalar(out=stat[:, 1:2], in0=stat[:, 1:2],
                                    scalar1=1.0 / D, scalar2=None, op0=OP.mult))
        tile_v_done[0] = s_v.n
        var_done = s_v.n

        @BS
        def _(sc, var_done=var_done):
            sc.wait_ge(s_v.h, var_done)
            sc.activation(out=stat[:, 2:3], in_=stat[:, 1:2], func=AF.Ln, bias=ccol(EPS), scale=1.0)
            s_sc.inc(sc.activation(out=stat[:, 2:3], in_=stat[:, 2:3], func=AF.Exp, scale=-0.5))
        rstd_done = s_sc.n

        @BV
        def _(v, rstd_done=rstd_done, is_halo=is_halo, tt=tt):
            v.wait_ge(s_sc.h, rstd_done)
            s_v.inc(v.tensor_scalar(out=xnt[:], in0=xct[:], scalar1=stat[:, 2:3],
                                    scalar2=None, op0=OP.mult))
            if not is_halo:
                v.tensor_reduce(out=stat[:, 3:4], in_=xnt[:], axis=AX,
                                op=OP.max, apply_absolute_value=True)
                s_v.inc(v.tensor_scalar(out=gx1_cols[:, tt:tt + 1], in0=stat[:, 3:4],
                                        scalar1=1e-5, scalar2=None, op0=OP.max))
        xn_done = s_v.n

        if not is_halo:
            @BS
            def _(sc, xn_done=xn_done):
                sc.wait_ge(s_v.h, xn_done)
                sc.activation(out=stat[:, 4:5], in_=stat[:, 3:4], func=AF.Ln)
                s_sc.inc(sc.activation(out=stat[:, 4:5], in_=stat[:, 4:5], func=AF.Exp,
                                       scale=-1.0, bias=ccol(LN127)))
            sc1_done = s_sc.n

            @BV
            def _(v, sc1_done=sc1_done):
                v.wait_ge(s_sc.h, sc1_done)
                v.tensor_scalar(out=qc[:], in0=xnt[:], scalar1=stat[:, 4:5],
                                scalar2=0.5, op0=OP.mult, op1=OP.add)
                v.tensor_copy(out=qi[:], in_=qc[:])
                v.scalar_tensor_tensor(out=xct[:], in0=qi[:], scalar=0.0,
                                       in1=qc[:], op0=OP.add, op1=OP.is_gt)
                s_v.inc(v.tensor_tensor(out=qout[:], in0=xct[:], in1=qi[:], op=OP.subtract))

        @BV
        def _(v):
            v.tensor_copy(out=hit[:], in_=xnt[:])
            s_v.inc(v.tensor_tensor(out=lot[:], in0=xnt[:], in1=hit[:], op=OP.subtract))
        split_done = s_v.n

        for ib in range(NIB_D):
            bi = ib % 2
            pb = pbank[bi]
            need_free = bank_free[bi]

            @block.tensor
            def _(t, ib=ib, pb=pb, need_free=need_free, split_done=split_done,
                  is_halo=is_halo):
                t.wait_ge(s_gp.h, 1)
                t.wait_ge(s_v.h, max(need_free, split_done))
                sl = slice(ib * 128, (ib + 1) * 128)
                if not is_halo:
                    t.transpose(pb[:, 0:128], qout[:, sl], identb[:])
                t.transpose(pb[:, 128:256], hit[:, sl], identb[:])
                s_mm.inc(t.transpose(pb[:, 256:384], lot[:, sl], identb[:]))
            tp_done = s_mm.n

            @BV
            def _(v, ib=ib, pb=pb, tp_done=tp_done, is_halo=is_halo, tt=tt):
                v.wait_ge(s_mm.h, tp_done)
                col = THAL if is_halo else tt * 128
                w = 3 if is_halo else 128
                if not is_halo:
                    v.tensor_copy(out=qxn_fm[:, ib, col:col + 128], in_=pb[:, 0:128])
                v.tensor_copy(out=xnhi[:, ib, col:col + w], in_=pb[:, 128:128 + w])
                s_v.inc(v.tensor_copy(out=xnlo[:, ib, col:col + w], in_=pb[:, 256:256 + w]))
            bank_free[bi] = s_v.n

    @BG
    def _(g, need=s_v.n):
        g.wait_ge(s_v.h, need)
        dma(g, s_st, d_qxn, qxn_fm[:])
    qxn_spilled = s_st.n

    if debug:
        d_xnhi = dbg("dbg_xnhi", [128, NIB_D, T + 4], BF16)
        d_xnlo = dbg("dbg_xnlo", [128, NIB_D, T + 4], BF16)
        d_qxnd = dbg("dbg_qxn", [128, NIB_D, T], BF16)

        @BG
        def _(g, need=s_v.n):
            g.wait_ge(s_v.h, need)
            dma(g, s_st, d_xnhi, xnhi[:])
            dma(g, s_st, d_xnlo, xnlo[:])
            dma(g, s_st, d_qxnd, qxn_fm[:])

    @BG
    def _(g):
        g.wait_ge(s_st.h, s_st.n)
        s_gp.inc(g.nop())
    p0_closed = s_gp.n

    for cm in reversed(p0_ctx):
        cm.__exit__(None, None, None)

    p1_ctx = []

    def sb1(name, shape, dtype):
        cm = nc.sbuf_tensor(name, list(shape), dtype)
        t = cm.__enter__()
        p1_ctx.append(cm)
        return t

    def ps1(name, shape, dtype=F32):
        cm = nc.psum_tensor(name, list(shape), dtype)
        t = cm.__enter__()
        p1_ctx.append(cm)
        return t

    wraw = [sb1(f"wraw{i}", [128, NIB_D * 128], F32) for i in range(2)]
    wh = [sb1(f"wh{i}", [128, NIB_D * 128], BF16) for i in range(2)]
    wl = [sb1(f"wl{i}", [128, NIB_D * 128], BF16) for i in range(2)]
    xraw = sb1("xraw", [128, 1027 + 5], F32)
    cacc = [sb1(f"cacc{i}", [128, T], F32) for i in range(2)]
    xsil = [sb1(f"xsil{i}", [128, T], F32) for i in range(2)]
    xsil_free = [0, 0]
    cacc_free = [0, 0]
    coll = [sb1(f"coll{i}", [128, NT * 128], F32) for i in range(2)]
    pk = [ps1(f"p1b{i}", [128, 512], F32) for i in range(6)]
    ptp_full = [ps1(f"p1t{i}", [128, 512], F32) for i in range(2)]
    ptp = [t[:, 0:128] for t in ptp_full]

    s_w = [S("w0"), S("w1")]
    s_sp = S("sp")
    wraw_free = [0, 0]
    whl_free = [0, 0]
    pset_free = [0, 0]
    tp_free = [0, 0]
    coll_free = [0, 0]

    @BS
    def _(sc):
        sc.preload_act_table(AF.Silu)

    def p1_transpose_spill(par, src_sil, dram_dst, cidx, sil_mark):
        cbuf = coll[cidx % 2]
        need_coll = coll_free[cidx % 2]
        for k in range(NT):
            tb = ptp[k % 2]
            need_tp = tp_free[k % 2]

            @block.tensor
            def _(t, k=k, tb=tb, need_tp=need_tp, need_coll=need_coll,
                  src_sil=src_sil, at_sc=sil_mark):
                t.wait_ge(s_sc.h, at_sc)
                t.wait_ge(s_v.h, need_tp)
                if need_coll:
                    t.wait_ge(s_sp.h, need_coll)
                s_mm.inc(t.transpose(tb, src_sil[:, k * 128:(k + 1) * 128], ident[:]))
            tpd = s_mm.n

            @BV
            def _(v, k=k, tb=tb, tpd=tpd, cbuf=cbuf):
                v.wait_ge(s_mm.h, tpd)
                s_v.inc(v.tensor_copy(out=cbuf[:, k * 128:(k + 1) * 128], in_=tb))
            tp_free[k % 2] = s_v.n
        xsil_free[par] = s_mm.n
        ev_done = s_v.n

        @BG
        def _(g, cbuf=cbuf, ev_done=ev_done, dram_dst=dram_dst):
            g.wait_ge(s_v.h, ev_done)
            dma(g, s_sp, dram_dst.rearrange("tt p c -> p tt c"),
                cbuf.ap().rearrange("p (tt c) -> p tt c", tt=NT))
        coll_free[cidx % 2] = s_sp.n

    def p1_evac_z(zb, par, mm_stop):
        po0, po1 = pk[3 * par], pk[3 * par + 1]
        need_xs = xsil_free[par]

        @BS
        def _(sc, po0=po0, po1=po1, mm_stop=mm_stop, need_xs=need_xs, par=par):
            sc.wait_ge(s_mm.h, max(mm_stop, need_xs))
            sc.activation(out=xsil[par][:, 0:512], in_=po0[:], func=AF.Silu)
            s_sc.inc(sc.activation(out=xsil[par][:, 512:1024], in_=po1[:], func=AF.Silu))
        pset_free[par] = -1
        sil_mark = s_sc.n
        return lambda: p1_transpose_spill(par, xsil[par], d_ztm[zb], zb, sil_mark)

    def p1_evac_x(xcb, par, mm_stop):
        po0, po1, ph = pk[3 * par], pk[3 * par + 1], pk[3 * par + 2]
        need_ca = cacc_free[par]

        @BV
        def _(v, po0=po0, po1=po1, ph=ph, mm_stop=mm_stop, xcb=xcb, par=par,
              need_ca=need_ca):
            v.wait_ge(s_mm.h, mm_stop)
            v.wait_ge(s_sc.h, need_ca)
            v.tensor_copy(out=xraw[:, 0:3], in_=ph[:, 0:3])
            v.tensor_copy(out=xraw[:, 3:515], in_=po0[:])
            v.tensor_copy(out=xraw[:, 515:1027], in_=po1[:])
            v.tensor_scalar(out=cacc[par][:], in0=xraw[:, 0:T],
                            scalar1=cw_sb[:, 4 * xcb:4 * xcb + 1],
                            scalar2=None, op0=OP.mult)
            for k2 in range(1, 3):
                v.scalar_tensor_tensor(out=cacc[par][:], in0=xraw[:, k2:k2 + T],
                                       scalar=cw_sb[:, 4 * xcb + k2:4 * xcb + k2 + 1],
                                       in1=cacc[par][:], op0=OP.mult, op1=OP.add)
            s_v.inc(v.scalar_tensor_tensor(out=cacc[par][:], in0=xraw[:, 3:3 + T],
                                           scalar=cw_sb[:, 4 * xcb + 3:4 * xcb + 4],
                                           in1=cacc[par][:], op0=OP.mult, op1=OP.add))
        conv_done = s_v.n
        pset_free[par] = conv_done

        if xcb < 32:
            need_xs = xsil_free[par]

            @BS
            def _(sc, xcb=xcb, conv_done=conv_done, par=par, need_xs=need_xs):
                sc.wait_ge(s_v.h, conv_done)
                sc.wait_ge(s_mm.h, need_xs)
                s_sc.inc(sc.activation(out=xsil[par][:, 0:T], in_=cacc[par][:],
                                       func=AF.Silu, bias=cb_sb[:, xcb:xcb + 1]))
            cacc_free[par] = s_sc.n
            sil_mark = s_sc.n
            return lambda: p1_transpose_spill(par, xsil[par], d_xtm[xcb], xcb, sil_mark)
        else:
            dst = B_sb if xcb == 32 else C_sb

            @BS
            def _(sc, xcb=xcb, conv_done=conv_done, dst=dst, par=par):
                sc.wait_ge(s_v.h, conv_done)
                s_sc.inc(sc.activation(out=dst[:], in_=cacc[par][:], func=AF.Silu,
                                       bias=cb_sb[:, xcb:xcb + 1]))
            cacc_free[par] = s_sc.n

    def p1_evac_dt(par, mm_stop):
        po0, po1 = pk[3 * par], pk[3 * par + 1]

        @BS
        def _(sc, po0=po0, po1=po1, mm_stop=mm_stop):
            sc.wait_ge(s_mm.h, mm_stop)
            sc.preload_act_table(AF.Exp)
            sc.activation(out=dt_s[:, 0:512], in_=po0[0:NH, :], func=AF.Exp,
                          bias=dtb_sb[:, 0:1])
            sc.activation(out=dt_s[:, 512:1024], in_=po1[0:NH, :], func=AF.Exp,
                          bias=dtb_sb[:, 0:1])
            sc.activation(out=dt_s[:, 0:512], in_=dt_s[:, 0:512], func=AF.Ln,
                          bias=ccol(1.0)[0:NH, :])
            s_sc.inc(sc.activation(out=dt_s[:, 512:1024], in_=dt_s[:, 512:1024],
                                   func=AF.Ln, bias=ccol(1.0)[0:NH, :]))
        dt_done = s_sc.n
        pset_free[par] = -2

        @BV
        def _(v, dt_done=dt_done):
            v.wait_ge(s_sc.h, dt_done)
            v.tensor_scalar(out=bb64[:], in0=dt_s[:], scalar1=negA[:, 0:1],
                            scalar2=None, op0=OP.mult)
            v.memset(cs_ext[:], 0.0)
            v.tensor_tensor_scan(out=cs_ext[:, 1:T + 1], data0=ones64[:],
                                 data1=bb64[:], initial=0.0, op0=OP.mult, op1=OP.add)
            s_v.inc(v.tensor_scalar(out=logdt[:], in0=dt_s[:], scalar1=1e-19,
                                    scalar2=None, op0=OP.max))
        l_done = s_v.n

        @BS
        def _(sc, l_done=l_done):
            sc.wait_ge(s_v.h, l_done)
            s_sc.inc(sc.activation(out=logdt[:], in_=logdt[:], func=AF.Ln))
        ln_done = s_sc.n

        @BV
        def _(v, ln_done=ln_done):
            v.wait_ge(s_sc.h, ln_done)
            s_v.inc(v.tensor_tensor(out=bb64[:], in0=logdt[:], in1=cs_ext[:, 1:T + 1],
                                    op=OP.subtract))
        return s_v.n

    gbuf = [sb1(f"gbuf{i}", [128, 2048], F32) for i in range(2)]
    gtmp = sb1("gtmp", [128, 2], F32)
    s_gm = [S("gm0"), S("gm1")]
    gq_free = [0, 0]

    @BV
    def _(v):
        s_v.inc(v.memset(gam_part[:], 0.0))

    gpieces = []
    for ob_ in range(8):
        gpieces.append((0, d_gg[ob_].rearrange("p ib o -> p (ib o)")))
    for ob_ in range(8):
        gpieces.append((1, d_gv[ob_].rearrange("p ib o -> p (ib o)")))
    for ob_ in range(2):
        flat = d_go[ob_].rearrange("p ib o -> p (ib o)")
        for k_ in range(4):
            gpieces.append((2, flat[:, k_ * 2048:(k_ + 1) * 2048]))

    for gi, (mat, srcap) in enumerate(gpieces):
        gpar = gi % 2

        @BG
        def _(g, gi=gi, gpar=gpar, srcap=srcap):
            g.wait_ge(s_v.h, gq_free[gpar])
            dma(g, s_gm[gpar], gbuf[gpar][:], srcap)
        gld = s_gm[gpar].n

        @BV
        def _(v, gi=gi, gpar=gpar, mat=mat, gld=gld):
            v.wait_ge(s_gm[gpar].h, gld)
            v.tensor_reduce(out=gtmp[:, 0:1], in_=gbuf[gpar][:], axis=AX,
                            op=OP.add, apply_absolute_value=True)
            s_v.inc(v.tensor_tensor(out=gam_part[:, mat:mat + 1],
                                    in0=gam_part[:, mat:mat + 1],
                                    in1=gtmp[:, 0:1], op=OP.add))
        gq_free[gpar] = s_v.n
    gam_done = s_v.n

    xn_ready = s_v.n
    fence = {s: s.n for s in (s_v, s_sc, s_mm, s_st, s_gp)}
    split_marks = {}

    def p1_load_split(ob):
        par = ob % 2

        @block.sync
        def _(sy, ob=ob, par=par):
            if ob == 0:
                for s_, n_ in fence.items():
                    sy.wait_ge(s_.h, n_)
            sy.wait_ge(s_v.h, wraw_free[par])
            dma(sy, s_w[par], wraw[par][:], d_win[ob])
        wld = s_w[par].n

        @BS
        def _(sc, par=par, wld=wld, wf=whl_free[par]):
            sc.wait_ge(s_w[par].h, wld)
            sc.wait_ge(s_mm.h, wf)
            s_sc.inc(sc.copy(out=wh[par][:], in_=wraw[par][:]))
        whd = s_sc.n

        @BV
        def _(v, par=par, wld=wld, whd=whd, wf=whl_free[par]):
            v.wait_ge(s_w[par].h, wld)
            v.wait_ge(s_sc.h, whd)
            v.wait_ge(s_mm.h, wf)
            s_v.inc(v.tensor_tensor(out=wl[par][:], in0=wraw[par][:], in1=wh[par][:],
                                    op=OP.subtract))
        wraw_free[par] = s_v.n
        split_marks[ob] = (whd, s_v.n)

    def p1_mm_evac(ob):
        par = ob % 2
        is_x = 32 <= ob < 66
        need_pset = pset_free[par]
        whd, wld_split = split_marks[ob]

        @block.tensor
        def _(t, ob=ob, par=par, whd=whd, wld_split=wld_split, is_x=is_x,
              need_pset=need_pset, xn_ready=xn_ready):
            t.wait_ge(s_sc.h, whd)
            t.wait_ge(s_v.h, wld_split)
            t.wait_ge(s_v.h, xn_ready)
            if need_pset > 0:
                t.wait_ge(s_v.h, need_pset)
            elif need_pset == -1 or need_pset == -2:
                t.wait_ge(s_sc.h, s_sc.n)
            po0, po1, ph = pk[3 * par], pk[3 * par + 1], pk[3 * par + 2]
            first = True
            last = None
            for ib in range(NIB_D):
                wha = wh[par][:, ib * 128:(ib + 1) * 128]
                wla = wl[par][:, ib * 128:(ib + 1) * 128]
                hi0 = xnhi[:, ib, 0:512]
                hi1 = xnhi[:, ib, 512:1024]
                lo0 = xnlo[:, ib, 0:512]
                lo1 = xnlo[:, ib, 512:1024]
                t.matmul(po0[:], wha, hi0, start=first, stop=False)
                t.matmul(po1[:], wha, hi1, start=first, stop=False)
                t.matmul(po0[:], wha, lo0, start=False, stop=False)
                t.matmul(po1[:], wha, lo1, start=False, stop=False)
                if is_x:
                    t.matmul(ph[:, 0:3], wha, xnhi[:, ib, T:T + 3], start=first, stop=False)
                    t.matmul(ph[:, 0:3], wha, xnlo[:, ib, T:T + 3], start=False, stop=False)
                t.matmul(po0[:], wla, hi0, start=False, stop=(ib == NIB_D - 1))
                last = t.matmul(po1[:], wla, hi1, start=False, stop=(ib == NIB_D - 1))
                if is_x:
                    last = t.matmul(ph[:, 0:3], wla, xnhi[:, ib, T:T + 3], start=False,
                                    stop=(ib == NIB_D - 1))
                first = False
            s_mm.inc(last)
        mm_stop = s_mm.n
        whl_free[par] = s_mm.n

        if ob < 32:
            return ("T", p1_evac_z(ob, par, mm_stop))
        elif is_x:
            return ("T", p1_evac_x(ob - 32, par, mm_stop))
        else:
            return ("dt", p1_evac_dt(par, mm_stop))

    dt_chain_done = None
    pending_T = None

    def run_mm_evac(ob):
        nonlocal dt_chain_done, pending_T
        kind, val = p1_mm_evac(ob)
        prev = pending_T
        pending_T = val if kind == "T" else None
        if kind == "dt":
            dt_chain_done = val
        if prev is not None:
            prev()

    for ob in range(NOB_IN):
        p1_load_split(ob)
        if ob >= 1:
            run_mm_evac(ob - 1)
    run_mm_evac(NOB_IN - 1)
    if pending_T is not None:
        pending_T()

    smallT = sb1("smallT", [NH, 128], F32)
    smallT2 = sb1("smallT2", [128, 128], F32)

    @BV
    def _(v):
        v.wait_ge(s_v.h, dt_chain_done)
        v.memset(smallT[:], 0.0)
        for c in range(NCH):
            v.tensor_scalar(out=smallT[:, c:c + 1],
                            in0=cs_ext[0:NH, (c + 1) * 128:(c + 1) * 128 + 1],
                            scalar1=cs_ext[0:NH, c * 128:c * 128 + 1],
                            scalar2=None, op0=OP.subtract)
            v.tensor_copy(out=smallT[:, NCH + c:NCH + c + 1],
                          in_=cs_ext[0:NH, c * 128:c * 128 + 1])
        s_v.inc(v.memset(smallT[:, 2 * NCH:2 * NCH + 1], 0.0))
    sm_args = s_v.n

    @BS
    def _(sc, sm_args=sm_args):
        sc.wait_ge(s_v.h, sm_args)
        s_sc.inc(sc.activation(out=smallT[:, 0:2 * NCH], in_=smallT[:, 0:2 * NCH], func=AF.Exp))
    sm_exp = s_sc.n

    @block.tensor
    def _(t, sm_exp=sm_exp):
        t.wait_ge(s_sc.h, sm_exp)
        t.wait_ge(s_v.h, tp_free[0])
        s_mm.inc(t.transpose(ptp_full[0][:, 0:NH], smallT[:], ident[0:NH, 0:NH]))
    smT_done = s_mm.n

    @BV
    def _(v, smT_done=smT_done):
        v.wait_ge(s_mm.h, smT_done)
        s_v.inc(v.tensor_copy(out=smallT2[:, 0:NH], in_=ptp_full[0][:, 0:NH]))
    smT2_done = s_v.n

    smflat = sb1("smflat", [1, 2 * NCH * NH], F32)

    @BG
    def _(g, smT2_done=smT2_done):
        g.wait_ge(s_v.h, smT2_done)
        dma(g, s_st, d_small, smallT2[0:2 * NCH, 0:NH])
        g.wait_ge(s_st.h, s_st.n)
        dma(g, s_st, smflat[0:1, :], d_small.rearrange("r c -> (r c)")[None, :])
        g.wait_ge(s_st.h, s_st.n)
        g.partition_broadcast(out_ap=aend_rep[:], in_ap=smflat[0:1, 0:NCH * NH])
        s_gp.inc(g.partition_broadcast(out_ap=rels_rep[:],
                                       in_ap=smflat[0:1, NCH * NH:2 * NCH * NH]))
    reps_done = s_gp.n

    scl64 = sb1("scl64", [NH, 128], F32)
    exr64 = sb1("exr64", [NH, 128], F32)
    for c in range(NCH):
        @BV
        def _(v, c=c):
            v.wait_ge(s_v.h, s_v.n)
            v.tensor_scalar(out=scl64[:], in0=bb64[:, c * 128:(c + 1) * 128],
                            scalar1=cs_ext[0:NH, (c + 1) * 128:(c + 1) * 128 + 1],
                            scalar2=None, op0=OP.add)
            s_v.inc(v.tensor_scalar(out=exr64[:], in0=cs_ext[0:NH, c * 128 + 1:(c + 1) * 128 + 1],
                                    scalar1=cs_ext[0:NH, c * 128:c * 128 + 1],
                                    scalar2=None, op0=OP.subtract))
        args_done = s_v.n

        @BS
        def _(sc, c=c, args_done=args_done):
            sc.wait_ge(s_v.h, args_done)
            sc.activation(out=scl64[:], in_=scl64[:], func=AF.Exp)
            s_sc.inc(sc.activation(out=exr64[:], in_=exr64[:], func=AF.Exp))
        exps_done = s_sc.n

        for srcbuf, dstbuf in ((bb64, bbT), (scl64, sclT), (exr64, exprelT)):
            @block.tensor
            def _(t, c=c, srcbuf=srcbuf, exps_done=exps_done, need_tp=tp_free[1]):
                t.wait_ge(s_sc.h, exps_done)
                t.wait_ge(s_v.h, need_tp)
                src_ap = srcbuf[:, c * 128:(c + 1) * 128] if srcbuf is bb64 else srcbuf[:]
                s_mm.inc(t.transpose(ptp_full[1][:, 0:NH], src_ap, ident[0:NH, 0:NH]))
            td = s_mm.n

            @BV
            def _(v, c=c, dstbuf=dstbuf, td=td):
                v.wait_ge(s_mm.h, td)
                s_v.inc(v.tensor_copy(out=dstbuf[:, c * NH:(c + 1) * NH], in_=ptp_full[1][:, 0:NH]))
            tp_free[1] = s_v.n
    prep_done = s_v.n

    if debug:
        d_dts = dbg("dbg_dts", [NH, T])
        d_cse = dbg("dbg_cse", [NH, T + 8])
        d_Bsb = dbg("dbg_B", [128, T])
        d_Csb = dbg("dbg_C", [128, T])
        d_zt = dbg("dbg_ztm", [32, NT, 128, 128])
        d_xt = dbg("dbg_xtm", [32, NT, 128, 128])

        @BG
        def _(g):
            g.wait_ge(s_v.h, prep_done)
            g.wait_ge(s_sp.h, s_sp.n)
            dma(g, s_st, d_dts, dt_s[:])
            dma(g, s_st, d_cse, cs_ext[:])
            dma(g, s_st, d_Bsb, B_sb[:])
            dma(g, s_st, d_Csb, C_sb[:])
            dma(g, s_st, d_zt, d_ztm)
            dma(g, s_st, d_xt, d_xtm)

    for cm in reversed(p1_ctx):
        cm.__exit__(None, None, None)
    for cm in reversed(xn_cms):
        cm.__exit__(None, None, None)

    p2_ctx = []

    def sb2(name, shape, dtype):
        cm = nc.sbuf_tensor(name, list(shape), dtype)
        t = cm.__enter__()
        p2_ctx.append(cm)
        return t

    def ps2(name, shape, dtype=F32):
        cm = nc.psum_tensor(name, list(shape), dtype)
        t = cm.__enter__()
        p2_ctx.append(cm)
        return t

    init_sb = sbsc("init_sb", [128, DSSM], F32)
    c8rep = sbsc("c8rep", [128, 8 * NH], F32)

    xh_t = [sb2(f"xh_t{i}", [128, DSSM], F32) for i in range(2)]
    xsc = sb2("xsc", [128, DSSM], F32)
    BT_sb = sb2("BT_sb", [128, 128], F32)
    chain = [sb2(f"chain{i}", [128, DSSM], F32) for i in range(2)]
    pst = [ps2(f"p2st{i}", [128, 512], F32) for i in range(2)]
    ptr2_full = ps2("p2tp", [128, 512], F32)
    s_x = S("x2")
    s_ch = S("ch")

    fence2 = {s: s.n for s in (s_v, s_sc, s_mm, s_st, s_gp, s_sp)}

    @BV
    def _(v):
        for s_, n_ in fence2.items():
            v.wait_ge(s_.h, n_)
        s_v.inc(v.memset(chain[0][:], 0.0))
    chain_ready = s_v.n

    xh_free = [0, 0]
    pst_free2 = [0, 0]
    chsp_prev = 0
    xsc_free = 0
    for c in range(NCH):
        par = c % 2

        @block.sync
        def _(sy, c=c, par=par, xhf=xh_free[par], chr_=chain_ready):
            sy.wait_ge(s_v.h, max(xhf, chr_))
            dma(sy, s_x, xh_t[par].ap().rearrange("p (ob c2) -> p ob c2", ob=32),
                d_xtm[:, c].rearrange("ob p c2 -> p ob c2"))
        xld = s_x.n

        @BV
        def _(v, c=c, par=par, xld=xld, xscf=xsc_free):
            v.wait_ge(s_x.h, xld)
            v.wait_ge(s_mm.h, xscf)
            s_v.inc(v.tensor_tensor(
                out=xsc.ap().rearrange("p (h q) -> p h q", h=NH),
                in0=xh_t[par].ap().rearrange("p (h q) -> p h q", h=NH),
                in1=bc64(sclT[:, c * NH:(c + 1) * NH], NH), op=OP.mult))
        xh_free[par] = s_v.n
        xsc_done = s_v.n

        @block.tensor
        def _(t, c=c):
            t.wait_ge(s_v.h, s_v.n)
            s_mm.inc(t.transpose(ptr2_full[:, 0:128], B_sb[:, c * 128:(c + 1) * 128],
                                 ident[:]))
        btd = s_mm.n

        @BV
        def _(v, btd=btd):
            v.wait_ge(s_mm.h, btd)
            s_v.inc(v.tensor_copy(out=BT_sb[:], in_=ptr2_full[:, 0:128]))
        bt_ready = s_v.n

        @BG
        def _(g, c=c, par=par, chain_ready=chain_ready):
            g.wait_ge(s_v.h, chain_ready)
            dma(g, s_ch, d_chain[c], chain[par][:])
        chsp = chsp_prev
        chsp_prev_next = s_ch.n

        for q in range(8):
            pb2 = pst[q % 2]
            at_v = pst_free2[q % 2]

            @block.tensor
            def _(t, q=q, pb2=pb2, xsc_done=xsc_done, bt_ready=bt_ready, at_v=at_v):
                t.wait_ge(s_v.h, max(xsc_done, bt_ready, at_v))
                s_mm.inc(t.matmul(pb2[:], BT_sb[:], xsc[:, q * 512:(q + 1) * 512],
                                  start=True, stop=True))
            mmd = s_mm.n

            @BV
            def _(v, q=q, c=c, par=par, pb2=pb2, mmd=mmd, chsp=chsp):
                v.wait_ge(s_mm.h, mmd)
                v.wait_ge(s_ch.h, chsp)
                sl = slice(q * 512, (q + 1) * 512)
                v.tensor_tensor(
                    out=chain[1 - par].ap()[:, sl].rearrange("p (h q2) -> p h q2", h=8),
                    in0=chain[par].ap()[:, sl].rearrange("p (h q2) -> p h q2", h=8),
                    in1=bc64(aend_rep[:, c * NH + q * 8:c * NH + (q + 1) * 8], 8),
                    op=OP.mult)
                s_v.inc(v.tensor_tensor(out=chain[1 - par][:, sl],
                                        in0=chain[1 - par][:, sl],
                                        in1=pb2[:], op=OP.add))
            pst_free2[q % 2] = s_v.n
        chain_ready = s_v.n
        chsp_prev = chsp_prev_next
        xsc_free = s_mm.n

    final_par = NCH % 2

    gam3 = sb2("gam3", [8, 8], F32)

    @block.tensor
    def _(t):
        t.wait_ge(s_v.h, s_v.n)
        s_mm.inc(t.transpose(ptr2_full[0:4, 0:128], gam_part[:, 0:4], ident[:]))
    g3t = s_mm.n

    @BV
    def _(v, g3t=g3t):
        v.wait_ge(s_mm.h, g3t)
        s_v.inc(v.tensor_reduce(out=gam3[0:4, 0:1], in_=ptr2_full[0:4, 0:128],
                                axis=AX, op=OP.add))
    g3_done = s_v.n

    zrow = sb2("zrow", [1, DSSM], F32)

    @BV
    def _(v):
        s_v.inc(v.memset(zrow[:], 0.0))
    zrow_done = s_v.n

    @BG
    def _(g, g3_done=g3_done, final_par=final_par, zrow_done=zrow_done):
        g.wait_ge(s_v.h, max(g3_done, zrow_done))
        dma(g, s_st, t_ccin.ap()[128:129, :], zrow[0:1, :])
        dma(g, s_st, t_ccin.ap()[0:128, :], chain[final_par][:])
        dma(g, s_st, t_ccin.ap()[128:129, 0:NH].rearrange("one h -> h one"),
            cs_ext[0:NH, T:T + 1])
        dma(g, s_st, t_ccin.ap()[128:129, NH:NH + 3].rearrange("one m -> m one"),
            gam3[0:3, 0:1])
        g.wait_ge(s_st.h, s_st.n)
        s_cc.inc(g.collective_compute(
            "AllGather", OP.bypass,
            replica_groups=[list(range(NCORE))],
            ins=[t_ccin.ap().opt()], outs=[t_ccout.ap().opt()]))
    cc_done = s_cc.n

    ext8T = sb2("ext8T", [80, 8], F32)
    mrow = sb2("mrow", [80, 8], F32)
    c8 = sb2("c8", [NH, 16], F32)

    @block.sync
    def _(sy, cc_done=cc_done):
        sy.wait_ge(s_cc.h, cc_done)
        dma(sy, s_ld, ext8T[0:NH + 3, :],
            t_ccout.ap().rearrange("(j r) q -> j r q", r=129)[:, 128, 0:NH + 3]
            .rearrange("j q -> q j"), allow_slow_non_contiguous=True)
        dma(sy, s_ld, mrow[0:1, :], d_mask.rearrange("j one -> (j one)")[None, :])
    ext_ld = s_ld.n

    @BG
    def _(g, ext_ld=ext_ld):
        g.wait_ge(s_ld.h, ext_ld)
        s_gp.inc(g.partition_broadcast(out_ap=mrow[0:NH, :], in_ap=mrow[0:1, :]))
    mrep_done = s_gp.n

    @BV
    def _(v, mrep_done=mrep_done):
        v.wait_ge(s_gp.h, mrep_done)
        v.tensor_tensor(out=c8[:, 8:16], in0=ext8T[0:NH, :], in1=mrow[0:NH, :],
                        op=OP.mult)
        v.memset(c8[:, 7:8], 0.0)
        for j in range(6, -1, -1):
            v.scalar_tensor_tensor(out=c8[:, j:j + 1], in0=c8[:, j + 1:j + 2],
                                   scalar=1.0, in1=c8[:, 8 + j + 1:8 + j + 2],
                                   op0=OP.mult, op1=OP.add)
        s_v.inc(v.memset(c8[:, 15:16], 0.0))
    sfx_done = s_v.n

    @BS
    def _(sc, sfx_done=sfx_done):
        sc.wait_ge(s_v.h, sfx_done)
        s_sc.inc(sc.activation(out=c8[:, 0:8], in_=c8[:, 0:8], func=AF.Exp))
    c8e_done = s_sc.n

    @BV
    def _(v, c8e_done=c8e_done):
        v.wait_ge(s_sc.h, c8e_done)
        s_v.inc(v.tensor_tensor(out=c8[:, 0:8], in0=c8[:, 0:8], in1=mrow[0:NH, :],
                                op=OP.mult))
    c8_done = s_v.n

    d_c8 = nc.dram_tensor("sp_c8", [NH, 8], F32).ap()
    c8row = sb2("c8row", [1, 8 * NH], F32)

    @BG
    def _(g, c8_done=c8_done):
        g.wait_ge(s_v.h, c8_done)
        dma(g, s_st, d_c8, c8[0:NH, 0:8])
        g.wait_ge(s_st.h, s_st.n)
        dma(g, s_st, c8row.ap().rearrange("one (j h) -> one j h", j=8),
            d_c8.rearrange("h j -> j h")[None, :, :], allow_slow_non_contiguous=True)
        g.wait_ge(s_st.h, s_st.n)
        s_gp.inc(g.partition_broadcast(out_ap=c8rep[:], in_ap=c8row[0:1, :]))
    c8rep_done = s_gp.n

    grow = sb2("grow", [1, 8], F32)

    @BV
    def _(v, mrep_done=mrep_done):
        v.wait_ge(s_gp.h, mrep_done)
        s_v.inc(v.tensor_reduce(out=ext8T[NH:NH + 3, 0:1], in_=ext8T[NH:NH + 3, :],
                                axis=AX, op=OP.add))
    gsum_done = s_v.n

    NW_GATE = float(HID * D)

    @BS
    def _(sc, gsum_done=gsum_done):
        sc.wait_ge(s_v.h, gsum_done)
        s_sc.inc(sc.activation(out=ext8T[NH:NH + 3, 1:2], in_=ext8T[NH:NH + 3, 0:1],
                               func=AF.Ln))
    gln_done = s_sc.n

    LNW = float(np.log(NW_GATE))

    @BV
    def _(v, gln_done=gln_done):
        v.wait_ge(s_sc.h, gln_done)
        s_v.inc(v.tensor_scalar(out=ext8T[NH:NH + 3, 1:2], in0=ext8T[NH:NH + 3, 1:2],
                                scalar1=LNW, scalar2=float(np.log(1e-5)),
                                op0=OP.subtract, op1=OP.max))
    glnc_done = s_v.n

    @BS
    def _(sc, glnc_done=glnc_done):
        sc.wait_ge(s_v.h, glnc_done)
        sc.activation(out=ext8T[NH:NH + 3, 2:3], in_=ext8T[NH:NH + 3, 1:2], func=AF.Exp)
        s_sc.inc(sc.activation(out=ext8T[NH:NH + 3, 3:4], in_=ext8T[NH:NH + 3, 1:2],
                               func=AF.Exp, scale=-1.0))
    gexp_done = s_sc.n

    d_g6 = nc.dram_tensor("sp_g6", [3, 2], F32).ap()

    @BG
    def _(g, gexp_done=gexp_done):
        g.wait_ge(s_sc.h, gexp_done)
        dma(g, s_st, d_g6, ext8T[NH:NH + 3, 2:4])
        g.wait_ge(s_st.h, s_st.n)
        dma(g, s_st, grow.ap()[0:1, 0:6].rearrange("one (k m) -> one k m", k=2),
            d_g6.rearrange("m k -> k m")[None, :, :], allow_slow_non_contiguous=True)
        g.wait_ge(s_st.h, s_st.n)
        s_gp.inc(g.partition_broadcast(out_ap=gsc[:, 0:6],
                                       in_ap=grow[0:1, 0:6]))
    gsc_done = s_gp.n

    stsl = [sb2(f"stsl{i}", [128, DSSM], F32) for i in range(2)]
    s_sl = [S("sl0"), S("sl1")]
    stsl_free = [0, 0]

    @BV
    def _(v):
        s_v.inc(v.memset(init_sb[:], 0.0))
    init_z = s_v.n

    for j in range(NCORE):
        jp = j % 2

        @block.sync
        def _(sy, j=j, jp=jp, cc_done=cc_done, sf=stsl_free[jp]):
            sy.wait_ge(s_cc.h, cc_done)
            sy.wait_ge(s_v.h, sf)
            dma(sy, s_sl[jp], stsl[jp][:], t_ccout.ap()[129 * j:129 * j + 128, :])
        sld = s_sl[jp].n

        @BV
        def _(v, j=j, jp=jp, sld=sld, c8rep_done=c8rep_done, init_z=init_z):
            v.wait_ge(s_sl[jp].h, sld)
            v.wait_ge(s_gp.h, c8rep_done)
            v.wait_ge(s_v.h, init_z)
            v.tensor_tensor(
                out=stsl[jp].ap().rearrange("p (h q) -> p h q", h=NH),
                in0=stsl[jp].ap().rearrange("p (h q) -> p h q", h=NH),
                in1=bc64(c8rep[:, j * NH:(j + 1) * NH], NH), op=OP.mult)
            s_v.inc(v.tensor_tensor(out=init_sb[:], in0=init_sb[:],
                                    in1=stsl[jp][:], op=OP.add))
        stsl_free[jp] = s_v.n
    init_done = s_v.n

    for cm in reversed(p2_ctx):
        cm.__exit__(None, None, None)

    if debug:
        d_chx = dbg("dbg_chain", [NCH, 128, DSSM])
        d_init = dbg("dbg_init", [128, DSSM])
        d_c8d = dbg("dbg_c8", [NH, 16])

        @BG
        def _(g, init_done=init_done, final_par=final_par):
            g.wait_ge(s_v.h, init_done)
            dma(g, s_st, d_chx, d_chain)
            dma(g, s_st, d_init, init_sb[:])
            dma(g, s_st, d_c8d, c8[0:NH, :])

    p3_ctx = []

    def sb3(name, shape, dtype):
        cm = nc.sbuf_tensor(name, list(shape), dtype)
        t = cm.__enter__()
        p3_ctx.append(cm)
        return t

    def ps3(name, shape, dtype=F32):
        cm = nc.psum_tensor(name, list(shape), dtype)
        t = cm.__enter__()
        p3_ctx.append(cm)
        return t

    xh3 = [sb3(f"xh3_{i}", [128, DSSM], F32) for i in range(2)]
    zch = [sb3("zch0", [128, DSSM], F32)]
    stcor = sb3("stcor", [128, DSSM], F32)
    Ebuf = [sb3(f"Ebuf{i}", [128, 16 * 128], F32) for i in range(2)]
    bbrow = [sb3(f"bbrow{i}", [1, 16 * 128], F32) for i in range(2)]
    csrow = [sb3(f"csrow{i}", [1, 16 * 128], F32) for i in range(2)]
    ones2k = sb3("ones2k", [1, 128], F32)
    ychunk = sb3("ychunk", [128, DSSM], F32)
    ynspill = sb3("ynspill", [128, 2 * 32 * 128], BF16)
    rcol = sb3("rcol", [128, 12], F32)
    CB_sb = sb3("CB_sb", [128, 128], F32)
    pcb = ps3("p3cb", [128, 512], F32)
    prb = ps3("p3rb", [128, 512], F32)
    py4 = [ps3(f"p3y{i}", [128, 512], F32) for i in range(4)]
    ptp3 = [ps3(f"p3tp{i}", [128, 512], F32) for i in range(2)]

    @BV
    def _(v):
        v.wait_ge(s_ch.h, s_ch.n)
        v.wait_ge(s_st.h, s_st.n)
        v.wait_ge(s_gp.h, s_gp.n)
        s_v.inc(v.memset(ones2k[:, 0:128], 1.0))
    art_ones = s_v.n

    s_x3 = S("x3")
    s_z3 = S("z3")
    s_c3 = S("c3")
    s_r3 = S("r3")
    s_yn = S("yn")

    fence3 = {s: s.n for s in (s_v, s_sc, s_mm, s_st, s_gp, s_sp, s_ch, s_ld)}

    d_csr = nc.dram_tensor("sp_csr", [NH, T], F32).ap()
    d_bbr = nc.dram_tensor("sp_bbr", [NH, T], F32).ap()

    @BG
    def _(g):
        for s_, n_ in fence3.items():
            g.wait_ge(s_.h, n_)
        dma(g, s_st, d_csr, cs_ext[0:NH, 1:T + 1])
        dma(g, s_st, d_bbr, bb64[0:NH, :])
        g.wait_ge(s_st.h, s_st.n)
        s_gp.inc(g.nop())
    nwtm_done = s_gp.n

    P3 = dict(
        xh3_free=[init_done, init_done], zch_free=init_done, yn_free=0,
        art_free=[0, 0],
        cbrep_free=init_done,
        tp3_free=[0, 0],
        ebuf_free=[0, 0], cb_free=init_done,
        pcb_free=(s_v, init_done), prb_free=(s_v, init_done),
        pool_free=[init_done, init_done],
        ya_gate=init_done,
        stcor_free=init_done,
        x3ld={}, z3ld={}, c3ld={}, intra={}, ydone={},
    )

    def p3_partA(c):
        par = c % 2

        @block.sync
        def _(sy, c=c, par=par, xf=P3["xh3_free"][par]):
            sy.wait_ge(s_v.h, xf)
            dma(sy, s_x3, xh3[par].ap().rearrange("p (ob c2) -> p ob c2", ob=32),
                d_xtm[:, c].rearrange("ob p c2 -> p ob c2"))
        P3["x3ld"][c] = s_x3.n

        @block.tensor
        def _(t, c=c, cbf=P3["cb_free"], pcbf=P3["pcb_free"]):
            t.wait_ge(s_v.h, cbf)
            t.wait_ge(pcbf[0].h, pcbf[1])
            s_mm.inc(t.matmul(pcb[:, 0:128], B_sb[:, c * 128:(c + 1) * 128],
                              C_sb[:, c * 128:(c + 1) * 128], start=True, stop=True))
        cbmm = s_mm.n

        @BV
        def _(v, cbmm=cbmm, crf=P3["cbrep_free"]):
            v.wait_ge(s_mm.h, cbmm)
            v.wait_ge(s_v.h, crf)
            s_v.inc(v.tensor_tensor(out=CB_sb[:], in0=pcb[:, 0:128], in1=mask01[:],
                                    op=OP.mult))
        cb_done = s_v.n
        P3["cb_free"] = cb_done
        P3["pcb_free"] = (s_v, cb_done)

        pend_intra = None
        pend_evac = None
        for grp in range(4):
            epar = grp % 2
            apar = grp % 2
            h0 = grp * 16

            @block.sync
            def _(sy, c=c, grp=grp, artf=P3["art_free"][apar]):
                sy.wait_ge(s_mm.h, artf)
                sy.wait_ge(s_gp.h, nwtm_done)
                dma(sy, s_r3,
                    bbrow[apar].ap().rearrange("one (h s) -> one h s", h=16),
                    d_bbr[grp * 16:(grp + 1) * 16,
                          c * 128:(c + 1) * 128][None, :, :])
                dma(sy, s_r3,
                    csrow[apar].ap().rearrange("one (h s) -> one h s", h=16),
                    d_csr[grp * 16:(grp + 1) * 16,
                          c * 128:(c + 1) * 128][None, :, :])
            artld = s_r3.n

            last_amm = None
            for piece in range(4):
                bank, bkey = ((prb, "prb_free") if piece % 2 == 0
                              else (pcb, "pcb_free"))
                bf = P3[bkey]

                @block.tensor
                def _(t, apar=apar, piece=piece, bank=bank, bf=bf, artld=artld):
                    t.wait_ge(s_r3.h, artld)
                    t.wait_ge(s_v.h, art_ones)
                    t.wait_ge(bf[0].h, bf[1])
                    last = None
                    for hh4 in range(4):
                        hh = piece * 4 + hh4
                        reg = bank[:, hh4 * 128:(hh4 + 1) * 128]
                        t.matmul(reg, bbrow[apar][:, hh * 128:(hh + 1) * 128],
                                 ones2k[:, 0:128],
                                 start=(hh4 == 0), stop=False)
                        last = t.matmul(reg, ones2k[:, 0:128],
                                        csrow[apar][:, hh * 128:(hh + 1) * 128],
                                        start=False, stop=(hh4 == 3))
                    s_mm.inc(last)
                amm = s_mm.n
                last_amm = amm

                @BS
                def _(sc, piece=piece, epar=epar, amm=amm, bank=bank,
                      ef=P3["ebuf_free"][epar]):
                    sc.wait_ge(s_mm.h, amm)
                    if piece == 0:
                        sc.wait_ge(s_mm.h, ef)
                    dst = Ebuf[epar][:, piece * 512:(piece + 1) * 512]
                    sc.activation(out=dst, in_=bank[:], func=AF.Relu,
                                  scale=-1.0, bias=ccol(30.0))
                    s_sc.inc(sc.activation(out=dst, in_=dst, func=AF.Exp,
                                           scale=-1.0, bias=ccol(30.0)))
                P3[bkey] = (s_sc, s_sc.n)
            exp_done = s_sc.n
            P3["art_free"][apar] = last_amm

            @BV
            def _(v, epar=epar, exp_done=exp_done, cb_done=cb_done):
                v.wait_ge(s_sc.h, exp_done)
                v.wait_ge(s_v.h, cb_done)
                s_v.inc(v.tensor_tensor(
                    out=Ebuf[epar].ap().rearrange("p (h t2) -> p h t2", h=16),
                    in0=Ebuf[epar].ap().rearrange("p (h t2) -> p h t2", h=16),
                    in1=CB_sb[:, None, :].broadcast_to([128, 16, 128]),
                    op=OP.mult))
            m_done = s_v.n

            def mk_intra(c=c, par=par, h0=h0, epar=epar, m_done=m_done,
                         ppi=grp % 2):
                pp = [py4[2 * ppi], py4[2 * ppi + 1]]

                def emit():
                    @block.tensor
                    def _(t, x3ld=P3["x3ld"][c], pf=P3["pool_free"][ppi]):
                        t.wait_ge(s_x3.h, x3ld)
                        t.wait_ge(s_v.h, max(m_done, pf))
                        last = None
                        for hh in range(16):
                            h = h0 + hh
                            bk = pp[hh // 8]
                            last = t.matmul(bk[:, (hh % 8) * HD:((hh % 8) + 1) * HD],
                                            Ebuf[epar][:, hh * 128:(hh + 1) * 128],
                                            xh3[par][:, h * HD:(h + 1) * HD],
                                            start=(hh % 8 == 0), stop=(hh % 8 == 7))
                        s_mm.inc(last)
                    imm = s_mm.n
                    P3["ebuf_free"][epar] = imm

                    def emit_evac():
                        @BV
                        def _(v, gate=P3["ya_gate"]):
                            v.wait_ge(s_mm.h, imm)
                            v.wait_ge(s_v.h, gate)
                            ins = None
                            for bk_i in range(2):
                                hh0 = h0 + bk_i * 8
                                sl = slice(hh0 * HD, (hh0 + 8) * HD)
                                v.tensor_tensor(
                                    out=ychunk.ap()[:, sl].rearrange("p (h q) -> p h q", h=8),
                                    in0=xh3[par].ap()[:, sl].rearrange("p (h q) -> p h q", h=8),
                                    in1=bc64(drep[:, hh0:hh0 + 8], 8), op=OP.mult)
                                ins = v.tensor_tensor(out=ychunk[:, sl], in0=ychunk[:, sl],
                                                      in1=pp[bk_i][:], op=OP.add)
                            s_v.inc(ins)
                        P3["pool_free"][ppi] = s_v.n
                    return emit_evac
                return emit

            this_intra = mk_intra()
            if pend_intra is not None:
                new_evac = pend_intra()
                if pend_evac is not None:
                    pend_evac()
                pend_evac = new_evac
            pend_intra = this_intra
        P3["cbrep_free"] = m_done
        new_evac = pend_intra()
        if pend_evac is not None:
            pend_evac()
        new_evac()
        P3["intra"][c] = s_v.n

    def p3_partB(c, init_done):
        par = c % 2

        @block.sync
        def _(sy, c=c, scf=P3["stcor_free"]):
            sy.wait_ge(s_v.h, scf)
            dma(sy, s_c3, stcor[:], d_chain[c])
        c3ld = s_c3.n

        @BV
        def _(v, c=c, c3ld=c3ld, init_done=init_done, zf=P3["zch_free"]):
            v.wait_ge(s_c3.h, c3ld)
            v.wait_ge(s_v.h, max(init_done, zf))
            v.tensor_tensor(
                out=zch[0].ap().rearrange("p (h q) -> p h q", h=NH),
                in0=init_sb.ap().rearrange("p (h q) -> p h q", h=NH),
                in1=bc64(rels_rep[:, c * NH:(c + 1) * NH], NH), op=OP.mult)
            s_v.inc(v.tensor_tensor(out=stcor[:], in0=stcor[:], in1=zch[0][:],
                                    op=OP.add))
        stcor_done = s_v.n

        @block.sync
        def _(sy, c=c, scd=stcor_done):
            sy.wait_ge(s_v.h, scd)
            dma(sy, s_z3, zch[0].ap().rearrange("p (ob c2) -> p ob c2", ob=32),
                d_ztm[:, c].rearrange("ob p c2 -> p ob c2"))
        z3ld = s_z3.n

        intra_done = P3["intra"][c]
        for grp in range(4):
            h0 = grp * 16
            pp = [py4[2 * (grp % 2)], py4[2 * (grp % 2) + 1]]

            @block.tensor
            def _(t, c=c, h0=h0, stcor_done=stcor_done, pp=pp,
                  pf=P3["pool_free"][grp % 2]):
                t.wait_ge(s_v.h, max(stcor_done, pf))
                last = None
                for hh in range(16):
                    h = h0 + hh
                    bk = pp[hh // 8]
                    last = t.matmul(bk[:, (hh % 8) * HD:((hh % 8) + 1) * HD],
                                    C_sb[:, c * 128:(c + 1) * 128],
                                    stcor[:, h * HD:(h + 1) * HD],
                                    start=(hh % 8 == 0), stop=(hh % 8 == 7))
                s_mm.inc(last)
            bmm = s_mm.n

            @BV
            def _(v, c=c, h0=h0, bmm=bmm, intra_done=intra_done, pp=pp):
                v.wait_ge(s_mm.h, bmm)
                v.wait_ge(s_v.h, intra_done)
                ins = None
                for bk_i in range(2):
                    hh0 = h0 + bk_i * 8
                    sl = slice(hh0 * HD, (hh0 + 8) * HD)
                    v.tensor_tensor(
                        out=pp[bk_i].ap().rearrange("p (h q) -> p h q", h=8),
                        in0=pp[bk_i].ap().rearrange("p (h q) -> p h q", h=8),
                        in1=bc64(exprelT[:, c * NH + hh0:c * NH + hh0 + 8], 8),
                        op=OP.mult)
                    ins = v.tensor_tensor(out=ychunk[:, sl], in0=ychunk[:, sl],
                                          in1=pp[bk_i][:], op=OP.add)
                s_v.inc(ins)
            P3["pool_free"][grp % 2] = s_v.n
        P3["stcor_free"] = s_v.n
        y_done = s_v.n

        @BV
        def _(v, c=c, y_done=y_done, z3ld=z3ld):
            v.wait_ge(s_z3.h, z3ld)
            s_v.inc(v.tensor_tensor(out=ychunk[:], in0=ychunk[:], in1=zch[0][:],
                                    op=OP.mult))
        yg_done = s_v.n
        P3["zch_free"] = yg_done

        @BS
        def _(sc, c=c, yg_done=yg_done):
            sc.wait_ge(s_v.h, yg_done)
            ins = None
            for pp in range(8):
                ins = sc.activation(out=ptp3[0][:, 0:512],
                                    in_=ychunk[:, pp * 512:(pp + 1) * 512],
                                    func=AF.Square, accum_out=rcol[:, pp:pp + 1])
            s_sc.inc(ins)
        sq_done = s_sc.n

        @BV
        def _(v, sq_done=sq_done):
            v.wait_ge(s_sc.h, sq_done)
            s_v.inc(v.tensor_reduce(out=rcol[:, 8:9], in_=rcol[:, 0:8], axis=AX,
                                    op=OP.add))
        rsum_done = s_v.n

        @BS
        def _(sc, rsum_done=rsum_done):
            sc.wait_ge(s_v.h, rsum_done)
            sc.activation(out=rcol[:, 9:10], in_=rcol[:, 8:9], func=AF.Ln,
                          bias=ccol(EPS), scale=1.0 / DSSM)
            s_sc.inc(sc.activation(out=rcol[:, 2:3], in_=rcol[:, 9:10], func=AF.Exp,
                                   scale=-0.5))
        rstd_done = s_sc.n

        @BV
        def _(v, c=c, par=par, rstd_done=rstd_done):
            v.wait_ge(s_sc.h, rstd_done)
            s_v.inc(v.tensor_scalar(out=xh3[par][:], in0=ychunk[:],
                                    scalar1=rcol[:, 2:3], scalar2=None, op0=OP.mult))
        yn_done = s_v.n
        P3["ya_gate"] = yn_done

        def p3_tp(c=c, par=par, yn_done=yn_done):
            for ib in range(32):
                tb3 = ptp3[ib % 2]
                at_v = P3["tp3_free"][ib % 2]

                @block.tensor
                def _(t, c=c, par=par, ib=ib, tb3=tb3, yn_done=yn_done, at_v=at_v):
                    t.wait_ge(s_v.h, max(yn_done, at_v))
                    s_mm.inc(t.transpose(tb3[:, 0:128],
                                         xh3[par][:, ib * 128:(ib + 1) * 128],
                                         ident[:]))
                tpd = s_mm.n

                @BS
                def _(sc, ib=ib, tb3=tb3, tpd=tpd, ynf=P3["yn_free"]):
                    sc.wait_ge(s_mm.h, tpd)
                    if ib == 0:
                        sc.wait_ge(s_yn.h, ynf)
                    s_sc.inc(sc.activation(out=ynspill[:, ib * 128:(ib + 1) * 128],
                                           in_=tb3[:, 0:128], func=AF.Copy,
                                           scale=nw_sb[:, ib:ib + 1]))
                hid = s_sc.n

                @BV
                def _(v, ib=ib, tb3=tb3, hid=hid):
                    v.wait_ge(s_sc.h, hid)
                    s_v.inc(v.scalar_tensor_tensor(
                        out=ynspill[:, 4096 + ib * 128:4096 + (ib + 1) * 128],
                        in0=tb3[:, 0:128], scalar=nw_sb[:, ib:ib + 1],
                        in1=ynspill[:, ib * 128:(ib + 1) * 128],
                        op0=OP.mult, op1=OP.subtract))
                P3["tp3_free"][ib % 2] = s_v.n
            sp_ready = s_v.n

            @BG
            def _(g, c=c, sp_ready=sp_ready):
                g.wait_ge(s_v.h, sp_ready)
                dma(g, s_yn, d_ynsp[c].rearrange("p two ib c2 -> p (two ib c2)"),
                    ynspill[:])
            P3["yn_free"] = s_yn.n
            P3["xh3_free"][par] = sp_ready
        return p3_tp

    @block.sync
    def _(sy):
        for s_, n_ in fence3.items():
            sy.wait_ge(s_.h, n_)
        sy.wait_ge(s_gp.h, s_gp.n)

    tp_thunk = None
    for c in range(NCH):
        p3_partA(c)
        if tp_thunk is not None:
            tp_thunk()
        tp_thunk = p3_partB(c, init_done)
    tp_thunk()

    for cm in reversed(p3_ctx):
        cm.__exit__(None, None, None)
    for cm in reversed(scan_cms):
        cm.__exit__(None, None, None)

    if debug:
        d_ynd = dbg("dbg_yn", [NCH, 128, 2, 32, 128], BF16)

        @BG
        def _(g):
            g.wait_ge(s_yn.h, s_yn.n)
            dma(g, s_st, d_ynd, d_ynsp)

    p4_ctx = []

    def sb4(name, shape, dtype):
        cm = nc.sbuf_tensor(name, list(shape), dtype)
        t = cm.__enter__()
        p4_ctx.append(cm)
        return t

    def ps4(name, shape, dtype=F32):
        cm = nc.psum_tensor(name, list(shape), dtype)
        t = cm.__enter__()
        p4_ctx.append(cm)
        return t

    ynres = sb4("ynres", [128, 2, 32, T], BF16)
    wraw4 = [sb4(f"wraw4_{i}", [128, 32 * 128], F32) for i in range(2)]
    wh4 = [sb4(f"wh4_{i}", [128, 32 * 128], BF16) for i in range(2)]
    wl4 = [sb4(f"wl4_{i}", [128, 32 * 128], BF16) for i in range(2)]
    sbuf4 = [sb4(f"sbuf4_{i}", [128, T], F32) for i in range(2)]
    pk4 = [ps4(f"p4b{i}", [128, 512], F32) for i in range(4)]

    s_w4 = S("w4")
    s_s4 = S("s4")

    fence4 = {s: s.n for s in (s_v, s_sc, s_mm, s_st, s_gp, s_sp, s_ch, s_yn, s_ld)}

    s_y4 = [S("y4a"), S("y4b")]

    @block.sync
    def _(sy):
        for s_, n_ in fence4.items():
            sy.wait_ge(s_.h, n_)
        for c in range(NCH):
            dma(sy, s_y4[c % 2], ynres[:, :, :, c * 128:(c + 1) * 128], d_ynsp[c])

    yn_ld4 = (s_y4[0].n, s_y4[1].n)

    wraw4_free = [0, 0]
    whl4_free = [0, 0]
    pset4_free = [0, 0]
    sbuf4_free = [0, 0]
    split4 = {}

    def p4_load_split(ob):
        par = ob % 2

        @block.sync
        def _(sy, ob=ob, par=par, wfree=wraw4_free[par]):
            sy.wait_ge(s_v.h, wfree)
            dma(sy, s_w4, wraw4[par][:], d_wout[ob].rearrange("p ib o -> p (ib o)"))
        wld = s_w4.n

        @BS
        def _(sc, par=par, wld=wld, wf=whl4_free[par]):
            sc.wait_ge(s_w4.h, wld)
            sc.wait_ge(s_mm.h, wf)
            s_sc.inc(sc.copy(out=wh4[par][:], in_=wraw4[par][:]))
        whd = s_sc.n

        @BV
        def _(v, par=par, wld=wld, whd=whd, wf=whl4_free[par]):
            v.wait_ge(s_w4.h, wld)
            v.wait_ge(s_sc.h, whd)
            v.wait_ge(s_mm.h, wf)
            s_v.inc(v.tensor_tensor(out=wl4[par][:], in0=wraw4[par][:], in1=wh4[par][:],
                                    op=OP.subtract))
        wraw4_free[par] = s_v.n
        split4[ob] = (whd, s_v.n)

    def p4_mm_evac(ob):
        par = ob % 2
        need_ps = pset4_free[par]
        whd, wsplit = split4[ob]

        @block.tensor
        def _(t, ob=ob, par=par, whd=whd, wsplit=wsplit, need_ps=need_ps, yn_ld4=yn_ld4):
            t.wait_ge(s_y4[0].h, yn_ld4[0])
            t.wait_ge(s_y4[1].h, yn_ld4[1])
            t.wait_ge(s_sc.h, whd)
            t.wait_ge(s_v.h, max(wsplit, need_ps))
            p0_, p1_ = pk4[2 * par], pk4[2 * par + 1]
            last = None
            for ib in range(32):
                wha = wh4[par][:, ib * 128:(ib + 1) * 128]
                wla = wl4[par][:, ib * 128:(ib + 1) * 128]
                hi0 = ynres[:, 0, ib, 0:512]
                hi1 = ynres[:, 0, ib, 512:1024]
                lo0 = ynres[:, 1, ib, 0:512]
                lo1 = ynres[:, 1, ib, 512:1024]
                t.matmul(p0_[:], wha, hi0, start=(ib == 0), stop=False)
                t.matmul(p1_[:], wha, hi1, start=(ib == 0), stop=False)
                t.matmul(p0_[:], wha, lo0, start=False, stop=False)
                t.matmul(p1_[:], wha, lo1, start=False, stop=False)
                t.matmul(p0_[:], wla, hi0, start=False, stop=(ib == 31))
                last = t.matmul(p1_[:], wla, hi1, start=False, stop=(ib == 31))
            s_mm.inc(last)
        mm4 = s_mm.n
        whl4_free[par] = s_mm.n

        @BV
        def _(v, ob=ob, par=par, mm4=mm4, sfree=sbuf4_free[par]):
            v.wait_ge(s_mm.h, mm4)
            v.wait_ge(s_s4.h, sfree)
            v.tensor_copy(out=sbuf4[par][:, 0:512], in_=pk4[2 * par][:])
            s_v.inc(v.tensor_copy(out=sbuf4[par][:, 512:1024], in_=pk4[2 * par + 1][:]))
        ev4 = s_v.n
        pset4_free[par] = ev4

        @BG
        def _(g, ob=ob, par=par, ev4=ev4):
            g.wait_ge(s_v.h, ev4)
            dma(g, s_s4, d_ssp[ob], sbuf4[par][:])
        sbuf4_free[par] = s_s4.n

    for ob in range(16):
        p4_load_split(ob)
        if ob >= 1:
            p4_mm_evac(ob - 1)
    p4_mm_evac(15)

    for cm in reversed(p4_ctx):
        cm.__exit__(None, None, None)

    if debug:
        d_sd = dbg("dbg_s", [16, 128, T])

        @BG
        def _(g):
            g.wait_ge(s_s4.h, s_s4.n)
            dma(g, s_st, d_sd, d_ssp)

    p5o_ctx = []

    def sb5o(name, shape, dtype):
        cm = nc.sbuf_tensor(name, list(shape), dtype)
        t = cm.__enter__()
        p5o_ctx.append(cm)
        return t

    s2_b = sb5o("s2_b", [128, T], F32)
    o3_b = sb5o("o3_b", [128, T], F32)
    gneg = sb5o("gneg", [128, 4], F32)
    d_row = nc.dram_tensor("sp_row", [128, 8], F32).ap()

    @BV
    def _(v, gsc_done=gsc_done):
        v.wait_ge(s_gp.h, gsc_done)
        s_v.inc(v.tensor_scalar(out=gneg[:, 0:3], in0=gsc[:, 3:6], scalar1=-1.0,
                                scalar2=None, op0=OP.mult))
    gneg_done = s_v.n

    p5a_ctx = []

    def sb5a(name, shape, dtype):
        cm = nc.sbuf_tensor(name, list(shape), dtype)
        t = cm.__enter__()
        p5a_ctx.append(cm)
        return t

    p5a_ps_ctx = []

    def ps5a(name, shape, dtype=F32):
        cm = nc.psum_tensor(name, list(shape), dtype)
        t = cm.__enter__()
        p5a_ps_ctx.append(cm)
        return t

    gxg_b = sb5a("gxg_b", [128, T], F32)
    gxv_b = sb5a("gxv_b", [128, T], F32)
    rowbuf = sb5a("rowbuf", [1, T], F32)
    gx2c = sb5a("gx2c", [128, 8], F32)
    qxn5 = sb5a("qxn5", [128, NIB_D, T], BF16)
    wgr = [sb5a(f"wgr{i}", [128, 2048], F32) for i in range(2)]
    wvr = [sb5a(f"wvr{i}", [128, 2048], F32) for i in range(2)]
    tg = [sb5a(f"tg{i}", [128, 2048], BF16) for i in range(2)]
    tv = [sb5a(f"tv{i}", [128, 2048], BF16) for i in range(2)]
    tscr5 = sb5a("tscr5", [128, 2048], F32)
    pregate = sb5a("pregate", [128, T], F32)
    gatebuf = sb5a("gatebuf", [128, T], F32)
    vbuf = [sb5a(f"vbuf{i}", [128, T], F32) for i in range(2)]
    rmax = sb5a("rmax", [128, T], F32)
    rmin = sb5a("rmin", [128, T], F32)
    pg5 = [ps5a(f"p5g{i}", [128, 512], F32) for i in range(4)]
    pv5 = [ps5a(f"p5v{i}", [128, 512], F32) for i in range(4)]

    s_wg = S("wg5")
    s_wv = S("wv5")
    s_q5 = S("q5")
    s_v5 = S("v5")

    fence5 = {s: s.n for s in (s_v, s_sc, s_mm, s_st, s_gp, s_sp, s_s4)}

    @block.sync
    def _(sy):
        for s_, n_ in fence5.items():
            sy.wait_ge(s_.h, n_)
        dma(sy, s_q5, qxn5[:], d_qxn)
    q5ld = s_q5.n

    @BG
    def _(g, q5ld=q5ld):
        g.wait_ge(s_q5.h, q5ld)
        dma(g, s_st, d_row, gx1_cols[:, 0:NT], allow_slow_non_contiguous=True)
        g.wait_ge(s_st.h, s_st.n)
        dma(g, s_st, rowbuf.ap().rearrange("one (tt p) -> one tt p", tt=NT),
            d_row.rearrange("p tt -> tt p")[None, :, :], allow_slow_non_contiguous=True)
        g.wait_ge(s_st.h, s_st.n)
        s_gp.inc(g.nop())
    gx1row_done = s_gp.n

    @BV
    def _(v, gx1row_done=gx1row_done):
        v.wait_ge(s_gp.h, gx1row_done)
        v.tensor_scalar(out=gxg_b[0:1, :], in0=rowbuf[0:1, :],
                        scalar1=gsc[0:1, 0:1], scalar2=1.0 / 127.0,
                        op0=OP.mult, op1=OP.mult)
        s_v.inc(v.tensor_scalar(out=gxv_b[0:1, :], in0=rowbuf[0:1, :],
                                scalar1=gsc[0:1, 1:2], scalar2=1.0 / 127.0,
                                op0=OP.mult, op1=OP.mult))
    rows_done = s_v.n

    @BG
    def _(g, rows_done=rows_done):
        g.wait_ge(s_v.h, rows_done)
        g.partition_broadcast(out_ap=gxg_b[:], in_ap=gxg_b[0:1, :])
        s_gp.inc(g.partition_broadcast(out_ap=gxv_b[:], in_ap=gxv_b[0:1, :]))
    gx_b_done = s_gp.n

    @BV
    def _(v):
        v.memset(rmax[:], -1e30)
        s_v.inc(v.memset(rmin[:], 1e30))
    rinit = s_v.n

    wg_free = [0, 0]
    tern_free = [0, 0]
    ps5_free = [0, 0]
    vb_free = [0, 0]
    tern5 = {}

    def p5a_load_tern(hb):
        par = hb % 2

        @block.sync
        def _(sy, hb=hb, par=par, wf=wg_free[par]):
            sy.wait_ge(s_v.h, wf)
            dma(sy, s_wg, wgr[par][:], d_wg[hb].rearrange("p ib o -> p (ib o)"))
            dma(sy, s_wv, wvr[par][:], d_wv[hb].rearrange("p ib o -> p (ib o)"))
        wgld, wvld = s_wg.n, s_wv.n

        @BV
        def _(v, hb=hb, par=par, wgld=wgld, wvld=wvld, tf=tern_free[par],
              gneg_done=gneg_done):
            v.wait_ge(s_wg.h, wgld)
            v.wait_ge(s_wv.h, wvld)
            v.wait_ge(s_mm.h, tf)
            v.wait_ge(s_v.h, gneg_done)
            v.tensor_scalar(out=tscr5[:], in0=wgr[par][:], scalar1=gneg[:, 0:1],
                            scalar2=0.5, op0=OP.mult, op1=OP.is_ge)
            v.tensor_scalar(out=wgr[par][:], in0=wgr[par][:], scalar1=gsc[:, 3:4],
                            scalar2=0.5, op0=OP.mult, op1=OP.is_ge)
            v.tensor_tensor(out=tg[par][:], in0=tscr5[:], in1=wgr[par][:], op=OP.subtract)
            v.tensor_scalar(out=tscr5[:], in0=wvr[par][:], scalar1=gneg[:, 1:2],
                            scalar2=0.5, op0=OP.mult, op1=OP.is_ge)
            v.tensor_scalar(out=wvr[par][:], in0=wvr[par][:], scalar1=gsc[:, 4:5],
                            scalar2=0.5, op0=OP.mult, op1=OP.is_ge)
            s_v.inc(v.tensor_tensor(out=tv[par][:], in0=tscr5[:], in1=wvr[par][:],
                                    op=OP.subtract))
        wg_free[par] = s_v.n
        tern5[hb] = s_v.n

    def p5a_mm_evac(hb):
        par = hb % 2
        tern_done = tern5[hb]
        pgA, pgB = pg5[2 * par], pg5[2 * par + 1]
        pvA, pvB = pv5[2 * par], pv5[2 * par + 1]
        need_ps5 = ps5_free[par]

        @block.tensor
        def _(t, hb=hb, par=par, tern_done=tern_done, q5ld=q5ld, need_ps5=need_ps5,
              pgA=pgA, pgB=pgB, pvA=pvA, pvB=pvB):
            t.wait_ge(s_q5.h, q5ld)
            t.wait_ge(s_v.h, max(tern_done, need_ps5))
            last = None
            for ib in range(NIB_D):
                tga = tg[par][:, ib * 128:(ib + 1) * 128]
                tva = tv[par][:, ib * 128:(ib + 1) * 128]
                q0 = qxn5[:, ib, 0:512]
                q1 = qxn5[:, ib, 512:1024]
                t.matmul(pgA[:], tga, q0, start=(ib == 0), stop=(ib == 15))
                t.matmul(pgB[:], tga, q1, start=(ib == 0), stop=(ib == 15))
                t.matmul(pvA[:], tva, q0, start=(ib == 0), stop=(ib == 15))
                last = t.matmul(pvB[:], tva, q1, start=(ib == 0), stop=(ib == 15))
            s_mm.inc(last)
        mm5 = s_mm.n
        tern_free[par] = mm5

        @BV
        def _(v, hb=hb, mm5=mm5, gx_b_done=gx_b_done, pgA=pgA, pgB=pgB):
            v.wait_ge(s_mm.h, mm5)
            v.wait_ge(s_gp.h, gx_b_done)
            v.tensor_tensor(out=pregate[:, 0:512], in0=pgA[:], in1=gxg_b[:, 0:512],
                            op=OP.mult)
            s_v.inc(v.tensor_tensor(out=pregate[:, 512:1024], in0=pgB[:],
                                    in1=gxg_b[:, 512:1024], op=OP.mult))
        pre_done = s_v.n

        @BS
        def _(sc, hb=hb, pre_done=pre_done):
            sc.wait_ge(s_v.h, pre_done)
            s_sc.inc(sc.activation(out=gatebuf[:], in_=pregate[:], func=AF.Silu,
                                   bias=gb_sb[:, hb:hb + 1]))
        gate_done = s_sc.n

        @BV
        def _(v, hb=hb, par=par, gate_done=gate_done, vfree=vb_free[par],
              pvA=pvA, pvB=pvB):
            v.wait_ge(s_sc.h, gate_done)
            v.wait_ge(s_v5.h, vfree)
            v.tensor_tensor(out=pregate[:, 0:512], in0=pvA[:], in1=gxv_b[:, 0:512],
                            op=OP.mult)
            v.tensor_tensor(out=pregate[:, 512:1024], in0=pvB[:],
                            in1=gxv_b[:, 512:1024], op=OP.mult)
            v.scalar_tensor_tensor(out=vbuf[par][:], in0=pregate[:],
                                   scalar=vb_sb[:, hb:hb + 1], in1=gatebuf[:],
                                   op0=OP.add, op1=OP.mult)
            v.tensor_tensor(out=rmax[:], in0=rmax[:], in1=vbuf[par][:], op=OP.max)
            s_v.inc(v.tensor_tensor(out=rmin[:], in0=rmin[:], in1=vbuf[par][:], op=OP.min))
        v_done = s_v.n
        ps5_free[par] = v_done

        @BG
        def _(g, hb=hb, par=par, v_done=v_done):
            g.wait_ge(s_v.h, v_done)
            dma(g, s_v5, d_vsp[hb], vbuf[par][:])
        vb_free[par] = s_v5.n

    for hb in range(64):
        p5a_load_tern(hb)
        if hb >= 1:
            p5a_mm_evac(hb - 1)
    p5a_mm_evac(63)
    va_done = s_v.n

    @BV
    def _(v, va_done=va_done):
        v.wait_ge(s_v.h, va_done)
        v.tensor_scalar(out=rmin[:], in0=rmin[:], scalar1=-1.0, scalar2=None,
                        op0=OP.mult)
        s_v.inc(v.tensor_tensor(out=rmax[:], in0=rmax[:], in1=rmin[:], op=OP.max))
    gmx_done = s_v.n

    for tt in range(NT):
        at_v5 = s_v.n

        @block.tensor
        def _(t, tt=tt, gmx_done=gmx_done, at_v5=at_v5):
            t.wait_ge(s_v.h, max(gmx_done, at_v5))
            s_mm.inc(t.transpose(pg5[0][:, 0:128], rmax[:, tt * 128:(tt + 1) * 128],
                                 ident[:]))
        gt_ = s_mm.n

        @BV
        def _(v, tt=tt, gt_=gt_):
            v.wait_ge(s_mm.h, gt_)
            v.tensor_reduce(out=gx2c[:, tt:tt + 1], in_=pg5[0][:, 0:128], axis=AX,
                            op=OP.max)
            s_v.inc(v.tensor_scalar(out=gx2c[:, tt:tt + 1], in0=gx2c[:, tt:tt + 1],
                                    scalar1=1e-5, scalar2=None, op0=OP.max))
    gx2_done = s_v.n

    @BG
    def _(g, gx2_done=gx2_done):
        g.wait_ge(s_v.h, gx2_done)
        dma(g, s_st, d_row, gx2c[:, 0:8], allow_slow_non_contiguous=True)
        g.wait_ge(s_st.h, s_st.n)
        dma(g, s_st, rowbuf.ap().rearrange("one (tt p) -> one tt p", tt=NT),
            d_row.rearrange("p tt -> tt p")[None, :, :], allow_slow_non_contiguous=True)
        g.wait_ge(s_st.h, s_st.n)
        s_gp.inc(g.nop())
    gx2row_done = s_gp.n

    @BS
    def _(sc, gx2row_done=gx2row_done):
        sc.wait_ge(s_gp.h, gx2row_done)
        s_sc.inc(sc.activation(out=s2_b[0:1, :], in_=rowbuf[0:1, :], func=AF.Ln))
    lngx2 = s_sc.n

    @BS
    def _(sc, lngx2=lngx2):
        sc.wait_ge(s_sc.h, lngx2)
        s_sc.inc(sc.activation(out=s2_b[0:1, :], in_=s2_b[0:1, :], func=AF.Exp,
                               scale=-1.0, bias=ccol(LN127)[0:1, :]))
    s2row = s_sc.n

    @BV
    def _(v, s2row=s2row):
        v.wait_ge(s_sc.h, s2row)
        s_v.inc(v.tensor_scalar(out=o3_b[0:1, :], in0=rowbuf[0:1, :],
                                scalar1=gsc[0:1, 2:3], scalar2=1.0 / 127.0,
                                op0=OP.mult, op1=OP.mult))
    o3row = s_v.n

    @BG
    def _(g, o3row=o3row):
        g.wait_ge(s_v.h, o3row)
        g.partition_broadcast(out_ap=s2_b[:], in_ap=s2_b[0:1, :])
        s_gp.inc(g.partition_broadcast(out_ap=o3_b[:], in_ap=o3_b[0:1, :]))
    s2o3_done = s_gp.n

    for cm in reversed(p5a_ps_ctx):
        cm.__exit__(None, None, None)
    for cm in reversed(p5a_ctx):
        cm.__exit__(None, None, None)

    p5b_ctx = []

    def sb5b(name, shape, dtype):
        cm = nc.sbuf_tensor(name, list(shape), dtype)
        t = cm.__enter__()
        p5b_ctx.append(cm)
        return t

    def ps5b(name, shape, dtype=F32):
        cm = nc.psum_tensor(name, list(shape), dtype)
        t = cm.__enter__()
        p5b_ctx.append(cm)
        return t

    vqres = sb5b("vqres", [128, 64, T], BF16)
    vq_cms = [nc.sbuf_tensor(f"vrel{i}", [128, T], F32) for i in range(4)] + [
              nc.sbuf_tensor("qtmp0", [128, T], F32),
              nc.sbuf_tensor("qtmp1", [128, T], F32),
              nc.sbuf_tensor("qint0", [128, T], I32),
              nc.sbuf_tensor("qint1", [128, T], I32)]
    vrel = [cm.__enter__() for cm in vq_cms[:4]]
    qtmp = [vq_cms[4].__enter__(), vq_cms[5].__enter__()]
    qint = [vq_cms[6].__enter__(), vq_cms[7].__enter__()]
    s_vr = S("vr5")

    fence5b = {s: s.n for s in (s_v, s_sc, s_mm, s_st, s_gp, s_v5)}
    vrel_free = [0, 0, 0, 0]
    q_free = [0, 0]
    pend_corr = None

    for hb in range(64):
        par = hb % 4
        qpar = hb % 2

        @block.sync
        def _(sy, hb=hb, par=par, vfree=vrel_free[par]):
            if hb == 0:
                for s_, n_ in fence5b.items():
                    sy.wait_ge(s_.h, n_)
            sy.wait_ge(s_v.h, vfree)
            dma(sy, s_vr, vrel[par][:], d_vsp[hb])
        vld = s_vr.n

        @BV
        def _(v, hb=hb, par=par, qpar=qpar, vld=vld, s2o3_done=s2o3_done,
              qf=q_free[qpar]):
            v.wait_ge(s_vr.h, vld)
            v.wait_ge(s_gp.h, s2o3_done)
            v.wait_ge(s_v.h, qf)
            s_v.inc(v.tensor_tensor(out=qtmp[qpar][:], in0=vrel[par][:], in1=s2_b[:],
                                    op=OP.mult))
        qm_done = s_v.n

        @BS
        def _(sc, hb=hb, qpar=qpar, qm_done=qm_done):
            sc.wait_ge(s_v.h, qm_done)
            s_sc.inc(sc.activation(out=qint[qpar][:], in_=qtmp[qpar][:],
                                   func=AF.Identity, bias=ccol(0.5), scale=1.0))
        qi_done = s_sc.n

        def mk_corr(hb=hb, par=par, qpar=qpar, qi_done=qi_done):
            def emit():
                @BV
                def _(v):
                    v.wait_ge(s_sc.h, qi_done)
                    v.scalar_tensor_tensor(out=vrel[par][:], in0=qint[qpar][:],
                                           scalar=-0.5, in1=qtmp[qpar][:],
                                           op0=OP.add, op1=OP.is_gt)
                    s_v.inc(v.tensor_tensor(out=vqres[:, hb, :], in0=vrel[par][:],
                                            in1=qint[qpar][:], op=OP.subtract))
                vrel_free[par] = s_v.n
                q_free[qpar] = s_v.n
            return emit

        if pend_corr is not None:
            pend_corr()
        pend_corr = mk_corr()
    pend_corr()
    vq_done = s_v.n

    for cm in reversed(vq_cms):
        cm.__exit__(None, None, None)

    wor = [sb5b(f"wor{i}", [128, 2048], F32) for i in range(2)]
    to3 = [sb5b(f"to3h{i}", [128, 4096], BF16) for i in range(2)]
    tscr6 = sb5b("tscr6", [128, 2048], F32)
    gb5 = [sb5b(f"gb5_{i}", [128, T], F32) for i in range(2)]
    preo = sb5b("preo", [128, T], F32)
    sres = [sb5b("sres0", [128, T], F32)]
    sres = [sres[0], sres[0]]
    ub2 = [sb5b(f"ub2_{i}", [128, NT * 128], F32) for i in range(2)]
    po5 = [ps5b(f"p5o{i}", [128, 512], F32) for i in range(4)]
    pt5 = [ps5b(f"p5t{i}", [128, 512], F32) for i in range(2)]

    s_wo = S("wo5")
    s_sr = S("sr5")
    s_ub = S("ub5")
    s_out = S("out5")

    wor_free = [vq_done, vq_done]
    to3_free = [0, 0]
    po5_free = [vq_done, vq_done]
    sres_free = [vq_done, vq_done]
    ub_free = [vq_done, vq_done]
    tp5_free = [vq_done, vq_done]

    def p5b_main(ob):
        par = ob % 2

        @block.sync
        def _(sy, ob=ob, par=par, sfree=sres_free[par], ubf=ub_free[par]):
            sy.wait_ge(s_v.h, sfree)
            dma(sy, s_sr, sres[par][:], d_ssp[ob])
            if ob < 2:
                sy.wait_ge(s_v.h, ubf)
            else:
                sy.wait_ge(s_out.h, ubf)
            dma(sy, s_ub, ub2[par].ap().rearrange("p (tt o) -> p tt o", tt=NT),
                d_u.rearrange("(tt p) dd -> p tt dd", p=128)[:, :, ob * 128:(ob + 1) * 128])
        srld, ubld = s_sr.n, s_ub.n

        poA, poB = po5[2 * par], po5[2 * par + 1]
        need_po = po5_free[par]

        for half in range(2):
            hpar = half
            tern6_done = None
            for qq in range(2):
                wpar = (half * 2 + qq) % 2

                @block.sync
                def _(sy, ob=ob, half=half, qq=qq, wpar=wpar, wfree=wor_free[wpar]):
                    sy.wait_ge(s_v.h, wfree)
                    dma(sy, s_wo, wor[wpar][:],
                        d_wo[ob].rearrange("p ib o -> p (ib o)")
                        [:, half * 4096 + qq * 2048:half * 4096 + (qq + 1) * 2048])
                wold = s_wo.n

                @BV
                def _(v, qq=qq, hpar=hpar, wpar=wpar, wold=wold,
                      tfree=to3_free[hpar], gneg_done=gneg_done):
                    v.wait_ge(s_wo.h, wold)
                    v.wait_ge(s_v.h, gneg_done)
                    if qq == 0:
                        v.wait_ge(s_mm.h, tfree)
                    sl = slice(qq * 2048, (qq + 1) * 2048)
                    v.tensor_scalar(out=tscr6[:], in0=wor[wpar][:], scalar1=gneg[:, 2:3],
                                    scalar2=0.5, op0=OP.mult, op1=OP.is_ge)
                    v.tensor_scalar(out=wor[wpar][:], in0=wor[wpar][:], scalar1=gsc[:, 5:6],
                                    scalar2=0.5, op0=OP.mult, op1=OP.is_ge)
                    s_v.inc(v.tensor_tensor(out=to3[hpar][:, sl], in0=tscr6[:],
                                            in1=wor[wpar][:], op=OP.subtract))
                wor_free[wpar] = s_v.n
            tern6_done = s_v.n

            @block.tensor
            def _(t, ob=ob, par=par, half=half, hpar=hpar, tern6_done=tern6_done,
                  vq_done=vq_done, need_po=need_po, poA=poA, poB=poB):
                t.wait_ge(s_v.h, max(tern6_done, vq_done, need_po))
                last = None
                for ib2 in range(32):
                    ib = half * 32 + ib2
                    ta = to3[hpar][:, ib2 * 128:(ib2 + 1) * 128]
                    t.matmul(poA[:], ta, vqres[:, ib, 0:512],
                             start=(ib == 0), stop=(ib == 63))
                    last = t.matmul(poB[:], ta, vqres[:, ib, 512:1024],
                                    start=(ib == 0), stop=(ib == 63))
                s_mm.inc(last)
            to3_free[hpar] = s_mm.n
        mmo = s_mm.n

        @BV
        def _(v, ob=ob, mmo=mmo, s2o3_done=s2o3_done, poA=poA, poB=poB):
            v.wait_ge(s_mm.h, mmo)
            v.wait_ge(s_gp.h, s2o3_done)
            v.tensor_tensor(out=preo[:, 0:512], in0=poA[:], in1=o3_b[:, 0:512],
                            op=OP.mult)
            s_v.inc(v.tensor_tensor(out=preo[:, 512:1024], in0=poB[:],
                                    in1=o3_b[:, 512:1024], op=OP.mult))
        preo_done = s_v.n
        po5_free[par] = preo_done

        @BS
        def _(sc, ob=ob, par=par, preo_done=preo_done):
            sc.wait_ge(s_v.h, preo_done)
            s_sc.inc(sc.activation(out=gb5[par][:], in_=preo[:], func=AF.Silu,
                                   bias=obias_sb[:, ob:ob + 1]))
        g_done = s_sc.n

        @BV
        def _(v, ob=ob, par=par, g_done=g_done, srld=srld):
            v.wait_ge(s_sc.h, g_done)
            v.wait_ge(s_sr.h, srld)
            s_v.inc(v.tensor_tensor(out=gb5[par][:], in0=gb5[par][:], in1=sres[par][:],
                                    op=OP.mult))
        r_done = s_v.n
        sres_free[0] = sres_free[1] = r_done

        def p5_tp(ob=ob, par=par, r_done=r_done, ubld=ubld):
            for tt in range(NT):
                tb5 = pt5[tt % 2]
                need_tp5 = tp5_free[tt % 2]

                @block.tensor
                def _(t, ob=ob, tt=tt, tb5=tb5, r_done=r_done, need_tp5=need_tp5):
                    t.wait_ge(s_v.h, max(r_done, need_tp5))
                    s_mm.inc(t.transpose(tb5[:, 0:128],
                                         gb5[par][:, tt * 128:(tt + 1) * 128],
                                         ident[:]))
                tpd5 = s_mm.n

                @BV
                def _(v, ob=ob, par=par, tt=tt, tb5=tb5, tpd5=tpd5, ubld=ubld):
                    v.wait_ge(s_mm.h, tpd5)
                    v.wait_ge(s_ub.h, ubld)
                    s_v.inc(v.tensor_tensor(out=ub2[par][:, tt * 128:(tt + 1) * 128],
                                            in0=tb5[:, 0:128],
                                            in1=ub2[par][:, tt * 128:(tt + 1) * 128],
                                            op=OP.add))
                tp5_free[tt % 2] = s_v.n
            ob_ready = s_v.n

            @BG
            def _(g, ob=ob, par=par, ob_ready=ob_ready):
                g.wait_ge(s_v.h, ob_ready)
                dma(g, s_out,
                    d_out.rearrange("(tt p) dd -> p tt dd", p=128)[:, :, ob * 128:(ob + 1) * 128],
                    ub2[par].ap().rearrange("p (tt o) -> p tt o", tt=NT))
            ub_free[par] = s_out.n
        return p5_tp

    tp5_thunk = None
    for ob in range(16):
        th = p5b_main(ob)
        if tp5_thunk is not None:
            tp5_thunk()
        tp5_thunk = th
    tp5_thunk()

    for cm in reversed(p5b_ctx):
        cm.__exit__(None, None, None)
    for cm in reversed(p5o_ctx):
        cm.__exit__(None, None, None)

    @BG
    def _(g):
        g.wait_ge(s_out.h, s_out.n)
        g.wait_ge(s_st.h, s_st.n)
        g.nop()

    blk_cm.__exit__(None, None, None)
    for cm, _t in reversed(ctx):
        cm.__exit__(None, None, None)
    nc.finalize()
    return nc, ["out"] + dbg_outs


# version 12
# speedup vs baseline: 1.5156x; 1.0153x over previous
import numpy as np
import concourse.bacc as bacc
import concourse.mybir as mybir
from concourse.alu_op_type import AluOpType as OP
from concourse.bass_utils import run_bass_kernel_spmd

F32 = mybir.dt.float32
BF16 = mybir.dt.bfloat16
I32 = mybir.dt.int32
AF = mybir.ActivationFunctionType
AX = mybir.AxisListType.X

NCORE = 8
D = 2048
T = 1024
NT = 8
SEQ = 8192
DIN = 8512
DSSM = 4096
NH = 64
HD = 64
DS = 128
HID = 8192
CH = 128
NCH = 8
EPS = 1e-5
NOB_IN = 67
NIB_D = 16
LN127 = float(np.log(np.float32(127.0)))


class Sem:
    def __init__(self, nc, name):
        self._cm = nc.semaphore(name)
        self.h = self._cm.__enter__()
        self.n = 0

    def inc(self, instr, k=1):
        instr.then_inc(self.h, k)
        self.n += k
        return instr


def bc64(col_ap, nh):
    return col_ap[:, :, None].broadcast_to([128, nh, HD])


def dma(eng, sem, out, in_, **kw):
    eng.wait_ge(sem.h, sem.n)
    return sem.inc(eng.dma_start(out=out, in_=in_, **kw), 16)


def block_w(w):
    O, I = w.shape
    Opad = ((O + 127) // 128) * 128
    wp = np.zeros((Opad, I), np.float32)
    wp[:O] = w
    r = wp.reshape(Opad // 128, 128, I // 128, 128)
    return np.ascontiguousarray(r.transpose(0, 3, 2, 1))


def prep_in_maps(inputs):
    u_full = np.asarray(inputs["L0_attn_u"], np.float32).reshape(SEQ, D)
    w_in_b = block_w(np.asarray(inputs["L0_attn_W_in"], np.float32))
    w_out_b = block_w(np.asarray(inputs["L0_attn_W_out"], np.float32))
    w_g_b = block_w(np.asarray(inputs["L0_ffn_gate_w"], np.float32))
    w_v_b = block_w(np.asarray(inputs["L0_ffn_val_w"], np.float32))
    w_o_b = block_w(np.asarray(inputs["L0_ffn_out_w"], np.float32))
    conv_w = np.asarray(inputs["L0_attn_conv_w"], np.float32).reshape(4352, 4)
    conv_w_p = np.ascontiguousarray(conv_w.reshape(34, 128, 4).transpose(1, 0, 2))
    conv_b_p = np.ascontiguousarray(
        np.asarray(inputs["L0_attn_conv_b"], np.float32).reshape(34, 128).T)
    dtb = np.asarray(inputs["L0_attn_dt_bias"], np.float32).reshape(NH, 1)
    alog = np.asarray(inputs["L0_attn_A_log"], np.float32).reshape(NH, 1)
    dflat = np.asarray(inputs["L0_attn_D"], np.float32).reshape(1, NH)
    nwflat = np.asarray(inputs["L0_attn_norm_w"], np.float32).reshape(1, DSSM)
    norm_w_p = np.ascontiguousarray(
        np.asarray(inputs["L0_attn_norm_w"], np.float32).reshape(32, 128).T)
    gb_p = np.ascontiguousarray(
        np.asarray(inputs["L0_ffn_gate_b"], np.float32).reshape(64, 128).T)
    vb_p = np.ascontiguousarray(
        np.asarray(inputs["L0_ffn_val_b"], np.float32).reshape(64, 128).T)
    ob_p = np.ascontiguousarray(
        np.asarray(inputs["L0_ffn_out_b"], np.float32).reshape(16, 128).T)

    in_maps = []
    for c in range(NCORE):
        t0 = c * T
        halo = np.zeros((128, D), np.float32)
        if c > 0:
            halo[0:3] = u_full[t0 - 3:t0]
        mask = np.zeros((8, 1), np.float32)
        mask[:c] = 1.0
        in_maps.append({
            "u": np.ascontiguousarray(u_full[t0:t0 + T]),
            "u_halo": halo,
            "gam_g": np.ascontiguousarray(w_g_b[8 * c:8 * c + 8]),
            "gam_v": np.ascontiguousarray(w_v_b[8 * c:8 * c + 8]),
            "gam_o": np.ascontiguousarray(w_o_b[2 * c:2 * c + 2]),
            "w_in": w_in_b, "w_out": w_out_b,
            "w_g": w_g_b, "w_v": w_v_b, "w_o": w_o_b,
            "conv_w": conv_w_p, "conv_b": conv_b_p,
            "dtb": dtb, "alog": alog, "dflat": dflat,
            "nwflat": nwflat, "norm_w_p": norm_w_p,
            "gb": gb_p, "vb": vb_p, "obias": ob_p,
            "mask_lt": mask,
        })
    return in_maps


def kernel(**inputs):
    in_maps = prep_in_maps(inputs)
    nc, out_names = build(debug=False)
    res = run_bass_kernel_spmd(nc, in_maps, core_ids=list(range(NCORE)))
    out = np.concatenate([res.results[c]["out"] for c in range(NCORE)], axis=0)
    return out.reshape(1, SEQ, D)


def build(debug=False):
    nc = bacc.Bacc("TRN2", target_bir_lowering=False, debug=False)
    dbg_outs = []

    d_u = nc.dram_tensor("u", [T, D], F32, kind="ExternalInput").ap()
    d_uh = nc.dram_tensor("u_halo", [128, D], F32, kind="ExternalInput").ap()
    d_win = nc.dram_tensor("w_in", [NOB_IN, 128, NIB_D, 128], F32, kind="ExternalInput").ap()
    d_wout = nc.dram_tensor("w_out", [16, 128, 32, 128], F32, kind="ExternalInput").ap()
    d_wg = nc.dram_tensor("w_g", [64, 128, 16, 128], F32, kind="ExternalInput").ap()
    d_wv = nc.dram_tensor("w_v", [64, 128, 16, 128], F32, kind="ExternalInput").ap()
    d_wo = nc.dram_tensor("w_o", [16, 128, 64, 128], F32, kind="ExternalInput").ap()
    d_cw = nc.dram_tensor("conv_w", [128, 34, 4], F32, kind="ExternalInput").ap()
    d_cb = nc.dram_tensor("conv_b", [128, 34], F32, kind="ExternalInput").ap()
    d_dtb = nc.dram_tensor("dtb", [NH, 1], F32, kind="ExternalInput").ap()
    d_alog = nc.dram_tensor("alog", [NH, 1], F32, kind="ExternalInput").ap()
    d_dflat = nc.dram_tensor("dflat", [1, NH], F32, kind="ExternalInput").ap()
    d_nwflat = nc.dram_tensor("nwflat", [1, DSSM], F32, kind="ExternalInput").ap()
    d_nwp = nc.dram_tensor("norm_w_p", [128, 32], F32, kind="ExternalInput").ap()
    d_gb = nc.dram_tensor("gb", [128, 64], F32, kind="ExternalInput").ap()
    d_vb = nc.dram_tensor("vb", [128, 64], F32, kind="ExternalInput").ap()
    d_obias = nc.dram_tensor("obias", [128, 16], F32, kind="ExternalInput").ap()
    d_mask = nc.dram_tensor("mask_lt", [8, 1], F32, kind="ExternalInput").ap()
    d_gg = nc.dram_tensor("gam_g", [8, 128, 16, 128], F32, kind="ExternalInput").ap()
    d_gv = nc.dram_tensor("gam_v", [8, 128, 16, 128], F32, kind="ExternalInput").ap()
    d_go = nc.dram_tensor("gam_o", [2, 128, 64, 128], F32, kind="ExternalInput").ap()
    d_out = nc.dram_tensor("out", [T, D], F32, kind="ExternalOutput").ap()

    d_ztm = nc.dram_tensor("sp_ztm", [32, NT, 128, 128], F32).ap()
    d_xtm = nc.dram_tensor("sp_xtm", [32, NT, 128, 128], F32).ap()
    d_chain = nc.dram_tensor("sp_chain", [NCH, 128, DSSM], F32).ap()
    d_ynsp = nc.dram_tensor("sp_yn", [NCH, 128, 2, 32, 128], BF16).ap()
    d_vsp = nc.dram_tensor("sp_v", [64, 128, T], F32).ap()
    d_ssp = nc.dram_tensor("sp_s", [16, 128, T], F32).ap()
    d_qxn = nc.dram_tensor("sp_qxn", [128, NIB_D, T], BF16).ap()
    d_small = nc.dram_tensor("sp_small", [2 * NCH, NH], F32).ap()
    t_ccin = nc.dram_tensor("cc_in", [129, DSSM], F32)
    t_ccout = nc.dram_tensor("cc_out", [129 * NCORE, DSSM], F32,
                             addr_space="Shared")

    def dbg(name, shape, dtype=F32):
        h = nc.dram_tensor(name, list(shape), dtype, kind="ExternalOutput").ap()
        dbg_outs.append(name)
        return h

    sems = {}

    def S(name):
        if name not in sems:
            sems[name] = Sem(nc, name)
        return sems[name]

    s_ld = S("ld")
    s_st = S("st")
    s_v = S("v")
    s_sc = S("sc")
    s_mm = S("mm")
    s_gp = S("gp")
    s_cc = S("cc")

    ctx = []

    def sb(name, shape, dtype):
        cm = nc.sbuf_tensor(name, list(shape), dtype)
        t = cm.__enter__()
        ctx.append((cm, t))
        return t

    def ps(name, shape, dtype=F32):
        cm = nc.psum_tensor(name, list(shape), dtype)
        t = cm.__enter__()
        ctx.append((cm, t))
        return t

    blk_cm = nc.Block()
    block = blk_cm.__enter__()

    _DRAIN_OPS = {
        "tensor_scalar", "tensor_tensor", "tensor_copy", "tensor_reduce",
        "scalar_tensor_tensor", "tensor_tensor_scan", "reciprocal", "memset",
        "activation", "copy", "add", "mul", "select", "affine_select",
        "partition_broadcast", "iota", "transpose", "reduce_sum", "reduce_max",
        "tensor_add", "tensor_sub", "tensor_mul", "tensor_max",
        "tensor_scalar_add", "tensor_scalar_mul", "bn_stats", "bn_aggr",
        "square", "sign", "sqrt",
    }

    def _ap_span(x):
        t = getattr(x, "tensor", None)
        if t is None or not hasattr(x, "offset"):
            return None
        try:
            lo = int(x.offset)
            span = 1
            for st, ct in list(x.ap):
                span += (int(ct) - 1) * abs(int(st))
            return (t.name, lo, lo + span)
        except Exception:
            return (t.name, 0, 1 << 60)

    _eng_state = {}

    class EW:
        def __init__(self, e):
            object.__setattr__(self, "_e", e)
            key = id(type(e)), getattr(e, "engine", None).__repr__() if hasattr(e, "engine") else type(e).__name__
            st = _eng_state.setdefault(type(e).__name__, {"w": [], "r": []})
            object.__setattr__(self, "_st", st)

        def __getattr__(self, n):
            a = getattr(self._e, n)
            if n not in _DRAIN_OPS:
                return a
            st = self._st

            def wrapped(*args, **kw):
                outs, ins = [], []
                if n == "memset" and args:
                    s = _ap_span(args[0])
                    if s: outs.append(s)
                else:
                    for x in args:
                        s = _ap_span(x)
                        if s: ins.append(s)
                for k2, x in kw.items():
                    s = _ap_span(x)
                    if s is None:
                        continue
                    (outs if ("out" in k2) else ins).append(s)

                def ovl(a_, lst):
                    for b_ in lst:
                        if a_[0] == b_[0] and a_[1] < b_[2] and b_[1] < a_[2]:
                            return True
                    return False

                hazard = any(ovl(i, st["w"]) for i in ins) or \
                    any(ovl(o, st["w"]) or ovl(o, st["r"]) for o in outs)
                if hazard:
                    self._e.drain()
                    st["w"].clear()
                    st["r"].clear()
                r = a(*args, **kw)
                st["w"].extend(outs)
                st["r"].extend(ins)
                return r
            return wrapped

    def BV(f):
        block.vector(lambda e: f(EW(e)))

    def BS(f):
        block.scalar(lambda e: f(EW(e)))

    def BG(f):
        block.gpsimd(lambda e: f(EW(e)))

    ident = sb("ident", [128, 128], F32)
    zscratch = sb("zscratch", [128, 128], F32)
    mask01 = sb("mask01", [128, 128], F32)
    nw_sb = sb("nw_sb", [128, 32], F32)
    identb = sb("identb", [128, 128], BF16)
    maskneg = sb("maskneg", [128, 128], F32)
    cw_sb = sb("cw_sb", [128, 34 * 4], F32)
    cb_sb = sb("cb_sb", [128, 34], F32)
    dtb_sb = sb("dtb_sb", [NH, 1], F32)
    alog_sb = sb("alog_sb", [NH, 1], F32)
    negA = sb("negA", [NH, 1], F32)
    drow = sb("drow", [1, NH], F32)
    drep = sb("drep", [128, NH], F32)
    gb_sb = sb("gb_sb", [128, 64], F32)
    vb_sb = sb("vb_sb", [128, 64], F32)
    obias_sb = sb("obias_sb", [128, 16], F32)
    mask_sb = sb("mask_sb", [8, 1], F32)
    constcol = sb("constcol", [128, 8], F32)
    _constvals = {}

    def ccol(val):
        if val not in _constvals:
            idx = len(_constvals)
            assert idx < 8
            _constvals[val] = idx
        i = _constvals[val]
        return constcol[:, i:i + 1]

    for _v_ in (EPS, LN127, 1.0, 0.5, 30.0):
        ccol(_v_)

    @BV
    def _(v):
        items = list(_constvals.items())
        for val, idx in items[:-1]:
            v.memset(constcol[:, idx:idx + 1], float(val))
        val, idx = items[-1]
        s_v.inc(v.memset(constcol[:, idx:idx + 1], float(val)))
    consts_done = s_v.n

    gam_part = sb("gam_part", [128, 4], F32)
    gx1_cols = sb("gx1_cols", [128, NT], F32)
    gsc = sb("gsc", [128, 8], F32)
    scan_cms = []

    def sbsc(name, shape, dtype):
        cm = nc.sbuf_tensor(name, list(shape), dtype)
        t = cm.__enter__()
        scan_cms.append(cm)
        return t

    B_sb = sbsc("B_sb", [128, T], F32)
    C_sb = sbsc("C_sb", [128, T], F32)
    cs_ext = sbsc("cs_ext", [NH, T + 8], F32)
    logdt = sbsc("logdt", [NH, T], F32)
    bb64 = sbsc("bb64", [NH, T], F32)
    ones64 = sbsc("ones64", [NH, T], F32)
    dt_s = sbsc("dt_s", [NH, T], F32)
    bbT = sbsc("bbT", [128, NCH * NH], F32)
    sclT = sbsc("sclT", [128, NCH * NH], F32)
    exprelT = sbsc("exprelT", [128, NCH * NH], F32)
    aend_rep = sbsc("aend_rep", [128, NCH * NH], F32)
    rels_rep = sbsc("rels_rep", [128, NCH * NH], F32)



    s_pld = S("pld")

    @block.sync
    def _(sy):
        for dst, src_ in [(cw_sb, d_cw), (cb_sb, d_cb), (dtb_sb, d_dtb),
                          (alog_sb, d_alog), (drow, d_dflat), (nw_sb, d_nwp),
                          (gb_sb, d_gb), (vb_sb, d_vb), (obias_sb, d_obias),
                          (mask_sb, d_mask)]:
            dma(sy, s_pld, dst[:], src_)
    n_param_ld = s_pld.n

    @BV
    def _(v):
        v.memset(zscratch[:], 0.0)
        v.memset(mask01[:], 1.0)
        s_v.inc(v.memset(ones64[:], 1.0))
    zt_done = s_v.n

    @BG
    def _(g):
        g.wait_ge(s_v.h, zt_done)
        g.affine_select(out=ident[:], in_=zscratch[:], compare_op=OP.not_equal,
                        fill=1.0, base=0, pattern=[[-1, 128]], channel_multiplier=1)
        g.affine_select(out=identb[:], in_=zscratch[:], compare_op=OP.not_equal,
                        fill=1.0, base=0, pattern=[[-1, 128]], channel_multiplier=1)
        g.affine_select(out=maskneg[:], in_=zscratch[:], compare_op=OP.is_ge,
                        fill=-1e30, base=0, pattern=[[1, 128]], channel_multiplier=-1)
        s_gp.inc(g.affine_select(out=mask01[:], in_=mask01[:], compare_op=OP.is_ge,
                        fill=0.0, base=0, pattern=[[1, 128]], channel_multiplier=-1))
        g.wait_ge(s_pld.h, n_param_ld)
        s_gp.inc(g.partition_broadcast(out_ap=drep[:], in_ap=drow[0:1, :]))
    setup_gp = s_gp.n

    @BS
    def _(sc):
        sc.preload_act_table(AF.Exp)
        sc.wait_ge(s_pld.h, n_param_ld)
        sc.wait_ge(s_v.h, consts_done)
        s_sc.inc(sc.activation(out=negA[:], in_=alog_sb[:], func=AF.Exp, scale=1.0))
    negA_sc = s_sc.n

    @BV
    def _(v):
        v.wait_ge(s_sc.h, negA_sc)
        s_v.inc(v.tensor_scalar(out=negA[:], in0=negA[:], scalar1=-1.0,
                                scalar2=None, op0=OP.mult))

    xn_cms = [nc.sbuf_tensor("xnhi", [128, NIB_D, T + 4], BF16),
              nc.sbuf_tensor("xnlo", [128, NIB_D, T + 4], BF16)]
    xnhi = xn_cms[0].__enter__()
    xnlo = xn_cms[1].__enter__()
    THAL = T

    p0_ctx = []

    def sb0(name, shape, dtype):
        cm = nc.sbuf_tensor(name, list(shape), dtype)
        t = cm.__enter__()
        p0_ctx.append(cm)
        return t

    @BV
    def _(v):
        v.memset(xnhi.ap()[:, :, T:T + 4], 0.0)
        s_v.inc(v.memset(xnlo.ap()[:, :, T:T + 4], 0.0))

    ut = [sb0("ut0", [128, D], F32), sb0("ut1", [128, D], F32)]
    xnt = sb0("xnt", [128, D], F32)
    xct = sb0("xct", [128, D], F32)
    qi = sb0("qi", [128, D], I32)
    qc = sb0("qc", [128, D], F32)
    qout = sb0("qout", [128, D], BF16)
    hit = sb0("hit", [128, D], BF16)
    lot = sb0("lot", [128, D], BF16)
    qxn_fm = sb0("qxn_fm", [128, NIB_D, T], BF16)
    stat = sb0("stat", [128, 8], F32)
    pb_cm = [nc.psum_tensor(f"p0tp{i}", [128, 1024], BF16) for i in range(2)]
    pbank = [cm.__enter__() for cm in pb_cm]
    p0_ctx.extend(pb_cm)
    bank_free = [0, 0]

    tile_v_done = [0, 0]

    for tt in range(NT + 1):
        buf = ut[tt % 2]
        is_halo = tt == NT
        need_v = tile_v_done[tt % 2]

        @block.sync
        def _(sy, buf=buf, is_halo=is_halo, tt=tt, need_v=need_v):
            sy.wait_ge(s_v.h, need_v)
            src_ = d_uh if is_halo else d_u[tt * 128:(tt + 1) * 128, :]
            dma(sy, s_ld, buf[:], src_)
        ld_done = s_ld.n

        @BV
        def _(v, buf=buf, ld_done=ld_done):
            v.wait_ge(s_ld.h, ld_done)
            v.tensor_reduce(out=stat[:, 0:1], in_=buf[:], axis=AX, op=OP.add)
            v.tensor_scalar(out=stat[:, 0:1], in0=stat[:, 0:1], scalar1=1.0 / D,
                            scalar2=None, op0=OP.mult)
            v.tensor_scalar(out=xct[:], in0=buf[:], scalar1=stat[:, 0:1],
                            scalar2=None, op0=OP.subtract)
            v.tensor_tensor(out=qc[:], in0=xct[:], in1=xct[:], op=OP.mult)
            v.tensor_reduce(out=stat[:, 1:2], in_=qc[:], axis=AX, op=OP.add)
            s_v.inc(v.tensor_scalar(out=stat[:, 1:2], in0=stat[:, 1:2],
                                    scalar1=1.0 / D, scalar2=None, op0=OP.mult))
        tile_v_done[tt % 2] = s_v.n
        var_done = s_v.n

        @BS
        def _(sc, var_done=var_done):
            sc.wait_ge(s_v.h, var_done)
            sc.activation(out=stat[:, 2:3], in_=stat[:, 1:2], func=AF.Ln, bias=ccol(EPS), scale=1.0)
            s_sc.inc(sc.activation(out=stat[:, 2:3], in_=stat[:, 2:3], func=AF.Exp, scale=-0.5))
        rstd_done = s_sc.n

        @BV
        def _(v, rstd_done=rstd_done, is_halo=is_halo, tt=tt):
            v.wait_ge(s_sc.h, rstd_done)
            s_v.inc(v.tensor_scalar(out=xnt[:], in0=xct[:], scalar1=stat[:, 2:3],
                                    scalar2=None, op0=OP.mult))
            if not is_halo:
                v.tensor_reduce(out=stat[:, 3:4], in_=xnt[:], axis=AX,
                                op=OP.max, apply_absolute_value=True)
                s_v.inc(v.tensor_scalar(out=gx1_cols[:, tt:tt + 1], in0=stat[:, 3:4],
                                        scalar1=1e-5, scalar2=None, op0=OP.max))
        xn_done = s_v.n

        if not is_halo:
            @BS
            def _(sc, xn_done=xn_done):
                sc.wait_ge(s_v.h, xn_done)
                sc.activation(out=stat[:, 4:5], in_=stat[:, 3:4], func=AF.Ln)
                s_sc.inc(sc.activation(out=stat[:, 4:5], in_=stat[:, 4:5], func=AF.Exp,
                                       scale=-1.0, bias=ccol(LN127)))
            sc1_done = s_sc.n

            @BV
            def _(v, sc1_done=sc1_done):
                v.wait_ge(s_sc.h, sc1_done)
                v.tensor_scalar(out=qc[:], in0=xnt[:], scalar1=stat[:, 4:5],
                                scalar2=0.5, op0=OP.mult, op1=OP.add)
                v.tensor_copy(out=qi[:], in_=qc[:])
                v.scalar_tensor_tensor(out=xct[:], in0=qi[:], scalar=0.0,
                                       in1=qc[:], op0=OP.add, op1=OP.is_gt)
                s_v.inc(v.tensor_tensor(out=qout[:], in0=xct[:], in1=qi[:], op=OP.subtract))

        @BV
        def _(v):
            v.tensor_copy(out=hit[:], in_=xnt[:])
            s_v.inc(v.tensor_tensor(out=lot[:], in0=xnt[:], in1=hit[:], op=OP.subtract))
        split_done = s_v.n

        for ib in range(NIB_D):
            bi = ib % 2
            pb = pbank[bi]
            need_free = bank_free[bi]

            @block.tensor
            def _(t, ib=ib, pb=pb, need_free=need_free, split_done=split_done,
                  is_halo=is_halo):
                t.wait_ge(s_gp.h, 1)
                t.wait_ge(s_v.h, max(need_free, split_done))
                sl = slice(ib * 128, (ib + 1) * 128)
                if not is_halo:
                    t.transpose(pb[:, 0:128], qout[:, sl], identb[:])
                t.transpose(pb[:, 128:256], hit[:, sl], identb[:])
                s_mm.inc(t.transpose(pb[:, 256:384], lot[:, sl], identb[:]))
            tp_done = s_mm.n

            @BV
            def _(v, ib=ib, pb=pb, tp_done=tp_done, is_halo=is_halo, tt=tt):
                v.wait_ge(s_mm.h, tp_done)
                col = THAL if is_halo else tt * 128
                w = 3 if is_halo else 128
                if not is_halo:
                    v.tensor_copy(out=qxn_fm[:, ib, col:col + 128], in_=pb[:, 0:128])
                v.tensor_copy(out=xnhi[:, ib, col:col + w], in_=pb[:, 128:128 + w])
                s_v.inc(v.tensor_copy(out=xnlo[:, ib, col:col + w], in_=pb[:, 256:256 + w]))
            bank_free[bi] = s_v.n

    @BG
    def _(g, need=s_v.n):
        g.wait_ge(s_v.h, need)
        dma(g, s_st, d_qxn, qxn_fm[:])
    qxn_spilled = s_st.n

    if debug:
        d_xnhi = dbg("dbg_xnhi", [128, NIB_D, T + 4], BF16)
        d_xnlo = dbg("dbg_xnlo", [128, NIB_D, T + 4], BF16)
        d_qxnd = dbg("dbg_qxn", [128, NIB_D, T], BF16)

        @BG
        def _(g, need=s_v.n):
            g.wait_ge(s_v.h, need)
            dma(g, s_st, d_xnhi, xnhi[:])
            dma(g, s_st, d_xnlo, xnlo[:])
            dma(g, s_st, d_qxnd, qxn_fm[:])

    @BG
    def _(g):
        g.wait_ge(s_st.h, s_st.n)
        s_gp.inc(g.nop())
    p0_closed = s_gp.n

    for cm in reversed(p0_ctx):
        cm.__exit__(None, None, None)

    p1_ctx = []

    def sb1(name, shape, dtype):
        cm = nc.sbuf_tensor(name, list(shape), dtype)
        t = cm.__enter__()
        p1_ctx.append(cm)
        return t

    def ps1(name, shape, dtype=F32):
        cm = nc.psum_tensor(name, list(shape), dtype)
        t = cm.__enter__()
        p1_ctx.append(cm)
        return t

    wraw = [sb1(f"wraw{i}", [128, NIB_D * 128], F32) for i in range(2)]
    wh = [sb1(f"wh{i}", [128, NIB_D * 128], BF16) for i in range(2)]
    wl = [sb1(f"wl{i}", [128, NIB_D * 128], BF16) for i in range(2)]
    xraw = sb1("xraw", [128, 1027 + 5], F32)
    cacc = [sb1(f"cacc{i}", [128, T], F32) for i in range(2)]
    xsil = [sb1(f"xsil{i}", [128, T], F32) for i in range(2)]
    xsil_free = [0, 0]
    cacc_free = [0, 0]
    coll = [sb1(f"coll{i}", [128, NT * 128], F32) for i in range(2)]
    pk = [ps1(f"p1b{i}", [128, 512], F32) for i in range(6)]
    ptp_full = [ps1(f"p1t{i}", [128, 512], F32) for i in range(2)]
    ptp = [t[:, 0:128] for t in ptp_full]

    s_w = [S("w0"), S("w1")]
    s_sp = S("sp")
    wraw_free = [0, 0]
    whl_free = [0, 0]
    pset_free = [0, 0]
    tp_free = [0, 0]
    coll_free = [0, 0]

    @BS
    def _(sc):
        sc.preload_act_table(AF.Silu)

    def p1_transpose_spill(par, src_sil, dram_dst, cidx, sil_mark):
        cbuf = coll[cidx % 2]
        need_coll = coll_free[cidx % 2]
        for k in range(NT):
            tb = ptp[k % 2]
            need_tp = tp_free[k % 2]

            @block.tensor
            def _(t, k=k, tb=tb, need_tp=need_tp, need_coll=need_coll,
                  src_sil=src_sil, at_sc=sil_mark):
                t.wait_ge(s_sc.h, at_sc)
                t.wait_ge(s_v.h, need_tp)
                if need_coll:
                    t.wait_ge(s_sp.h, need_coll)
                s_mm.inc(t.transpose(tb, src_sil[:, k * 128:(k + 1) * 128], ident[:]))
            tpd = s_mm.n

            @BV
            def _(v, k=k, tb=tb, tpd=tpd, cbuf=cbuf):
                v.wait_ge(s_mm.h, tpd)
                s_v.inc(v.tensor_copy(out=cbuf[:, k * 128:(k + 1) * 128], in_=tb))
            tp_free[k % 2] = s_v.n
        xsil_free[par] = s_mm.n
        ev_done = s_v.n

        @BG
        def _(g, cbuf=cbuf, ev_done=ev_done, dram_dst=dram_dst):
            g.wait_ge(s_v.h, ev_done)
            dma(g, s_sp, dram_dst.rearrange("tt p c -> p tt c"),
                cbuf.ap().rearrange("p (tt c) -> p tt c", tt=NT))
        coll_free[cidx % 2] = s_sp.n

    def p1_evac_z(zb, par, mm_stop):
        po0, po1 = pk[3 * par], pk[3 * par + 1]
        need_xs = xsil_free[par]

        @BS
        def _(sc, po0=po0, po1=po1, mm_stop=mm_stop, need_xs=need_xs, par=par):
            sc.wait_ge(s_mm.h, max(mm_stop, need_xs))
            sc.activation(out=xsil[par][:, 0:512], in_=po0[:], func=AF.Silu)
            s_sc.inc(sc.activation(out=xsil[par][:, 512:1024], in_=po1[:], func=AF.Silu))
        pset_free[par] = -1
        sil_mark = s_sc.n
        return lambda: p1_transpose_spill(par, xsil[par], d_ztm[zb], zb, sil_mark)

    def p1_evac_x(xcb, par, mm_stop):
        po0, po1, ph = pk[3 * par], pk[3 * par + 1], pk[3 * par + 2]
        need_ca = cacc_free[par]

        @BV
        def _(v, po0=po0, po1=po1, ph=ph, mm_stop=mm_stop, xcb=xcb, par=par,
              need_ca=need_ca):
            v.wait_ge(s_mm.h, mm_stop)
            v.wait_ge(s_sc.h, need_ca)
            v.tensor_copy(out=xraw[:, 0:3], in_=ph[:, 0:3])
            v.tensor_copy(out=xraw[:, 3:515], in_=po0[:])
            v.tensor_copy(out=xraw[:, 515:1027], in_=po1[:])
            v.tensor_scalar(out=cacc[par][:], in0=xraw[:, 0:T],
                            scalar1=cw_sb[:, 4 * xcb:4 * xcb + 1],
                            scalar2=None, op0=OP.mult)
            for k2 in range(1, 3):
                v.scalar_tensor_tensor(out=cacc[par][:], in0=xraw[:, k2:k2 + T],
                                       scalar=cw_sb[:, 4 * xcb + k2:4 * xcb + k2 + 1],
                                       in1=cacc[par][:], op0=OP.mult, op1=OP.add)
            s_v.inc(v.scalar_tensor_tensor(out=cacc[par][:], in0=xraw[:, 3:3 + T],
                                           scalar=cw_sb[:, 4 * xcb + 3:4 * xcb + 4],
                                           in1=cacc[par][:], op0=OP.mult, op1=OP.add))
        conv_done = s_v.n
        pset_free[par] = conv_done

        if xcb < 32:
            need_xs = xsil_free[par]

            @BS
            def _(sc, xcb=xcb, conv_done=conv_done, par=par, need_xs=need_xs):
                sc.wait_ge(s_v.h, conv_done)
                sc.wait_ge(s_mm.h, need_xs)
                s_sc.inc(sc.activation(out=xsil[par][:, 0:T], in_=cacc[par][:],
                                       func=AF.Silu, bias=cb_sb[:, xcb:xcb + 1]))
            cacc_free[par] = s_sc.n
            sil_mark = s_sc.n
            return lambda: p1_transpose_spill(par, xsil[par], d_xtm[xcb], xcb, sil_mark)
        else:
            dst = B_sb if xcb == 32 else C_sb

            @BS
            def _(sc, xcb=xcb, conv_done=conv_done, dst=dst, par=par):
                sc.wait_ge(s_v.h, conv_done)
                s_sc.inc(sc.activation(out=dst[:], in_=cacc[par][:], func=AF.Silu,
                                       bias=cb_sb[:, xcb:xcb + 1]))
            cacc_free[par] = s_sc.n

    def p1_evac_dt(par, mm_stop):
        po0, po1 = pk[3 * par], pk[3 * par + 1]

        @BS
        def _(sc, po0=po0, po1=po1, mm_stop=mm_stop):
            sc.wait_ge(s_mm.h, mm_stop)
            sc.preload_act_table(AF.Exp)
            sc.activation(out=dt_s[:, 0:512], in_=po0[0:NH, :], func=AF.Exp,
                          bias=dtb_sb[:, 0:1])
            sc.activation(out=dt_s[:, 512:1024], in_=po1[0:NH, :], func=AF.Exp,
                          bias=dtb_sb[:, 0:1])
            sc.activation(out=dt_s[:, 0:512], in_=dt_s[:, 0:512], func=AF.Ln,
                          bias=ccol(1.0)[0:NH, :])
            s_sc.inc(sc.activation(out=dt_s[:, 512:1024], in_=dt_s[:, 512:1024],
                                   func=AF.Ln, bias=ccol(1.0)[0:NH, :]))
        dt_done = s_sc.n
        pset_free[par] = -2

        @BV
        def _(v, dt_done=dt_done):
            v.wait_ge(s_sc.h, dt_done)
            v.tensor_scalar(out=bb64[:], in0=dt_s[:], scalar1=negA[:, 0:1],
                            scalar2=None, op0=OP.mult)
            v.memset(cs_ext[:], 0.0)
            v.tensor_tensor_scan(out=cs_ext[:, 1:T + 1], data0=ones64[:],
                                 data1=bb64[:], initial=0.0, op0=OP.mult, op1=OP.add)
            s_v.inc(v.tensor_scalar(out=logdt[:], in0=dt_s[:], scalar1=1e-19,
                                    scalar2=None, op0=OP.max))
        l_done = s_v.n

        @BS
        def _(sc, l_done=l_done):
            sc.wait_ge(s_v.h, l_done)
            s_sc.inc(sc.activation(out=logdt[:], in_=logdt[:], func=AF.Ln))
        ln_done = s_sc.n

        @BV
        def _(v, ln_done=ln_done):
            v.wait_ge(s_sc.h, ln_done)
            s_v.inc(v.tensor_tensor(out=bb64[:], in0=logdt[:], in1=cs_ext[:, 1:T + 1],
                                    op=OP.subtract))
        return s_v.n

    gbuf = [sb1(f"gbuf{i}", [128, 2048], F32) for i in range(2)]
    gtmp = sb1("gtmp", [128, 2], F32)
    s_gm = [S("gm0"), S("gm1")]
    gq_free = [0, 0]

    @BV
    def _(v):
        s_v.inc(v.memset(gam_part[:], 0.0))

    gpieces = []
    for ob_ in range(8):
        gpieces.append((0, d_gg[ob_].rearrange("p ib o -> p (ib o)")))
    for ob_ in range(8):
        gpieces.append((1, d_gv[ob_].rearrange("p ib o -> p (ib o)")))
    for ob_ in range(2):
        flat = d_go[ob_].rearrange("p ib o -> p (ib o)")
        for k_ in range(4):
            gpieces.append((2, flat[:, k_ * 2048:(k_ + 1) * 2048]))

    for gi, (mat, srcap) in enumerate(gpieces):
        gpar = gi % 2

        @BG
        def _(g, gi=gi, gpar=gpar, srcap=srcap):
            g.wait_ge(s_v.h, gq_free[gpar])
            dma(g, s_gm[gpar], gbuf[gpar][:], srcap)
        gld = s_gm[gpar].n

        @BV
        def _(v, gi=gi, gpar=gpar, mat=mat, gld=gld):
            v.wait_ge(s_gm[gpar].h, gld)
            v.tensor_reduce(out=gtmp[:, 0:1], in_=gbuf[gpar][:], axis=AX,
                            op=OP.add, apply_absolute_value=True)
            s_v.inc(v.tensor_tensor(out=gam_part[:, mat:mat + 1],
                                    in0=gam_part[:, mat:mat + 1],
                                    in1=gtmp[:, 0:1], op=OP.add))
        gq_free[gpar] = s_v.n
    gam_done = s_v.n

    xn_ready = s_v.n
    fence = {s: s.n for s in (s_v, s_sc, s_mm, s_st, s_gp)}
    split_marks = {}

    def p1_load_split(ob):
        par = ob % 2

        @block.sync
        def _(sy, ob=ob, par=par):
            if ob == 0:
                for s_, n_ in fence.items():
                    sy.wait_ge(s_.h, n_)
            sy.wait_ge(s_v.h, wraw_free[par])
            dma(sy, s_w[par], wraw[par][:], d_win[ob])
        wld = s_w[par].n

        @BS
        def _(sc, par=par, wld=wld, wf=whl_free[par]):
            sc.wait_ge(s_w[par].h, wld)
            sc.wait_ge(s_mm.h, wf)
            s_sc.inc(sc.copy(out=wh[par][:], in_=wraw[par][:]))
        whd = s_sc.n

        @BV
        def _(v, par=par, wld=wld, whd=whd, wf=whl_free[par]):
            v.wait_ge(s_w[par].h, wld)
            v.wait_ge(s_sc.h, whd)
            v.wait_ge(s_mm.h, wf)
            s_v.inc(v.tensor_tensor(out=wl[par][:], in0=wraw[par][:], in1=wh[par][:],
                                    op=OP.subtract))
        wraw_free[par] = s_v.n
        split_marks[ob] = (whd, s_v.n)

    def p1_mm_evac(ob):
        par = ob % 2
        is_x = 32 <= ob < 66
        need_pset = pset_free[par]
        whd, wld_split = split_marks[ob]

        @block.tensor
        def _(t, ob=ob, par=par, whd=whd, wld_split=wld_split, is_x=is_x,
              need_pset=need_pset, xn_ready=xn_ready):
            t.wait_ge(s_sc.h, whd)
            t.wait_ge(s_v.h, wld_split)
            t.wait_ge(s_v.h, xn_ready)
            if need_pset > 0:
                t.wait_ge(s_v.h, need_pset)
            elif need_pset == -1 or need_pset == -2:
                t.wait_ge(s_sc.h, s_sc.n)
            po0, po1, ph = pk[3 * par], pk[3 * par + 1], pk[3 * par + 2]
            first = True
            last = None
            for ib in range(NIB_D):
                wha = wh[par][:, ib * 128:(ib + 1) * 128]
                wla = wl[par][:, ib * 128:(ib + 1) * 128]
                hi0 = xnhi[:, ib, 0:512]
                hi1 = xnhi[:, ib, 512:1024]
                lo0 = xnlo[:, ib, 0:512]
                lo1 = xnlo[:, ib, 512:1024]
                t.matmul(po0[:], wha, hi0, start=first, stop=False)
                t.matmul(po1[:], wha, hi1, start=first, stop=False)
                t.matmul(po0[:], wha, lo0, start=False, stop=False)
                t.matmul(po1[:], wha, lo1, start=False, stop=False)
                if is_x:
                    t.matmul(ph[:, 0:3], wha, xnhi[:, ib, T:T + 3], start=first, stop=False)
                    t.matmul(ph[:, 0:3], wha, xnlo[:, ib, T:T + 3], start=False, stop=False)
                t.matmul(po0[:], wla, hi0, start=False, stop=(ib == NIB_D - 1))
                last = t.matmul(po1[:], wla, hi1, start=False, stop=(ib == NIB_D - 1))
                if is_x:
                    last = t.matmul(ph[:, 0:3], wla, xnhi[:, ib, T:T + 3], start=False,
                                    stop=(ib == NIB_D - 1))
                first = False
            s_mm.inc(last)
        mm_stop = s_mm.n
        whl_free[par] = s_mm.n

        if ob < 32:
            return ("T", p1_evac_z(ob, par, mm_stop))
        elif is_x:
            return ("T", p1_evac_x(ob - 32, par, mm_stop))
        else:
            return ("dt", p1_evac_dt(par, mm_stop))

    dt_chain_done = None
    pending_T = None

    def run_mm_evac(ob):
        nonlocal dt_chain_done, pending_T
        kind, val = p1_mm_evac(ob)
        prev = pending_T
        pending_T = val if kind == "T" else None
        if kind == "dt":
            dt_chain_done = val
        if prev is not None:
            prev()

    for ob in range(NOB_IN):
        p1_load_split(ob)
        if ob >= 1:
            run_mm_evac(ob - 1)
    run_mm_evac(NOB_IN - 1)
    if pending_T is not None:
        pending_T()

    smallT = sb1("smallT", [NH, 128], F32)
    smallT2 = sb1("smallT2", [128, 128], F32)

    @BV
    def _(v):
        v.wait_ge(s_v.h, dt_chain_done)
        v.memset(smallT[:], 0.0)
        for c in range(NCH):
            v.tensor_scalar(out=smallT[:, c:c + 1],
                            in0=cs_ext[0:NH, (c + 1) * 128:(c + 1) * 128 + 1],
                            scalar1=cs_ext[0:NH, c * 128:c * 128 + 1],
                            scalar2=None, op0=OP.subtract)
            v.tensor_copy(out=smallT[:, NCH + c:NCH + c + 1],
                          in_=cs_ext[0:NH, c * 128:c * 128 + 1])
        s_v.inc(v.memset(smallT[:, 2 * NCH:2 * NCH + 1], 0.0))
    sm_args = s_v.n

    @BS
    def _(sc, sm_args=sm_args):
        sc.wait_ge(s_v.h, sm_args)
        s_sc.inc(sc.activation(out=smallT[:, 0:2 * NCH], in_=smallT[:, 0:2 * NCH], func=AF.Exp))
    sm_exp = s_sc.n

    @block.tensor
    def _(t, sm_exp=sm_exp):
        t.wait_ge(s_sc.h, sm_exp)
        t.wait_ge(s_v.h, tp_free[0])
        s_mm.inc(t.transpose(ptp_full[0][:, 0:NH], smallT[:], ident[0:NH, 0:NH]))
    smT_done = s_mm.n

    @BV
    def _(v, smT_done=smT_done):
        v.wait_ge(s_mm.h, smT_done)
        s_v.inc(v.tensor_copy(out=smallT2[:, 0:NH], in_=ptp_full[0][:, 0:NH]))
    smT2_done = s_v.n

    smflat = sb1("smflat", [1, 2 * NCH * NH], F32)

    @BG
    def _(g, smT2_done=smT2_done):
        g.wait_ge(s_v.h, smT2_done)
        dma(g, s_st, d_small, smallT2[0:2 * NCH, 0:NH])
        g.wait_ge(s_st.h, s_st.n)
        dma(g, s_st, smflat[0:1, :], d_small.rearrange("r c -> (r c)")[None, :])
        g.wait_ge(s_st.h, s_st.n)
        g.partition_broadcast(out_ap=aend_rep[:], in_ap=smflat[0:1, 0:NCH * NH])
        s_gp.inc(g.partition_broadcast(out_ap=rels_rep[:],
                                       in_ap=smflat[0:1, NCH * NH:2 * NCH * NH]))
    reps_done = s_gp.n

    scl64 = sb1("scl64", [NH, 128], F32)
    exr64 = sb1("exr64", [NH, 128], F32)
    for c in range(NCH):
        @BV
        def _(v, c=c):
            v.wait_ge(s_v.h, s_v.n)
            v.tensor_scalar(out=scl64[:], in0=bb64[:, c * 128:(c + 1) * 128],
                            scalar1=cs_ext[0:NH, (c + 1) * 128:(c + 1) * 128 + 1],
                            scalar2=None, op0=OP.add)
            s_v.inc(v.tensor_scalar(out=exr64[:], in0=cs_ext[0:NH, c * 128 + 1:(c + 1) * 128 + 1],
                                    scalar1=cs_ext[0:NH, c * 128:c * 128 + 1],
                                    scalar2=None, op0=OP.subtract))
        args_done = s_v.n

        @BS
        def _(sc, c=c, args_done=args_done):
            sc.wait_ge(s_v.h, args_done)
            sc.activation(out=scl64[:], in_=scl64[:], func=AF.Exp)
            s_sc.inc(sc.activation(out=exr64[:], in_=exr64[:], func=AF.Exp))
        exps_done = s_sc.n

        for srcbuf, dstbuf in ((bb64, bbT), (scl64, sclT), (exr64, exprelT)):
            @block.tensor
            def _(t, c=c, srcbuf=srcbuf, exps_done=exps_done, need_tp=tp_free[1]):
                t.wait_ge(s_sc.h, exps_done)
                t.wait_ge(s_v.h, need_tp)
                src_ap = srcbuf[:, c * 128:(c + 1) * 128] if srcbuf is bb64 else srcbuf[:]
                s_mm.inc(t.transpose(ptp_full[1][:, 0:NH], src_ap, ident[0:NH, 0:NH]))
            td = s_mm.n

            @BV
            def _(v, c=c, dstbuf=dstbuf, td=td):
                v.wait_ge(s_mm.h, td)
                s_v.inc(v.tensor_copy(out=dstbuf[:, c * NH:(c + 1) * NH], in_=ptp_full[1][:, 0:NH]))
            tp_free[1] = s_v.n
    prep_done = s_v.n

    if debug:
        d_dts = dbg("dbg_dts", [NH, T])
        d_cse = dbg("dbg_cse", [NH, T + 8])
        d_Bsb = dbg("dbg_B", [128, T])
        d_Csb = dbg("dbg_C", [128, T])
        d_zt = dbg("dbg_ztm", [32, NT, 128, 128])
        d_xt = dbg("dbg_xtm", [32, NT, 128, 128])

        @BG
        def _(g):
            g.wait_ge(s_v.h, prep_done)
            g.wait_ge(s_sp.h, s_sp.n)
            dma(g, s_st, d_dts, dt_s[:])
            dma(g, s_st, d_cse, cs_ext[:])
            dma(g, s_st, d_Bsb, B_sb[:])
            dma(g, s_st, d_Csb, C_sb[:])
            dma(g, s_st, d_zt, d_ztm)
            dma(g, s_st, d_xt, d_xtm)

    for cm in reversed(p1_ctx):
        cm.__exit__(None, None, None)
    for cm in reversed(xn_cms):
        cm.__exit__(None, None, None)

    p2_ctx = []

    def sb2(name, shape, dtype):
        cm = nc.sbuf_tensor(name, list(shape), dtype)
        t = cm.__enter__()
        p2_ctx.append(cm)
        return t

    def ps2(name, shape, dtype=F32):
        cm = nc.psum_tensor(name, list(shape), dtype)
        t = cm.__enter__()
        p2_ctx.append(cm)
        return t

    init_sb = sbsc("init_sb", [128, DSSM], F32)
    c8rep = sbsc("c8rep", [128, 8 * NH], F32)

    xh_t = [sb2(f"xh_t{i}", [128, DSSM], F32) for i in range(2)]
    xsc = sb2("xsc", [128, DSSM], F32)
    BT_sb = sb2("BT_sb", [128, 128], F32)
    chain = [sb2(f"chain{i}", [128, DSSM], F32) for i in range(2)]
    pst = [ps2(f"p2st{i}", [128, 512], F32) for i in range(2)]
    ptr2_full = ps2("p2tp", [128, 512], F32)
    s_x = S("x2")
    s_ch = S("ch")

    fence2 = {s: s.n for s in (s_v, s_sc, s_mm, s_st, s_gp, s_sp)}

    @BV
    def _(v):
        for s_, n_ in fence2.items():
            v.wait_ge(s_.h, n_)
        s_v.inc(v.memset(chain[0][:], 0.0))
    chain_ready = s_v.n

    xh_free = [0, 0]
    pst_free2 = [0, 0]
    chsp_prev = 0
    xsc_free = 0
    for c in range(NCH):
        par = c % 2

        @block.sync
        def _(sy, c=c, par=par, xhf=xh_free[par], chr_=chain_ready):
            sy.wait_ge(s_v.h, max(xhf, chr_))
            dma(sy, s_x, xh_t[par].ap().rearrange("p (ob c2) -> p ob c2", ob=32),
                d_xtm[:, c].rearrange("ob p c2 -> p ob c2"))
        xld = s_x.n

        @BV
        def _(v, c=c, par=par, xld=xld, xscf=xsc_free):
            v.wait_ge(s_x.h, xld)
            v.wait_ge(s_mm.h, xscf)
            s_v.inc(v.tensor_tensor(
                out=xsc.ap().rearrange("p (h q) -> p h q", h=NH),
                in0=xh_t[par].ap().rearrange("p (h q) -> p h q", h=NH),
                in1=bc64(sclT[:, c * NH:(c + 1) * NH], NH), op=OP.mult))
        xh_free[par] = s_v.n
        xsc_done = s_v.n

        @block.tensor
        def _(t, c=c):
            t.wait_ge(s_v.h, s_v.n)
            s_mm.inc(t.transpose(ptr2_full[:, 0:128], B_sb[:, c * 128:(c + 1) * 128],
                                 ident[:]))
        btd = s_mm.n

        @BV
        def _(v, btd=btd):
            v.wait_ge(s_mm.h, btd)
            s_v.inc(v.tensor_copy(out=BT_sb[:], in_=ptr2_full[:, 0:128]))
        bt_ready = s_v.n

        @BG
        def _(g, c=c, par=par, chain_ready=chain_ready):
            g.wait_ge(s_v.h, chain_ready)
            dma(g, s_ch, d_chain[c], chain[par][:])
        chsp = chsp_prev
        chsp_prev_next = s_ch.n

        for q in range(8):
            pb2 = pst[q % 2]
            at_v = pst_free2[q % 2]

            @block.tensor
            def _(t, q=q, pb2=pb2, xsc_done=xsc_done, bt_ready=bt_ready, at_v=at_v):
                t.wait_ge(s_v.h, max(xsc_done, bt_ready, at_v))
                s_mm.inc(t.matmul(pb2[:], BT_sb[:], xsc[:, q * 512:(q + 1) * 512],
                                  start=True, stop=True))
            mmd = s_mm.n

            @BV
            def _(v, q=q, c=c, par=par, pb2=pb2, mmd=mmd, chsp=chsp):
                v.wait_ge(s_mm.h, mmd)
                v.wait_ge(s_ch.h, chsp)
                sl = slice(q * 512, (q + 1) * 512)
                v.tensor_tensor(
                    out=chain[1 - par].ap()[:, sl].rearrange("p (h q2) -> p h q2", h=8),
                    in0=chain[par].ap()[:, sl].rearrange("p (h q2) -> p h q2", h=8),
                    in1=bc64(aend_rep[:, c * NH + q * 8:c * NH + (q + 1) * 8], 8),
                    op=OP.mult)
                s_v.inc(v.tensor_tensor(out=chain[1 - par][:, sl],
                                        in0=chain[1 - par][:, sl],
                                        in1=pb2[:], op=OP.add))
            pst_free2[q % 2] = s_v.n
        chain_ready = s_v.n
        chsp_prev = chsp_prev_next
        xsc_free = s_mm.n

    final_par = NCH % 2

    gam3 = sb2("gam3", [8, 8], F32)

    @block.tensor
    def _(t):
        t.wait_ge(s_v.h, s_v.n)
        s_mm.inc(t.transpose(ptr2_full[0:4, 0:128], gam_part[:, 0:4], ident[:]))
    g3t = s_mm.n

    @BV
    def _(v, g3t=g3t):
        v.wait_ge(s_mm.h, g3t)
        s_v.inc(v.tensor_reduce(out=gam3[0:4, 0:1], in_=ptr2_full[0:4, 0:128],
                                axis=AX, op=OP.add))
    g3_done = s_v.n

    zrow = sb2("zrow", [1, DSSM], F32)

    @BV
    def _(v):
        s_v.inc(v.memset(zrow[:], 0.0))
    zrow_done = s_v.n

    @BG
    def _(g, g3_done=g3_done, final_par=final_par, zrow_done=zrow_done):
        g.wait_ge(s_v.h, max(g3_done, zrow_done))
        dma(g, s_st, t_ccin.ap()[128:129, :], zrow[0:1, :])
        dma(g, s_st, t_ccin.ap()[0:128, :], chain[final_par][:])
        dma(g, s_st, t_ccin.ap()[128:129, 0:NH].rearrange("one h -> h one"),
            cs_ext[0:NH, T:T + 1])
        dma(g, s_st, t_ccin.ap()[128:129, NH:NH + 3].rearrange("one m -> m one"),
            gam3[0:3, 0:1])
        g.wait_ge(s_st.h, s_st.n)
        s_cc.inc(g.collective_compute(
            "AllGather", OP.bypass,
            replica_groups=[list(range(NCORE))],
            ins=[t_ccin.ap().opt()], outs=[t_ccout.ap().opt()]))
    cc_done = s_cc.n

    ext8T = sb2("ext8T", [80, 8], F32)
    mrow = sb2("mrow", [80, 8], F32)
    c8 = sb2("c8", [NH, 16], F32)

    @block.sync
    def _(sy, cc_done=cc_done):
        sy.wait_ge(s_cc.h, cc_done)
        dma(sy, s_ld, ext8T[0:NH + 3, :],
            t_ccout.ap().rearrange("(j r) q -> j r q", r=129)[:, 128, 0:NH + 3]
            .rearrange("j q -> q j"), allow_slow_non_contiguous=True)
        dma(sy, s_ld, mrow[0:1, :], d_mask.rearrange("j one -> (j one)")[None, :])
    ext_ld = s_ld.n

    @BG
    def _(g, ext_ld=ext_ld):
        g.wait_ge(s_ld.h, ext_ld)
        s_gp.inc(g.partition_broadcast(out_ap=mrow[0:NH, :], in_ap=mrow[0:1, :]))
    mrep_done = s_gp.n

    @BV
    def _(v, mrep_done=mrep_done):
        v.wait_ge(s_gp.h, mrep_done)
        v.tensor_tensor(out=c8[:, 8:16], in0=ext8T[0:NH, :], in1=mrow[0:NH, :],
                        op=OP.mult)
        v.memset(c8[:, 7:8], 0.0)
        for j in range(6, -1, -1):
            v.scalar_tensor_tensor(out=c8[:, j:j + 1], in0=c8[:, j + 1:j + 2],
                                   scalar=1.0, in1=c8[:, 8 + j + 1:8 + j + 2],
                                   op0=OP.mult, op1=OP.add)
        s_v.inc(v.memset(c8[:, 15:16], 0.0))
    sfx_done = s_v.n

    @BS
    def _(sc, sfx_done=sfx_done):
        sc.wait_ge(s_v.h, sfx_done)
        s_sc.inc(sc.activation(out=c8[:, 0:8], in_=c8[:, 0:8], func=AF.Exp))
    c8e_done = s_sc.n

    @BV
    def _(v, c8e_done=c8e_done):
        v.wait_ge(s_sc.h, c8e_done)
        s_v.inc(v.tensor_tensor(out=c8[:, 0:8], in0=c8[:, 0:8], in1=mrow[0:NH, :],
                                op=OP.mult))
    c8_done = s_v.n

    d_c8 = nc.dram_tensor("sp_c8", [NH, 8], F32).ap()
    c8row = sb2("c8row", [1, 8 * NH], F32)

    @BG
    def _(g, c8_done=c8_done):
        g.wait_ge(s_v.h, c8_done)
        dma(g, s_st, d_c8, c8[0:NH, 0:8])
        g.wait_ge(s_st.h, s_st.n)
        dma(g, s_st, c8row.ap().rearrange("one (j h) -> one j h", j=8),
            d_c8.rearrange("h j -> j h")[None, :, :], allow_slow_non_contiguous=True)
        g.wait_ge(s_st.h, s_st.n)
        s_gp.inc(g.partition_broadcast(out_ap=c8rep[:], in_ap=c8row[0:1, :]))
    c8rep_done = s_gp.n

    grow = sb2("grow", [1, 8], F32)

    @BV
    def _(v, mrep_done=mrep_done):
        v.wait_ge(s_gp.h, mrep_done)
        s_v.inc(v.tensor_reduce(out=ext8T[NH:NH + 3, 0:1], in_=ext8T[NH:NH + 3, :],
                                axis=AX, op=OP.add))
    gsum_done = s_v.n

    NW_GATE = float(HID * D)

    @BS
    def _(sc, gsum_done=gsum_done):
        sc.wait_ge(s_v.h, gsum_done)
        s_sc.inc(sc.activation(out=ext8T[NH:NH + 3, 1:2], in_=ext8T[NH:NH + 3, 0:1],
                               func=AF.Ln))
    gln_done = s_sc.n

    LNW = float(np.log(NW_GATE))

    @BV
    def _(v, gln_done=gln_done):
        v.wait_ge(s_sc.h, gln_done)
        s_v.inc(v.tensor_scalar(out=ext8T[NH:NH + 3, 1:2], in0=ext8T[NH:NH + 3, 1:2],
                                scalar1=LNW, scalar2=float(np.log(1e-5)),
                                op0=OP.subtract, op1=OP.max))
    glnc_done = s_v.n

    @BS
    def _(sc, glnc_done=glnc_done):
        sc.wait_ge(s_v.h, glnc_done)
        sc.activation(out=ext8T[NH:NH + 3, 2:3], in_=ext8T[NH:NH + 3, 1:2], func=AF.Exp)
        s_sc.inc(sc.activation(out=ext8T[NH:NH + 3, 3:4], in_=ext8T[NH:NH + 3, 1:2],
                               func=AF.Exp, scale=-1.0))
    gexp_done = s_sc.n

    d_g6 = nc.dram_tensor("sp_g6", [3, 2], F32).ap()

    @BG
    def _(g, gexp_done=gexp_done):
        g.wait_ge(s_sc.h, gexp_done)
        dma(g, s_st, d_g6, ext8T[NH:NH + 3, 2:4])
        g.wait_ge(s_st.h, s_st.n)
        dma(g, s_st, grow.ap()[0:1, 0:6].rearrange("one (k m) -> one k m", k=2),
            d_g6.rearrange("m k -> k m")[None, :, :], allow_slow_non_contiguous=True)
        g.wait_ge(s_st.h, s_st.n)
        s_gp.inc(g.partition_broadcast(out_ap=gsc[:, 0:6],
                                       in_ap=grow[0:1, 0:6]))
    gsc_done = s_gp.n

    stsl = [sb2(f"stsl{i}", [128, DSSM], F32) for i in range(2)]
    s_sl = [S("sl0"), S("sl1")]
    stsl_free = [0, 0]

    @BV
    def _(v):
        s_v.inc(v.memset(init_sb[:], 0.0))
    init_z = s_v.n

    for j in range(NCORE):
        jp = j % 2

        @block.sync
        def _(sy, j=j, jp=jp, cc_done=cc_done, sf=stsl_free[jp]):
            sy.wait_ge(s_cc.h, cc_done)
            sy.wait_ge(s_v.h, sf)
            dma(sy, s_sl[jp], stsl[jp][:], t_ccout.ap()[129 * j:129 * j + 128, :])
        sld = s_sl[jp].n

        @BV
        def _(v, j=j, jp=jp, sld=sld, c8rep_done=c8rep_done, init_z=init_z):
            v.wait_ge(s_sl[jp].h, sld)
            v.wait_ge(s_gp.h, c8rep_done)
            v.wait_ge(s_v.h, init_z)
            v.tensor_tensor(
                out=stsl[jp].ap().rearrange("p (h q) -> p h q", h=NH),
                in0=stsl[jp].ap().rearrange("p (h q) -> p h q", h=NH),
                in1=bc64(c8rep[:, j * NH:(j + 1) * NH], NH), op=OP.mult)
            s_v.inc(v.tensor_tensor(out=init_sb[:], in0=init_sb[:],
                                    in1=stsl[jp][:], op=OP.add))
        stsl_free[jp] = s_v.n
    init_done = s_v.n

    for cm in reversed(p2_ctx):
        cm.__exit__(None, None, None)

    if debug:
        d_chx = dbg("dbg_chain", [NCH, 128, DSSM])
        d_init = dbg("dbg_init", [128, DSSM])
        d_c8d = dbg("dbg_c8", [NH, 16])

        @BG
        def _(g, init_done=init_done, final_par=final_par):
            g.wait_ge(s_v.h, init_done)
            dma(g, s_st, d_chx, d_chain)
            dma(g, s_st, d_init, init_sb[:])
            dma(g, s_st, d_c8d, c8[0:NH, :])

    p3_ctx = []

    def sb3(name, shape, dtype):
        cm = nc.sbuf_tensor(name, list(shape), dtype)
        t = cm.__enter__()
        p3_ctx.append(cm)
        return t

    def ps3(name, shape, dtype=F32):
        cm = nc.psum_tensor(name, list(shape), dtype)
        t = cm.__enter__()
        p3_ctx.append(cm)
        return t

    xh3 = [sb3(f"xh3_{i}", [128, DSSM], F32) for i in range(2)]
    zch = [sb3("zch0", [128, DSSM], F32)]
    stcor = sb3("stcor", [128, DSSM], F32)
    Ebuf = [sb3(f"Ebuf{i}", [128, 16 * 128], F32) for i in range(2)]
    bbrow = [sb3(f"bbrow{i}", [1, 16 * 128], F32) for i in range(2)]
    csrow = [sb3(f"csrow{i}", [1, 16 * 128], F32) for i in range(2)]
    ones2k = sb3("ones2k", [1, 128], F32)
    ychunk = sb3("ychunk", [128, DSSM], F32)
    ynspill = sb3("ynspill", [128, 2 * 32 * 128], BF16)
    rcol = sb3("rcol", [128, 12], F32)
    CB_sb = sb3("CB_sb", [128, 128], F32)
    pcb = ps3("p3cb", [128, 512], F32)
    prb = ps3("p3rb", [128, 512], F32)
    py4 = [ps3(f"p3y{i}", [128, 512], F32) for i in range(4)]
    ptp3 = [ps3(f"p3tp{i}", [128, 512], F32) for i in range(2)]

    @BV
    def _(v):
        v.wait_ge(s_ch.h, s_ch.n)
        v.wait_ge(s_st.h, s_st.n)
        v.wait_ge(s_gp.h, s_gp.n)
        s_v.inc(v.memset(ones2k[:, 0:128], 1.0))
    art_ones = s_v.n

    s_x3 = S("x3")
    s_z3 = S("z3")
    s_c3 = S("c3")
    s_r3 = S("r3")
    s_yn = S("yn")

    fence3 = {s: s.n for s in (s_v, s_sc, s_mm, s_st, s_gp, s_sp, s_ch, s_ld)}

    d_csr = nc.dram_tensor("sp_csr", [NH, T], F32).ap()
    d_bbr = nc.dram_tensor("sp_bbr", [NH, T], F32).ap()

    @BG
    def _(g):
        for s_, n_ in fence3.items():
            g.wait_ge(s_.h, n_)
        dma(g, s_st, d_csr, cs_ext[0:NH, 1:T + 1])
        dma(g, s_st, d_bbr, bb64[0:NH, :])
        g.wait_ge(s_st.h, s_st.n)
        s_gp.inc(g.nop())
    nwtm_done = s_gp.n

    P3 = dict(
        xh3_free=[init_done, init_done], zch_free=init_done, yn_free=0,
        art_free=[0, 0],
        cbrep_free=init_done,
        tp3_free=[0, 0],
        ebuf_free=[0, 0], cb_free=init_done,
        pcb_free=(s_v, init_done), prb_free=(s_v, init_done),
        pool_free=[init_done, init_done],
        ya_gate=init_done,
        stcor_free=init_done,
        x3ld={}, z3ld={}, c3ld={}, intra={}, ydone={},
    )

    def p3_partA(c):
        par = c % 2

        @block.sync
        def _(sy, c=c, par=par, xf=P3["xh3_free"][par]):
            sy.wait_ge(s_v.h, xf)
            dma(sy, s_x3, xh3[par].ap().rearrange("p (ob c2) -> p ob c2", ob=32),
                d_xtm[:, c].rearrange("ob p c2 -> p ob c2"))
        P3["x3ld"][c] = s_x3.n

        @block.tensor
        def _(t, c=c, cbf=P3["cb_free"], pcbf=P3["pcb_free"]):
            t.wait_ge(s_v.h, cbf)
            t.wait_ge(pcbf[0].h, pcbf[1])
            s_mm.inc(t.matmul(pcb[:, 0:128], B_sb[:, c * 128:(c + 1) * 128],
                              C_sb[:, c * 128:(c + 1) * 128], start=True, stop=True))
        cbmm = s_mm.n

        @BV
        def _(v, cbmm=cbmm, crf=P3["cbrep_free"]):
            v.wait_ge(s_mm.h, cbmm)
            v.wait_ge(s_v.h, crf)
            s_v.inc(v.tensor_tensor(out=CB_sb[:], in0=pcb[:, 0:128], in1=mask01[:],
                                    op=OP.mult))
        cb_done = s_v.n
        P3["cb_free"] = cb_done
        P3["pcb_free"] = (s_v, cb_done)

        pend_intra = None
        pend_evac = None
        for grp in range(4):
            epar = grp % 2
            apar = grp % 2
            h0 = grp * 16

            @block.sync
            def _(sy, c=c, grp=grp, artf=P3["art_free"][apar]):
                sy.wait_ge(s_mm.h, artf)
                sy.wait_ge(s_gp.h, nwtm_done)
                dma(sy, s_r3,
                    bbrow[apar].ap().rearrange("one (h s) -> one h s", h=16),
                    d_bbr[grp * 16:(grp + 1) * 16,
                          c * 128:(c + 1) * 128][None, :, :])
                dma(sy, s_r3,
                    csrow[apar].ap().rearrange("one (h s) -> one h s", h=16),
                    d_csr[grp * 16:(grp + 1) * 16,
                          c * 128:(c + 1) * 128][None, :, :])
            artld = s_r3.n

            last_amm = None
            for piece in range(4):
                bank, bkey = ((prb, "prb_free") if piece % 2 == 0
                              else (pcb, "pcb_free"))
                bf = P3[bkey]

                @block.tensor
                def _(t, apar=apar, piece=piece, bank=bank, bf=bf, artld=artld):
                    t.wait_ge(s_r3.h, artld)
                    t.wait_ge(s_v.h, art_ones)
                    t.wait_ge(bf[0].h, bf[1])
                    last = None
                    for hh4 in range(4):
                        hh = piece * 4 + hh4
                        reg = bank[:, hh4 * 128:(hh4 + 1) * 128]
                        t.matmul(reg, bbrow[apar][:, hh * 128:(hh + 1) * 128],
                                 ones2k[:, 0:128],
                                 start=(hh4 == 0), stop=False)
                        last = t.matmul(reg, ones2k[:, 0:128],
                                        csrow[apar][:, hh * 128:(hh + 1) * 128],
                                        start=False, stop=(hh4 == 3))
                    s_mm.inc(last)
                amm = s_mm.n
                last_amm = amm

                @BS
                def _(sc, piece=piece, epar=epar, amm=amm, bank=bank,
                      ef=P3["ebuf_free"][epar]):
                    sc.wait_ge(s_mm.h, amm)
                    if piece == 0:
                        sc.wait_ge(s_mm.h, ef)
                    dst = Ebuf[epar][:, piece * 512:(piece + 1) * 512]
                    sc.activation(out=dst, in_=bank[:], func=AF.Relu,
                                  scale=-1.0, bias=ccol(30.0))
                    s_sc.inc(sc.activation(out=dst, in_=dst, func=AF.Exp,
                                           scale=-1.0, bias=ccol(30.0)))
                P3[bkey] = (s_sc, s_sc.n)
            exp_done = s_sc.n
            P3["art_free"][apar] = last_amm

            @BV
            def _(v, epar=epar, exp_done=exp_done, cb_done=cb_done):
                v.wait_ge(s_sc.h, exp_done)
                v.wait_ge(s_v.h, cb_done)
                s_v.inc(v.tensor_tensor(
                    out=Ebuf[epar].ap().rearrange("p (h t2) -> p h t2", h=16),
                    in0=Ebuf[epar].ap().rearrange("p (h t2) -> p h t2", h=16),
                    in1=CB_sb[:, None, :].broadcast_to([128, 16, 128]),
                    op=OP.mult))
            m_done = s_v.n

            def mk_intra(c=c, par=par, h0=h0, epar=epar, m_done=m_done,
                         ppi=grp % 2):
                pp = [py4[2 * ppi], py4[2 * ppi + 1]]

                def emit():
                    @block.tensor
                    def _(t, x3ld=P3["x3ld"][c], pf=P3["pool_free"][ppi]):
                        t.wait_ge(s_x3.h, x3ld)
                        t.wait_ge(s_v.h, max(m_done, pf))
                        last = None
                        for hh in range(16):
                            h = h0 + hh
                            bk = pp[hh // 8]
                            last = t.matmul(bk[:, (hh % 8) * HD:((hh % 8) + 1) * HD],
                                            Ebuf[epar][:, hh * 128:(hh + 1) * 128],
                                            xh3[par][:, h * HD:(h + 1) * HD],
                                            start=(hh % 8 == 0), stop=(hh % 8 == 7))
                        s_mm.inc(last)
                    imm = s_mm.n
                    P3["ebuf_free"][epar] = imm

                    def emit_evac():
                        @BV
                        def _(v, gate=P3["ya_gate"]):
                            v.wait_ge(s_mm.h, imm)
                            v.wait_ge(s_v.h, gate)
                            ins = None
                            for bk_i in range(2):
                                hh0 = h0 + bk_i * 8
                                sl = slice(hh0 * HD, (hh0 + 8) * HD)
                                v.tensor_tensor(
                                    out=ychunk.ap()[:, sl].rearrange("p (h q) -> p h q", h=8),
                                    in0=xh3[par].ap()[:, sl].rearrange("p (h q) -> p h q", h=8),
                                    in1=bc64(drep[:, hh0:hh0 + 8], 8), op=OP.mult)
                                ins = v.tensor_tensor(out=ychunk[:, sl], in0=ychunk[:, sl],
                                                      in1=pp[bk_i][:], op=OP.add)
                            s_v.inc(ins)
                        P3["pool_free"][ppi] = s_v.n
                    return emit_evac
                return emit

            this_intra = mk_intra()
            if pend_intra is not None:
                new_evac = pend_intra()
                if pend_evac is not None:
                    pend_evac()
                pend_evac = new_evac
            pend_intra = this_intra
        P3["cbrep_free"] = m_done
        new_evac = pend_intra()
        if pend_evac is not None:
            pend_evac()
        new_evac()
        P3["intra"][c] = s_v.n

    def p3_partB(c, init_done):
        par = c % 2

        @block.sync
        def _(sy, c=c, scf=P3["stcor_free"]):
            sy.wait_ge(s_v.h, scf)
            dma(sy, s_c3, stcor[:], d_chain[c])
        c3ld = s_c3.n

        @BV
        def _(v, c=c, c3ld=c3ld, init_done=init_done, zf=P3["zch_free"]):
            v.wait_ge(s_c3.h, c3ld)
            v.wait_ge(s_v.h, max(init_done, zf))
            v.tensor_tensor(
                out=zch[0].ap().rearrange("p (h q) -> p h q", h=NH),
                in0=init_sb.ap().rearrange("p (h q) -> p h q", h=NH),
                in1=bc64(rels_rep[:, c * NH:(c + 1) * NH], NH), op=OP.mult)
            s_v.inc(v.tensor_tensor(out=stcor[:], in0=stcor[:], in1=zch[0][:],
                                    op=OP.add))
        stcor_done = s_v.n

        @block.sync
        def _(sy, c=c, scd=stcor_done):
            sy.wait_ge(s_v.h, scd)
            dma(sy, s_z3, zch[0].ap().rearrange("p (ob c2) -> p ob c2", ob=32),
                d_ztm[:, c].rearrange("ob p c2 -> p ob c2"))
        z3ld = s_z3.n

        intra_done = P3["intra"][c]
        for grp in range(4):
            h0 = grp * 16
            pp = [py4[2 * (grp % 2)], py4[2 * (grp % 2) + 1]]

            @block.tensor
            def _(t, c=c, h0=h0, stcor_done=stcor_done, pp=pp,
                  pf=P3["pool_free"][grp % 2]):
                t.wait_ge(s_v.h, max(stcor_done, pf))
                last = None
                for hh in range(16):
                    h = h0 + hh
                    bk = pp[hh // 8]
                    last = t.matmul(bk[:, (hh % 8) * HD:((hh % 8) + 1) * HD],
                                    C_sb[:, c * 128:(c + 1) * 128],
                                    stcor[:, h * HD:(h + 1) * HD],
                                    start=(hh % 8 == 0), stop=(hh % 8 == 7))
                s_mm.inc(last)
            bmm = s_mm.n

            @BV
            def _(v, c=c, h0=h0, bmm=bmm, intra_done=intra_done, pp=pp):
                v.wait_ge(s_mm.h, bmm)
                v.wait_ge(s_v.h, intra_done)
                ins = None
                for bk_i in range(2):
                    hh0 = h0 + bk_i * 8
                    sl = slice(hh0 * HD, (hh0 + 8) * HD)
                    v.tensor_tensor(
                        out=pp[bk_i].ap().rearrange("p (h q) -> p h q", h=8),
                        in0=pp[bk_i].ap().rearrange("p (h q) -> p h q", h=8),
                        in1=bc64(exprelT[:, c * NH + hh0:c * NH + hh0 + 8], 8),
                        op=OP.mult)
                    ins = v.tensor_tensor(out=ychunk[:, sl], in0=ychunk[:, sl],
                                          in1=pp[bk_i][:], op=OP.add)
                s_v.inc(ins)
            P3["pool_free"][grp % 2] = s_v.n
        P3["stcor_free"] = s_v.n
        y_done = s_v.n

        @BV
        def _(v, c=c, y_done=y_done, z3ld=z3ld):
            v.wait_ge(s_z3.h, z3ld)
            s_v.inc(v.tensor_tensor(out=ychunk[:], in0=ychunk[:], in1=zch[0][:],
                                    op=OP.mult))
        yg_done = s_v.n
        P3["zch_free"] = yg_done

        @BS
        def _(sc, c=c, yg_done=yg_done):
            sc.wait_ge(s_v.h, yg_done)
            ins = None
            for pp in range(8):
                ins = sc.activation(out=ptp3[0][:, 0:512],
                                    in_=ychunk[:, pp * 512:(pp + 1) * 512],
                                    func=AF.Square, accum_out=rcol[:, pp:pp + 1])
            s_sc.inc(ins)
        sq_done = s_sc.n

        @BV
        def _(v, sq_done=sq_done):
            v.wait_ge(s_sc.h, sq_done)
            s_v.inc(v.tensor_reduce(out=rcol[:, 8:9], in_=rcol[:, 0:8], axis=AX,
                                    op=OP.add))
        rsum_done = s_v.n

        @BS
        def _(sc, rsum_done=rsum_done):
            sc.wait_ge(s_v.h, rsum_done)
            sc.activation(out=rcol[:, 9:10], in_=rcol[:, 8:9], func=AF.Ln,
                          bias=ccol(EPS), scale=1.0 / DSSM)
            s_sc.inc(sc.activation(out=rcol[:, 2:3], in_=rcol[:, 9:10], func=AF.Exp,
                                   scale=-0.5))
        rstd_done = s_sc.n

        @BV
        def _(v, c=c, par=par, rstd_done=rstd_done):
            v.wait_ge(s_sc.h, rstd_done)
            s_v.inc(v.tensor_scalar(out=xh3[par][:], in0=ychunk[:],
                                    scalar1=rcol[:, 2:3], scalar2=None, op0=OP.mult))
        yn_done = s_v.n
        P3["ya_gate"] = yn_done

        def p3_tp(c=c, par=par, yn_done=yn_done):
            for ib in range(32):
                tb3 = ptp3[ib % 2]
                at_v = P3["tp3_free"][ib % 2]

                @block.tensor
                def _(t, c=c, par=par, ib=ib, tb3=tb3, yn_done=yn_done, at_v=at_v):
                    t.wait_ge(s_v.h, max(yn_done, at_v))
                    s_mm.inc(t.transpose(tb3[:, 0:128],
                                         xh3[par][:, ib * 128:(ib + 1) * 128],
                                         ident[:]))
                tpd = s_mm.n

                @BS
                def _(sc, ib=ib, tb3=tb3, tpd=tpd, ynf=P3["yn_free"]):
                    sc.wait_ge(s_mm.h, tpd)
                    if ib == 0:
                        sc.wait_ge(s_yn.h, ynf)
                    s_sc.inc(sc.activation(out=ynspill[:, ib * 128:(ib + 1) * 128],
                                           in_=tb3[:, 0:128], func=AF.Copy,
                                           scale=nw_sb[:, ib:ib + 1]))
                hid = s_sc.n

                @BV
                def _(v, ib=ib, tb3=tb3, hid=hid):
                    v.wait_ge(s_sc.h, hid)
                    s_v.inc(v.scalar_tensor_tensor(
                        out=ynspill[:, 4096 + ib * 128:4096 + (ib + 1) * 128],
                        in0=tb3[:, 0:128], scalar=nw_sb[:, ib:ib + 1],
                        in1=ynspill[:, ib * 128:(ib + 1) * 128],
                        op0=OP.mult, op1=OP.subtract))
                P3["tp3_free"][ib % 2] = s_v.n
            sp_ready = s_v.n

            @BG
            def _(g, c=c, sp_ready=sp_ready):
                g.wait_ge(s_v.h, sp_ready)
                dma(g, s_yn, d_ynsp[c].rearrange("p two ib c2 -> p (two ib c2)"),
                    ynspill[:])
            P3["yn_free"] = s_yn.n
            P3["xh3_free"][par] = sp_ready
        return p3_tp

    @block.sync
    def _(sy):
        for s_, n_ in fence3.items():
            sy.wait_ge(s_.h, n_)
        sy.wait_ge(s_gp.h, s_gp.n)

    tp_thunk = None
    for c in range(NCH):
        p3_partA(c)
        if tp_thunk is not None:
            tp_thunk()
        tp_thunk = p3_partB(c, init_done)
    tp_thunk()

    for cm in reversed(p3_ctx):
        cm.__exit__(None, None, None)
    for cm in reversed(scan_cms):
        cm.__exit__(None, None, None)

    if debug:
        d_ynd = dbg("dbg_yn", [NCH, 128, 2, 32, 128], BF16)

        @BG
        def _(g):
            g.wait_ge(s_yn.h, s_yn.n)
            dma(g, s_st, d_ynd, d_ynsp)

    p4_ctx = []

    def sb4(name, shape, dtype):
        cm = nc.sbuf_tensor(name, list(shape), dtype)
        t = cm.__enter__()
        p4_ctx.append(cm)
        return t

    def ps4(name, shape, dtype=F32):
        cm = nc.psum_tensor(name, list(shape), dtype)
        t = cm.__enter__()
        p4_ctx.append(cm)
        return t

    ynres = sb4("ynres", [128, 2, 32, T], BF16)
    wraw4 = [sb4(f"wraw4_{i}", [128, 32 * 128], F32) for i in range(2)]
    wh4 = [sb4(f"wh4_{i}", [128, 32 * 128], BF16) for i in range(2)]
    wl4 = [sb4(f"wl4_{i}", [128, 32 * 128], BF16) for i in range(2)]
    sbuf4 = [sb4(f"sbuf4_{i}", [128, T], F32) for i in range(2)]
    pk4 = [ps4(f"p4b{i}", [128, 512], F32) for i in range(4)]

    s_w4 = S("w4")
    s_s4 = S("s4")

    fence4 = {s: s.n for s in (s_v, s_sc, s_mm, s_st, s_gp, s_sp, s_ch, s_yn, s_ld)}

    s_y4 = [S("y4a"), S("y4b")]

    @block.sync
    def _(sy):
        for s_, n_ in fence4.items():
            sy.wait_ge(s_.h, n_)
        for c in range(NCH):
            dma(sy, s_y4[c % 2], ynres[:, :, :, c * 128:(c + 1) * 128], d_ynsp[c])

    yn_ld4 = (s_y4[0].n, s_y4[1].n)

    wraw4_free = [0, 0]
    whl4_free = [0, 0]
    pset4_free = [0, 0]
    sbuf4_free = [0, 0]
    split4 = {}

    def p4_load_split(ob):
        par = ob % 2

        @block.sync
        def _(sy, ob=ob, par=par, wfree=wraw4_free[par]):
            sy.wait_ge(s_v.h, wfree)
            dma(sy, s_w4, wraw4[par][:], d_wout[ob].rearrange("p ib o -> p (ib o)"))
        wld = s_w4.n

        @BS
        def _(sc, par=par, wld=wld, wf=whl4_free[par]):
            sc.wait_ge(s_w4.h, wld)
            sc.wait_ge(s_mm.h, wf)
            s_sc.inc(sc.copy(out=wh4[par][:], in_=wraw4[par][:]))
        whd = s_sc.n

        @BV
        def _(v, par=par, wld=wld, whd=whd, wf=whl4_free[par]):
            v.wait_ge(s_w4.h, wld)
            v.wait_ge(s_sc.h, whd)
            v.wait_ge(s_mm.h, wf)
            s_v.inc(v.tensor_tensor(out=wl4[par][:], in0=wraw4[par][:], in1=wh4[par][:],
                                    op=OP.subtract))
        wraw4_free[par] = s_v.n
        split4[ob] = (whd, s_v.n)

    def p4_mm_evac(ob):
        par = ob % 2
        need_ps = pset4_free[par]
        whd, wsplit = split4[ob]

        @block.tensor
        def _(t, ob=ob, par=par, whd=whd, wsplit=wsplit, need_ps=need_ps, yn_ld4=yn_ld4):
            t.wait_ge(s_y4[0].h, yn_ld4[0])
            t.wait_ge(s_y4[1].h, yn_ld4[1])
            t.wait_ge(s_sc.h, whd)
            t.wait_ge(s_v.h, max(wsplit, need_ps))
            p0_, p1_ = pk4[2 * par], pk4[2 * par + 1]
            last = None
            for ib in range(32):
                wha = wh4[par][:, ib * 128:(ib + 1) * 128]
                wla = wl4[par][:, ib * 128:(ib + 1) * 128]
                hi0 = ynres[:, 0, ib, 0:512]
                hi1 = ynres[:, 0, ib, 512:1024]
                lo0 = ynres[:, 1, ib, 0:512]
                lo1 = ynres[:, 1, ib, 512:1024]
                t.matmul(p0_[:], wha, hi0, start=(ib == 0), stop=False)
                t.matmul(p1_[:], wha, hi1, start=(ib == 0), stop=False)
                t.matmul(p0_[:], wha, lo0, start=False, stop=False)
                t.matmul(p1_[:], wha, lo1, start=False, stop=False)
                t.matmul(p0_[:], wla, hi0, start=False, stop=(ib == 31))
                last = t.matmul(p1_[:], wla, hi1, start=False, stop=(ib == 31))
            s_mm.inc(last)
        mm4 = s_mm.n
        whl4_free[par] = s_mm.n

        @BV
        def _(v, ob=ob, par=par, mm4=mm4, sfree=sbuf4_free[par]):
            v.wait_ge(s_mm.h, mm4)
            v.wait_ge(s_s4.h, sfree)
            v.tensor_copy(out=sbuf4[par][:, 0:512], in_=pk4[2 * par][:])
            s_v.inc(v.tensor_copy(out=sbuf4[par][:, 512:1024], in_=pk4[2 * par + 1][:]))
        ev4 = s_v.n
        pset4_free[par] = ev4

        @BG
        def _(g, ob=ob, par=par, ev4=ev4):
            g.wait_ge(s_v.h, ev4)
            dma(g, s_s4, d_ssp[ob], sbuf4[par][:])
        sbuf4_free[par] = s_s4.n

    for ob in range(16):
        p4_load_split(ob)
        if ob >= 1:
            p4_mm_evac(ob - 1)
    p4_mm_evac(15)

    for cm in reversed(p4_ctx):
        cm.__exit__(None, None, None)

    if debug:
        d_sd = dbg("dbg_s", [16, 128, T])

        @BG
        def _(g):
            g.wait_ge(s_s4.h, s_s4.n)
            dma(g, s_st, d_sd, d_ssp)

    p5o_ctx = []

    def sb5o(name, shape, dtype):
        cm = nc.sbuf_tensor(name, list(shape), dtype)
        t = cm.__enter__()
        p5o_ctx.append(cm)
        return t

    s2_b = sb5o("s2_b", [128, T], F32)
    o3_b = sb5o("o3_b", [128, T], F32)
    gneg = sb5o("gneg", [128, 4], F32)
    d_row = nc.dram_tensor("sp_row", [128, 8], F32).ap()

    @BV
    def _(v, gsc_done=gsc_done):
        v.wait_ge(s_gp.h, gsc_done)
        s_v.inc(v.tensor_scalar(out=gneg[:, 0:3], in0=gsc[:, 3:6], scalar1=-1.0,
                                scalar2=None, op0=OP.mult))
    gneg_done = s_v.n

    p5a_ctx = []

    def sb5a(name, shape, dtype):
        cm = nc.sbuf_tensor(name, list(shape), dtype)
        t = cm.__enter__()
        p5a_ctx.append(cm)
        return t

    p5a_ps_ctx = []

    def ps5a(name, shape, dtype=F32):
        cm = nc.psum_tensor(name, list(shape), dtype)
        t = cm.__enter__()
        p5a_ps_ctx.append(cm)
        return t

    gxg_b = sb5a("gxg_b", [128, T], F32)
    gxv_b = sb5a("gxv_b", [128, T], F32)
    rowbuf = sb5a("rowbuf", [1, T], F32)
    gx2c = sb5a("gx2c", [128, 8], F32)
    qxn5 = sb5a("qxn5", [128, NIB_D, T], BF16)
    wgr = [sb5a(f"wgr{i}", [128, 2048], F32) for i in range(2)]
    wvr = [sb5a(f"wvr{i}", [128, 2048], F32) for i in range(2)]
    tg = [sb5a(f"tg{i}", [128, 2048], BF16) for i in range(2)]
    tv = [sb5a(f"tv{i}", [128, 2048], BF16) for i in range(2)]
    tscr5 = sb5a("tscr5", [128, 2048], F32)
    pregate = sb5a("pregate", [128, T], F32)
    gatebuf = sb5a("gatebuf", [128, T], F32)
    vbuf = [sb5a(f"vbuf{i}", [128, T], F32) for i in range(2)]
    rmax = sb5a("rmax", [128, T], F32)
    rmin = sb5a("rmin", [128, T], F32)
    pg5 = [ps5a(f"p5g{i}", [128, 512], F32) for i in range(4)]
    pv5 = [ps5a(f"p5v{i}", [128, 512], F32) for i in range(4)]

    s_wg = S("wg5")
    s_wv = S("wv5")
    s_q5 = S("q5")
    s_v5 = S("v5")

    fence5 = {s: s.n for s in (s_v, s_sc, s_mm, s_st, s_gp, s_sp, s_s4)}

    @block.sync
    def _(sy):
        for s_, n_ in fence5.items():
            sy.wait_ge(s_.h, n_)
        dma(sy, s_q5, qxn5[:], d_qxn)
    q5ld = s_q5.n

    @BG
    def _(g, q5ld=q5ld):
        g.wait_ge(s_q5.h, q5ld)
        dma(g, s_st, d_row, gx1_cols[:, 0:NT], allow_slow_non_contiguous=True)
        g.wait_ge(s_st.h, s_st.n)
        dma(g, s_st, rowbuf.ap().rearrange("one (tt p) -> one tt p", tt=NT),
            d_row.rearrange("p tt -> tt p")[None, :, :], allow_slow_non_contiguous=True)
        g.wait_ge(s_st.h, s_st.n)
        s_gp.inc(g.nop())
    gx1row_done = s_gp.n

    @BV
    def _(v, gx1row_done=gx1row_done):
        v.wait_ge(s_gp.h, gx1row_done)
        v.tensor_scalar(out=gxg_b[0:1, :], in0=rowbuf[0:1, :],
                        scalar1=gsc[0:1, 0:1], scalar2=1.0 / 127.0,
                        op0=OP.mult, op1=OP.mult)
        s_v.inc(v.tensor_scalar(out=gxv_b[0:1, :], in0=rowbuf[0:1, :],
                                scalar1=gsc[0:1, 1:2], scalar2=1.0 / 127.0,
                                op0=OP.mult, op1=OP.mult))
    rows_done = s_v.n

    @BG
    def _(g, rows_done=rows_done):
        g.wait_ge(s_v.h, rows_done)
        g.partition_broadcast(out_ap=gxg_b[:], in_ap=gxg_b[0:1, :])
        s_gp.inc(g.partition_broadcast(out_ap=gxv_b[:], in_ap=gxv_b[0:1, :]))
    gx_b_done = s_gp.n

    @BV
    def _(v):
        v.memset(rmax[:], -1e30)
        s_v.inc(v.memset(rmin[:], 1e30))
    rinit = s_v.n

    wg_free = [0, 0]
    tern_free = [0, 0]
    ps5_free = [0, 0]
    vb_free = [0, 0]
    tern5 = {}

    def p5a_load_tern(hb):
        par = hb % 2

        @block.sync
        def _(sy, hb=hb, par=par, wf=wg_free[par]):
            sy.wait_ge(s_v.h, wf)
            dma(sy, s_wg, wgr[par][:], d_wg[hb].rearrange("p ib o -> p (ib o)"))
            dma(sy, s_wv, wvr[par][:], d_wv[hb].rearrange("p ib o -> p (ib o)"))
        wgld, wvld = s_wg.n, s_wv.n

        @BV
        def _(v, hb=hb, par=par, wgld=wgld, wvld=wvld, tf=tern_free[par],
              gneg_done=gneg_done):
            v.wait_ge(s_wg.h, wgld)
            v.wait_ge(s_wv.h, wvld)
            v.wait_ge(s_mm.h, tf)
            v.wait_ge(s_v.h, gneg_done)
            v.tensor_scalar(out=tscr5[:], in0=wgr[par][:], scalar1=gneg[:, 0:1],
                            scalar2=0.5, op0=OP.mult, op1=OP.is_ge)
            v.tensor_scalar(out=wgr[par][:], in0=wgr[par][:], scalar1=gsc[:, 3:4],
                            scalar2=0.5, op0=OP.mult, op1=OP.is_ge)
            v.tensor_tensor(out=tg[par][:], in0=tscr5[:], in1=wgr[par][:], op=OP.subtract)
            v.tensor_scalar(out=tscr5[:], in0=wvr[par][:], scalar1=gneg[:, 1:2],
                            scalar2=0.5, op0=OP.mult, op1=OP.is_ge)
            v.tensor_scalar(out=wvr[par][:], in0=wvr[par][:], scalar1=gsc[:, 4:5],
                            scalar2=0.5, op0=OP.mult, op1=OP.is_ge)
            s_v.inc(v.tensor_tensor(out=tv[par][:], in0=tscr5[:], in1=wvr[par][:],
                                    op=OP.subtract))
        wg_free[par] = s_v.n
        tern5[hb] = s_v.n

    def p5a_mm_evac(hb):
        par = hb % 2
        tern_done = tern5[hb]
        pgA, pgB = pg5[2 * par], pg5[2 * par + 1]
        pvA, pvB = pv5[2 * par], pv5[2 * par + 1]
        need_ps5 = ps5_free[par]

        @block.tensor
        def _(t, hb=hb, par=par, tern_done=tern_done, q5ld=q5ld, need_ps5=need_ps5,
              pgA=pgA, pgB=pgB, pvA=pvA, pvB=pvB):
            t.wait_ge(s_q5.h, q5ld)
            t.wait_ge(s_v.h, max(tern_done, need_ps5))
            last = None
            for ib in range(NIB_D):
                tga = tg[par][:, ib * 128:(ib + 1) * 128]
                tva = tv[par][:, ib * 128:(ib + 1) * 128]
                q0 = qxn5[:, ib, 0:512]
                q1 = qxn5[:, ib, 512:1024]
                t.matmul(pgA[:], tga, q0, start=(ib == 0), stop=(ib == 15))
                t.matmul(pgB[:], tga, q1, start=(ib == 0), stop=(ib == 15))
                t.matmul(pvA[:], tva, q0, start=(ib == 0), stop=(ib == 15))
                last = t.matmul(pvB[:], tva, q1, start=(ib == 0), stop=(ib == 15))
            s_mm.inc(last)
        mm5 = s_mm.n
        tern_free[par] = mm5

        @BV
        def _(v, hb=hb, mm5=mm5, gx_b_done=gx_b_done, pgA=pgA, pgB=pgB):
            v.wait_ge(s_mm.h, mm5)
            v.wait_ge(s_gp.h, gx_b_done)
            v.tensor_tensor(out=pregate[:, 0:512], in0=pgA[:], in1=gxg_b[:, 0:512],
                            op=OP.mult)
            s_v.inc(v.tensor_tensor(out=pregate[:, 512:1024], in0=pgB[:],
                                    in1=gxg_b[:, 512:1024], op=OP.mult))
        pre_done = s_v.n

        @BS
        def _(sc, hb=hb, pre_done=pre_done):
            sc.wait_ge(s_v.h, pre_done)
            s_sc.inc(sc.activation(out=gatebuf[:], in_=pregate[:], func=AF.Silu,
                                   bias=gb_sb[:, hb:hb + 1]))
        gate_done = s_sc.n

        @BV
        def _(v, hb=hb, par=par, gate_done=gate_done, vfree=vb_free[par],
              pvA=pvA, pvB=pvB):
            v.wait_ge(s_sc.h, gate_done)
            v.wait_ge(s_v5.h, vfree)
            v.tensor_tensor(out=pregate[:, 0:512], in0=pvA[:], in1=gxv_b[:, 0:512],
                            op=OP.mult)
            v.tensor_tensor(out=pregate[:, 512:1024], in0=pvB[:],
                            in1=gxv_b[:, 512:1024], op=OP.mult)
            v.scalar_tensor_tensor(out=vbuf[par][:], in0=pregate[:],
                                   scalar=vb_sb[:, hb:hb + 1], in1=gatebuf[:],
                                   op0=OP.add, op1=OP.mult)
            v.tensor_tensor(out=rmax[:], in0=rmax[:], in1=vbuf[par][:], op=OP.max)
            s_v.inc(v.tensor_tensor(out=rmin[:], in0=rmin[:], in1=vbuf[par][:], op=OP.min))
        v_done = s_v.n
        ps5_free[par] = v_done

        @BG
        def _(g, hb=hb, par=par, v_done=v_done):
            g.wait_ge(s_v.h, v_done)
            dma(g, s_v5, d_vsp[hb], vbuf[par][:])
        vb_free[par] = s_v5.n

    for hb in range(64):
        p5a_load_tern(hb)
        if hb >= 1:
            p5a_mm_evac(hb - 1)
    p5a_mm_evac(63)
    va_done = s_v.n

    @BV
    def _(v, va_done=va_done):
        v.wait_ge(s_v.h, va_done)
        v.tensor_scalar(out=rmin[:], in0=rmin[:], scalar1=-1.0, scalar2=None,
                        op0=OP.mult)
        s_v.inc(v.tensor_tensor(out=rmax[:], in0=rmax[:], in1=rmin[:], op=OP.max))
    gmx_done = s_v.n

    for tt in range(NT):
        at_v5 = s_v.n

        @block.tensor
        def _(t, tt=tt, gmx_done=gmx_done, at_v5=at_v5):
            t.wait_ge(s_v.h, max(gmx_done, at_v5))
            s_mm.inc(t.transpose(pg5[0][:, 0:128], rmax[:, tt * 128:(tt + 1) * 128],
                                 ident[:]))
        gt_ = s_mm.n

        @BV
        def _(v, tt=tt, gt_=gt_):
            v.wait_ge(s_mm.h, gt_)
            v.tensor_reduce(out=gx2c[:, tt:tt + 1], in_=pg5[0][:, 0:128], axis=AX,
                            op=OP.max)
            s_v.inc(v.tensor_scalar(out=gx2c[:, tt:tt + 1], in0=gx2c[:, tt:tt + 1],
                                    scalar1=1e-5, scalar2=None, op0=OP.max))
    gx2_done = s_v.n

    @BG
    def _(g, gx2_done=gx2_done):
        g.wait_ge(s_v.h, gx2_done)
        dma(g, s_st, d_row, gx2c[:, 0:8], allow_slow_non_contiguous=True)
        g.wait_ge(s_st.h, s_st.n)
        dma(g, s_st, rowbuf.ap().rearrange("one (tt p) -> one tt p", tt=NT),
            d_row.rearrange("p tt -> tt p")[None, :, :], allow_slow_non_contiguous=True)
        g.wait_ge(s_st.h, s_st.n)
        s_gp.inc(g.nop())
    gx2row_done = s_gp.n

    @BS
    def _(sc, gx2row_done=gx2row_done):
        sc.wait_ge(s_gp.h, gx2row_done)
        s_sc.inc(sc.activation(out=s2_b[0:1, :], in_=rowbuf[0:1, :], func=AF.Ln))
    lngx2 = s_sc.n

    @BS
    def _(sc, lngx2=lngx2):
        sc.wait_ge(s_sc.h, lngx2)
        s_sc.inc(sc.activation(out=s2_b[0:1, :], in_=s2_b[0:1, :], func=AF.Exp,
                               scale=-1.0, bias=ccol(LN127)[0:1, :]))
    s2row = s_sc.n

    @BV
    def _(v, s2row=s2row):
        v.wait_ge(s_sc.h, s2row)
        s_v.inc(v.tensor_scalar(out=o3_b[0:1, :], in0=rowbuf[0:1, :],
                                scalar1=gsc[0:1, 2:3], scalar2=1.0 / 127.0,
                                op0=OP.mult, op1=OP.mult))
    o3row = s_v.n

    @BG
    def _(g, o3row=o3row):
        g.wait_ge(s_v.h, o3row)
        g.partition_broadcast(out_ap=s2_b[:], in_ap=s2_b[0:1, :])
        s_gp.inc(g.partition_broadcast(out_ap=o3_b[:], in_ap=o3_b[0:1, :]))
    s2o3_done = s_gp.n

    for cm in reversed(p5a_ps_ctx):
        cm.__exit__(None, None, None)
    for cm in reversed(p5a_ctx):
        cm.__exit__(None, None, None)

    p5b_ctx = []

    def sb5b(name, shape, dtype):
        cm = nc.sbuf_tensor(name, list(shape), dtype)
        t = cm.__enter__()
        p5b_ctx.append(cm)
        return t

    def ps5b(name, shape, dtype=F32):
        cm = nc.psum_tensor(name, list(shape), dtype)
        t = cm.__enter__()
        p5b_ctx.append(cm)
        return t

    vqres = sb5b("vqres", [128, 64, T], BF16)
    vq_cms = [nc.sbuf_tensor(f"vrel{i}", [128, T], F32) for i in range(4)] + [
              nc.sbuf_tensor("qtmp0", [128, T], F32),
              nc.sbuf_tensor("qtmp1", [128, T], F32),
              nc.sbuf_tensor("qint0", [128, T], I32),
              nc.sbuf_tensor("qint1", [128, T], I32)]
    vrel = [cm.__enter__() for cm in vq_cms[:4]]
    qtmp = [vq_cms[4].__enter__(), vq_cms[5].__enter__()]
    qint = [vq_cms[6].__enter__(), vq_cms[7].__enter__()]
    s_vr = S("vr5")

    fence5b = {s: s.n for s in (s_v, s_sc, s_mm, s_st, s_gp, s_v5)}
    vrel_free = [0, 0, 0, 0]
    q_free = [0, 0]
    pend_corr = None

    for hb in range(64):
        par = hb % 4
        qpar = hb % 2

        @block.sync
        def _(sy, hb=hb, par=par, vfree=vrel_free[par]):
            if hb == 0:
                for s_, n_ in fence5b.items():
                    sy.wait_ge(s_.h, n_)
            sy.wait_ge(s_v.h, vfree)
            dma(sy, s_vr, vrel[par][:], d_vsp[hb])
        vld = s_vr.n

        @BV
        def _(v, hb=hb, par=par, qpar=qpar, vld=vld, s2o3_done=s2o3_done,
              qf=q_free[qpar]):
            v.wait_ge(s_vr.h, vld)
            v.wait_ge(s_gp.h, s2o3_done)
            v.wait_ge(s_v.h, qf)
            s_v.inc(v.tensor_tensor(out=qtmp[qpar][:], in0=vrel[par][:], in1=s2_b[:],
                                    op=OP.mult))
        qm_done = s_v.n

        @BS
        def _(sc, hb=hb, qpar=qpar, qm_done=qm_done):
            sc.wait_ge(s_v.h, qm_done)
            s_sc.inc(sc.activation(out=qint[qpar][:], in_=qtmp[qpar][:],
                                   func=AF.Identity, bias=ccol(0.5), scale=1.0))
        qi_done = s_sc.n

        def mk_corr(hb=hb, par=par, qpar=qpar, qi_done=qi_done):
            def emit():
                @BV
                def _(v):
                    v.wait_ge(s_sc.h, qi_done)
                    v.scalar_tensor_tensor(out=vrel[par][:], in0=qint[qpar][:],
                                           scalar=-0.5, in1=qtmp[qpar][:],
                                           op0=OP.add, op1=OP.is_gt)
                    s_v.inc(v.tensor_tensor(out=vqres[:, hb, :], in0=vrel[par][:],
                                            in1=qint[qpar][:], op=OP.subtract))
                vrel_free[par] = s_v.n
                q_free[qpar] = s_v.n
            return emit

        if pend_corr is not None:
            pend_corr()
        pend_corr = mk_corr()
    pend_corr()
    vq_done = s_v.n

    for cm in reversed(vq_cms):
        cm.__exit__(None, None, None)

    wor = [sb5b(f"wor{i}", [128, 2048], F32) for i in range(2)]
    to3 = [sb5b(f"to3h{i}", [128, 4096], BF16) for i in range(2)]
    tscr6 = sb5b("tscr6", [128, 2048], F32)
    gb5 = [sb5b(f"gb5_{i}", [128, T], F32) for i in range(2)]
    preo = sb5b("preo", [128, T], F32)
    sres = [sb5b("sres0", [128, T], F32)]
    sres = [sres[0], sres[0]]
    ub2 = [sb5b(f"ub2_{i}", [128, NT * 128], F32) for i in range(2)]
    po5 = [ps5b(f"p5o{i}", [128, 512], F32) for i in range(4)]
    pt5 = [ps5b(f"p5t{i}", [128, 512], F32) for i in range(2)]

    s_wo = S("wo5")
    s_sr = S("sr5")
    s_ub = S("ub5")
    s_out = S("out5")

    wor_free = [vq_done, vq_done]
    to3_free = [0, 0]
    po5_free = [vq_done, vq_done]
    sres_free = [vq_done, vq_done]
    ub_free = [vq_done, vq_done]
    tp5_free = [vq_done, vq_done]

    def p5b_main(ob):
        par = ob % 2

        @block.sync
        def _(sy, ob=ob, par=par, sfree=sres_free[par], ubf=ub_free[par]):
            sy.wait_ge(s_v.h, sfree)
            dma(sy, s_sr, sres[par][:], d_ssp[ob])
            if ob < 2:
                sy.wait_ge(s_v.h, ubf)
            else:
                sy.wait_ge(s_out.h, ubf)
            dma(sy, s_ub, ub2[par].ap().rearrange("p (tt o) -> p tt o", tt=NT),
                d_u.rearrange("(tt p) dd -> p tt dd", p=128)[:, :, ob * 128:(ob + 1) * 128])
        srld, ubld = s_sr.n, s_ub.n

        poA, poB = po5[2 * par], po5[2 * par + 1]
        need_po = po5_free[par]

        for half in range(2):
            hpar = half
            tern6_done = None
            for qq in range(2):
                wpar = (half * 2 + qq) % 2

                @block.sync
                def _(sy, ob=ob, half=half, qq=qq, wpar=wpar, wfree=wor_free[wpar]):
                    sy.wait_ge(s_v.h, wfree)
                    dma(sy, s_wo, wor[wpar][:],
                        d_wo[ob].rearrange("p ib o -> p (ib o)")
                        [:, half * 4096 + qq * 2048:half * 4096 + (qq + 1) * 2048])
                wold = s_wo.n

                @BV
                def _(v, qq=qq, hpar=hpar, wpar=wpar, wold=wold,
                      tfree=to3_free[hpar], gneg_done=gneg_done):
                    v.wait_ge(s_wo.h, wold)
                    v.wait_ge(s_v.h, gneg_done)
                    if qq == 0:
                        v.wait_ge(s_mm.h, tfree)
                    sl = slice(qq * 2048, (qq + 1) * 2048)
                    v.tensor_scalar(out=tscr6[:], in0=wor[wpar][:], scalar1=gneg[:, 2:3],
                                    scalar2=0.5, op0=OP.mult, op1=OP.is_ge)
                    v.tensor_scalar(out=wor[wpar][:], in0=wor[wpar][:], scalar1=gsc[:, 5:6],
                                    scalar2=0.5, op0=OP.mult, op1=OP.is_ge)
                    s_v.inc(v.tensor_tensor(out=to3[hpar][:, sl], in0=tscr6[:],
                                            in1=wor[wpar][:], op=OP.subtract))
                wor_free[wpar] = s_v.n
            tern6_done = s_v.n

            @block.tensor
            def _(t, ob=ob, par=par, half=half, hpar=hpar, tern6_done=tern6_done,
                  vq_done=vq_done, need_po=need_po, poA=poA, poB=poB):
                t.wait_ge(s_v.h, max(tern6_done, vq_done, need_po))
                last = None
                for ib2 in range(32):
                    ib = half * 32 + ib2
                    ta = to3[hpar][:, ib2 * 128:(ib2 + 1) * 128]
                    t.matmul(poA[:], ta, vqres[:, ib, 0:512],
                             start=(ib == 0), stop=(ib == 63))
                    last = t.matmul(poB[:], ta, vqres[:, ib, 512:1024],
                                    start=(ib == 0), stop=(ib == 63))
                s_mm.inc(last)
            to3_free[hpar] = s_mm.n
        mmo = s_mm.n

        @BV
        def _(v, ob=ob, mmo=mmo, s2o3_done=s2o3_done, poA=poA, poB=poB):
            v.wait_ge(s_mm.h, mmo)
            v.wait_ge(s_gp.h, s2o3_done)
            v.tensor_tensor(out=preo[:, 0:512], in0=poA[:], in1=o3_b[:, 0:512],
                            op=OP.mult)
            s_v.inc(v.tensor_tensor(out=preo[:, 512:1024], in0=poB[:],
                                    in1=o3_b[:, 512:1024], op=OP.mult))
        preo_done = s_v.n
        po5_free[par] = preo_done

        @BS
        def _(sc, ob=ob, par=par, preo_done=preo_done):
            sc.wait_ge(s_v.h, preo_done)
            s_sc.inc(sc.activation(out=gb5[par][:], in_=preo[:], func=AF.Silu,
                                   bias=obias_sb[:, ob:ob + 1]))
        g_done = s_sc.n

        @BV
        def _(v, ob=ob, par=par, g_done=g_done, srld=srld):
            v.wait_ge(s_sc.h, g_done)
            v.wait_ge(s_sr.h, srld)
            s_v.inc(v.tensor_tensor(out=gb5[par][:], in0=gb5[par][:], in1=sres[par][:],
                                    op=OP.mult))
        r_done = s_v.n
        sres_free[0] = sres_free[1] = r_done

        def p5_tp(ob=ob, par=par, r_done=r_done, ubld=ubld):
            for tt in range(NT):
                tb5 = pt5[tt % 2]
                need_tp5 = tp5_free[tt % 2]

                @block.tensor
                def _(t, ob=ob, tt=tt, tb5=tb5, r_done=r_done, need_tp5=need_tp5):
                    t.wait_ge(s_v.h, max(r_done, need_tp5))
                    s_mm.inc(t.transpose(tb5[:, 0:128],
                                         gb5[par][:, tt * 128:(tt + 1) * 128],
                                         ident[:]))
                tpd5 = s_mm.n

                @BV
                def _(v, ob=ob, par=par, tt=tt, tb5=tb5, tpd5=tpd5, ubld=ubld):
                    v.wait_ge(s_mm.h, tpd5)
                    v.wait_ge(s_ub.h, ubld)
                    s_v.inc(v.tensor_tensor(out=ub2[par][:, tt * 128:(tt + 1) * 128],
                                            in0=tb5[:, 0:128],
                                            in1=ub2[par][:, tt * 128:(tt + 1) * 128],
                                            op=OP.add))
                tp5_free[tt % 2] = s_v.n
            ob_ready = s_v.n

            @BG
            def _(g, ob=ob, par=par, ob_ready=ob_ready):
                g.wait_ge(s_v.h, ob_ready)
                dma(g, s_out,
                    d_out.rearrange("(tt p) dd -> p tt dd", p=128)[:, :, ob * 128:(ob + 1) * 128],
                    ub2[par].ap().rearrange("p (tt o) -> p tt o", tt=NT))
            ub_free[par] = s_out.n
        return p5_tp

    tp5_thunk = None
    for ob in range(16):
        th = p5b_main(ob)
        if tp5_thunk is not None:
            tp5_thunk()
        tp5_thunk = th
    tp5_thunk()

    for cm in reversed(p5b_ctx):
        cm.__exit__(None, None, None)
    for cm in reversed(p5o_ctx):
        cm.__exit__(None, None, None)

    @BG
    def _(g):
        g.wait_ge(s_out.h, s_out.n)
        g.wait_ge(s_st.h, s_st.n)
        g.nop()

    blk_cm.__exit__(None, None, None)
    for cm, _t in reversed(ctx):
        cm.__exit__(None, None, None)
    nc.finalize()
    return nc, ["out"] + dbg_outs


# version 13
# speedup vs baseline: 1.5165x; 1.0006x over previous
import numpy as np
import concourse.bacc as bacc
import concourse.mybir as mybir
from concourse.alu_op_type import AluOpType as OP
from concourse.bass_utils import run_bass_kernel_spmd

F32 = mybir.dt.float32
BF16 = mybir.dt.bfloat16
I32 = mybir.dt.int32
AF = mybir.ActivationFunctionType
AX = mybir.AxisListType.X

NCORE = 8
D = 2048
T = 1024
NT = 8
SEQ = 8192
DIN = 8512
DSSM = 4096
NH = 64
HD = 64
DS = 128
HID = 8192
CH = 128
NCH = 8
EPS = 1e-5
NOB_IN = 67
NIB_D = 16
LN127 = float(np.log(np.float32(127.0)))


class Sem:
    def __init__(self, nc, name):
        self._cm = nc.semaphore(name)
        self.h = self._cm.__enter__()
        self.n = 0

    def inc(self, instr, k=1):
        instr.then_inc(self.h, k)
        self.n += k
        return instr


def bc64(col_ap, nh):
    return col_ap[:, :, None].broadcast_to([128, nh, HD])


def dma(eng, sem, out, in_, **kw):
    eng.wait_ge(sem.h, sem.n)
    return sem.inc(eng.dma_start(out=out, in_=in_, **kw), 16)


def block_w(w):
    O, I = w.shape
    Opad = ((O + 127) // 128) * 128
    wp = np.zeros((Opad, I), np.float32)
    wp[:O] = w
    r = wp.reshape(Opad // 128, 128, I // 128, 128)
    return np.ascontiguousarray(r.transpose(0, 3, 2, 1))


def prep_in_maps(inputs):
    u_full = np.asarray(inputs["L0_attn_u"], np.float32).reshape(SEQ, D)
    w_in_b = block_w(np.asarray(inputs["L0_attn_W_in"], np.float32))
    w_out_b = block_w(np.asarray(inputs["L0_attn_W_out"], np.float32))
    w_g_b = block_w(np.asarray(inputs["L0_ffn_gate_w"], np.float32))
    w_v_b = block_w(np.asarray(inputs["L0_ffn_val_w"], np.float32))
    w_o_b = block_w(np.asarray(inputs["L0_ffn_out_w"], np.float32))
    conv_w = np.asarray(inputs["L0_attn_conv_w"], np.float32).reshape(4352, 4)
    conv_w_p = np.ascontiguousarray(conv_w.reshape(34, 128, 4).transpose(1, 0, 2))
    conv_b_p = np.ascontiguousarray(
        np.asarray(inputs["L0_attn_conv_b"], np.float32).reshape(34, 128).T)
    dtb = np.asarray(inputs["L0_attn_dt_bias"], np.float32).reshape(NH, 1)
    alog = np.asarray(inputs["L0_attn_A_log"], np.float32).reshape(NH, 1)
    dflat = np.asarray(inputs["L0_attn_D"], np.float32).reshape(1, NH)
    nwflat = np.asarray(inputs["L0_attn_norm_w"], np.float32).reshape(1, DSSM)
    norm_w_p = np.ascontiguousarray(
        np.asarray(inputs["L0_attn_norm_w"], np.float32).reshape(32, 128).T)
    gb_p = np.ascontiguousarray(
        np.asarray(inputs["L0_ffn_gate_b"], np.float32).reshape(64, 128).T)
    vb_p = np.ascontiguousarray(
        np.asarray(inputs["L0_ffn_val_b"], np.float32).reshape(64, 128).T)
    ob_p = np.ascontiguousarray(
        np.asarray(inputs["L0_ffn_out_b"], np.float32).reshape(16, 128).T)

    in_maps = []
    for c in range(NCORE):
        t0 = c * T
        halo = np.zeros((128, D), np.float32)
        if c > 0:
            halo[0:3] = u_full[t0 - 3:t0]
        mask = np.zeros((8, 1), np.float32)
        mask[:c] = 1.0
        in_maps.append({
            "u": np.ascontiguousarray(u_full[t0:t0 + T]),
            "u_halo": halo,
            "gam_g": np.ascontiguousarray(w_g_b[8 * c:8 * c + 8]),
            "gam_v": np.ascontiguousarray(w_v_b[8 * c:8 * c + 8]),
            "gam_o": np.ascontiguousarray(w_o_b[2 * c:2 * c + 2]),
            "w_in": w_in_b, "w_out": w_out_b,
            "w_g": w_g_b, "w_v": w_v_b, "w_o": w_o_b,
            "conv_w": conv_w_p, "conv_b": conv_b_p,
            "dtb": dtb, "alog": alog, "dflat": dflat,
            "nwflat": nwflat, "norm_w_p": norm_w_p,
            "gb": gb_p, "vb": vb_p, "obias": ob_p,
            "mask_lt": mask,
        })
    return in_maps


def kernel(**inputs):
    in_maps = prep_in_maps(inputs)
    nc, out_names = build(debug=False)
    res = run_bass_kernel_spmd(nc, in_maps, core_ids=list(range(NCORE)))
    out = np.concatenate([res.results[c]["out"] for c in range(NCORE)], axis=0)
    return out.reshape(1, SEQ, D)


def build(debug=False):
    nc = bacc.Bacc("TRN2", target_bir_lowering=False, debug=False)
    dbg_outs = []

    d_u = nc.dram_tensor("u", [T, D], F32, kind="ExternalInput").ap()
    d_uh = nc.dram_tensor("u_halo", [128, D], F32, kind="ExternalInput").ap()
    d_win = nc.dram_tensor("w_in", [NOB_IN, 128, NIB_D, 128], F32, kind="ExternalInput").ap()
    d_wout = nc.dram_tensor("w_out", [16, 128, 32, 128], F32, kind="ExternalInput").ap()
    d_wg = nc.dram_tensor("w_g", [64, 128, 16, 128], F32, kind="ExternalInput").ap()
    d_wv = nc.dram_tensor("w_v", [64, 128, 16, 128], F32, kind="ExternalInput").ap()
    d_wo = nc.dram_tensor("w_o", [16, 128, 64, 128], F32, kind="ExternalInput").ap()
    d_cw = nc.dram_tensor("conv_w", [128, 34, 4], F32, kind="ExternalInput").ap()
    d_cb = nc.dram_tensor("conv_b", [128, 34], F32, kind="ExternalInput").ap()
    d_dtb = nc.dram_tensor("dtb", [NH, 1], F32, kind="ExternalInput").ap()
    d_alog = nc.dram_tensor("alog", [NH, 1], F32, kind="ExternalInput").ap()
    d_dflat = nc.dram_tensor("dflat", [1, NH], F32, kind="ExternalInput").ap()
    d_nwflat = nc.dram_tensor("nwflat", [1, DSSM], F32, kind="ExternalInput").ap()
    d_nwp = nc.dram_tensor("norm_w_p", [128, 32], F32, kind="ExternalInput").ap()
    d_gb = nc.dram_tensor("gb", [128, 64], F32, kind="ExternalInput").ap()
    d_vb = nc.dram_tensor("vb", [128, 64], F32, kind="ExternalInput").ap()
    d_obias = nc.dram_tensor("obias", [128, 16], F32, kind="ExternalInput").ap()
    d_mask = nc.dram_tensor("mask_lt", [8, 1], F32, kind="ExternalInput").ap()
    d_gg = nc.dram_tensor("gam_g", [8, 128, 16, 128], F32, kind="ExternalInput").ap()
    d_gv = nc.dram_tensor("gam_v", [8, 128, 16, 128], F32, kind="ExternalInput").ap()
    d_go = nc.dram_tensor("gam_o", [2, 128, 64, 128], F32, kind="ExternalInput").ap()
    d_out = nc.dram_tensor("out", [T, D], F32, kind="ExternalOutput").ap()

    d_ztm = nc.dram_tensor("sp_ztm", [32, NT, 128, 128], F32).ap()
    d_xtm = nc.dram_tensor("sp_xtm", [32, NT, 128, 128], F32).ap()
    d_chain = nc.dram_tensor("sp_chain", [NCH, 128, DSSM], F32).ap()
    d_ynsp = nc.dram_tensor("sp_yn", [NCH, 128, 2, 32, 128], BF16).ap()
    d_vsp = nc.dram_tensor("sp_v", [64, 128, T], F32).ap()
    d_ssp = nc.dram_tensor("sp_s", [16, 128, T], F32).ap()
    d_qxn = nc.dram_tensor("sp_qxn", [128, NIB_D, T], BF16).ap()
    d_small = nc.dram_tensor("sp_small", [2 * NCH, NH], F32).ap()
    t_ccin = nc.dram_tensor("cc_in", [129, DSSM], F32)
    t_ccout = nc.dram_tensor("cc_out", [129 * NCORE, DSSM], F32,
                             addr_space="Shared")

    def dbg(name, shape, dtype=F32):
        h = nc.dram_tensor(name, list(shape), dtype, kind="ExternalOutput").ap()
        dbg_outs.append(name)
        return h

    sems = {}

    def S(name):
        if name not in sems:
            sems[name] = Sem(nc, name)
        return sems[name]

    s_ld = S("ld")
    s_st = S("st")
    s_v = S("v")
    s_sc = S("sc")
    s_mm = S("mm")
    s_gp = S("gp")
    s_cc = S("cc")

    ctx = []

    def sb(name, shape, dtype):
        cm = nc.sbuf_tensor(name, list(shape), dtype)
        t = cm.__enter__()
        ctx.append((cm, t))
        return t

    def ps(name, shape, dtype=F32):
        cm = nc.psum_tensor(name, list(shape), dtype)
        t = cm.__enter__()
        ctx.append((cm, t))
        return t

    blk_cm = nc.Block()
    block = blk_cm.__enter__()

    _DRAIN_OPS = {
        "tensor_scalar", "tensor_tensor", "tensor_copy", "tensor_reduce",
        "scalar_tensor_tensor", "tensor_tensor_scan", "reciprocal", "memset",
        "activation", "copy", "add", "mul", "select", "affine_select",
        "partition_broadcast", "iota", "transpose", "reduce_sum", "reduce_max",
        "tensor_add", "tensor_sub", "tensor_mul", "tensor_max",
        "tensor_scalar_add", "tensor_scalar_mul", "bn_stats", "bn_aggr",
        "square", "sign", "sqrt",
    }

    def _ap_span(x):
        t = getattr(x, "tensor", None)
        if t is None or not hasattr(x, "offset"):
            return None
        try:
            lo = int(x.offset)
            span = 1
            for st, ct in list(x.ap):
                span += (int(ct) - 1) * abs(int(st))
            return (t.name, lo, lo + span)
        except Exception:
            return (t.name, 0, 1 << 60)

    _eng_state = {}

    class EW:
        def __init__(self, e):
            object.__setattr__(self, "_e", e)
            key = id(type(e)), getattr(e, "engine", None).__repr__() if hasattr(e, "engine") else type(e).__name__
            st = _eng_state.setdefault(type(e).__name__, {"w": [], "r": []})
            object.__setattr__(self, "_st", st)

        def __getattr__(self, n):
            a = getattr(self._e, n)
            if n not in _DRAIN_OPS:
                return a
            st = self._st

            def wrapped(*args, **kw):
                outs, ins = [], []
                if n == "memset" and args:
                    s = _ap_span(args[0])
                    if s: outs.append(s)
                else:
                    for x in args:
                        s = _ap_span(x)
                        if s: ins.append(s)
                for k2, x in kw.items():
                    s = _ap_span(x)
                    if s is None:
                        continue
                    (outs if ("out" in k2) else ins).append(s)

                def ovl(a_, lst):
                    for b_ in lst:
                        if a_[0] == b_[0] and a_[1] < b_[2] and b_[1] < a_[2]:
                            return True
                    return False

                hazard = any(ovl(i, st["w"]) for i in ins) or \
                    any(ovl(o, st["w"]) or ovl(o, st["r"]) for o in outs)
                if hazard:
                    self._e.drain()
                    st["w"].clear()
                    st["r"].clear()
                r = a(*args, **kw)
                st["w"].extend(outs)
                st["r"].extend(ins)
                return r
            return wrapped

    def BV(f):
        block.vector(lambda e: f(EW(e)))

    def BS(f):
        block.scalar(lambda e: f(EW(e)))

    def BG(f):
        block.gpsimd(lambda e: f(EW(e)))

    ident = sb("ident", [128, 128], F32)
    zscratch = sb("zscratch", [128, 128], F32)
    mask01 = sb("mask01", [128, 128], F32)
    nw_sb = sb("nw_sb", [128, 32], F32)
    identb = sb("identb", [128, 128], BF16)
    maskneg = sb("maskneg", [128, 128], F32)
    cw_sb = sb("cw_sb", [128, 34 * 4], F32)
    cb_sb = sb("cb_sb", [128, 34], F32)
    dtb_sb = sb("dtb_sb", [NH, 1], F32)
    alog_sb = sb("alog_sb", [NH, 1], F32)
    negA = sb("negA", [NH, 1], F32)
    drow = sb("drow", [1, NH], F32)
    drep = sb("drep", [128, NH], F32)
    gb_sb = sb("gb_sb", [128, 64], F32)
    vb_sb = sb("vb_sb", [128, 64], F32)
    obias_sb = sb("obias_sb", [128, 16], F32)
    mask_sb = sb("mask_sb", [8, 1], F32)
    constcol = sb("constcol", [128, 8], F32)
    _constvals = {}

    def ccol(val):
        if val not in _constvals:
            idx = len(_constvals)
            assert idx < 8
            _constvals[val] = idx
        i = _constvals[val]
        return constcol[:, i:i + 1]

    for _v_ in (EPS, LN127, 1.0, 0.5, 30.0):
        ccol(_v_)

    @BV
    def _(v):
        items = list(_constvals.items())
        for val, idx in items[:-1]:
            v.memset(constcol[:, idx:idx + 1], float(val))
        val, idx = items[-1]
        s_v.inc(v.memset(constcol[:, idx:idx + 1], float(val)))
    consts_done = s_v.n

    gam_part = sb("gam_part", [128, 4], F32)
    gx1_cols = sb("gx1_cols", [128, NT], F32)
    gsc = sb("gsc", [128, 8], F32)
    scan_cms = []

    def sbsc(name, shape, dtype):
        cm = nc.sbuf_tensor(name, list(shape), dtype)
        t = cm.__enter__()
        scan_cms.append(cm)
        return t

    B_sb = sbsc("B_sb", [128, T], F32)
    C_sb = sbsc("C_sb", [128, T], F32)
    cs_ext = sbsc("cs_ext", [NH, T + 8], F32)
    logdt = sbsc("logdt", [NH, T], F32)
    bb64 = sbsc("bb64", [NH, T], F32)
    ones64 = sbsc("ones64", [NH, T], F32)
    dt_s = sbsc("dt_s", [NH, T], F32)
    bbT = sbsc("bbT", [128, NCH * NH], F32)
    sclT = sbsc("sclT", [128, NCH * NH], F32)
    exprelT = sbsc("exprelT", [128, NCH * NH], F32)
    aend_rep = sbsc("aend_rep", [128, NCH * NH], F32)
    rels_rep = sbsc("rels_rep", [128, NCH * NH], F32)



    s_pld = S("pld")

    @block.sync
    def _(sy):
        for dst, src_ in [(cw_sb, d_cw), (cb_sb, d_cb), (dtb_sb, d_dtb),
                          (alog_sb, d_alog), (drow, d_dflat), (nw_sb, d_nwp),
                          (gb_sb, d_gb), (vb_sb, d_vb), (obias_sb, d_obias),
                          (mask_sb, d_mask)]:
            dma(sy, s_pld, dst[:], src_)
    n_param_ld = s_pld.n

    @BV
    def _(v):
        v.memset(zscratch[:], 0.0)
        v.memset(mask01[:], 1.0)
        s_v.inc(v.memset(ones64[:], 1.0))
    zt_done = s_v.n

    @BG
    def _(g):
        g.wait_ge(s_v.h, zt_done)
        g.affine_select(out=ident[:], in_=zscratch[:], compare_op=OP.not_equal,
                        fill=1.0, base=0, pattern=[[-1, 128]], channel_multiplier=1)
        g.affine_select(out=identb[:], in_=zscratch[:], compare_op=OP.not_equal,
                        fill=1.0, base=0, pattern=[[-1, 128]], channel_multiplier=1)
        g.affine_select(out=maskneg[:], in_=zscratch[:], compare_op=OP.is_ge,
                        fill=-1e30, base=0, pattern=[[1, 128]], channel_multiplier=-1)
        s_gp.inc(g.affine_select(out=mask01[:], in_=mask01[:], compare_op=OP.is_ge,
                        fill=0.0, base=0, pattern=[[1, 128]], channel_multiplier=-1))
        g.wait_ge(s_pld.h, n_param_ld)
        s_gp.inc(g.partition_broadcast(out_ap=drep[:], in_ap=drow[0:1, :]))
    setup_gp = s_gp.n

    @BS
    def _(sc):
        sc.preload_act_table(AF.Exp)
        sc.wait_ge(s_pld.h, n_param_ld)
        sc.wait_ge(s_v.h, consts_done)
        s_sc.inc(sc.activation(out=negA[:], in_=alog_sb[:], func=AF.Exp, scale=1.0))
    negA_sc = s_sc.n

    @BV
    def _(v):
        v.wait_ge(s_sc.h, negA_sc)
        s_v.inc(v.tensor_scalar(out=negA[:], in0=negA[:], scalar1=-1.0,
                                scalar2=None, op0=OP.mult))

    xn_cms = [nc.sbuf_tensor("xnhi", [128, NIB_D, T + 4], BF16),
              nc.sbuf_tensor("xnlo", [128, NIB_D, T + 4], BF16)]
    xnhi = xn_cms[0].__enter__()
    xnlo = xn_cms[1].__enter__()
    THAL = T

    p0_ctx = []

    def sb0(name, shape, dtype):
        cm = nc.sbuf_tensor(name, list(shape), dtype)
        t = cm.__enter__()
        p0_ctx.append(cm)
        return t

    @BV
    def _(v):
        v.memset(xnhi.ap()[:, :, T:T + 4], 0.0)
        s_v.inc(v.memset(xnlo.ap()[:, :, T:T + 4], 0.0))

    ut = [sb0("ut0", [128, D], F32), sb0("ut1", [128, D], F32)]
    xnt = sb0("xnt", [128, D], F32)
    xct = sb0("xct", [128, D], F32)
    qi = sb0("qi", [128, D], I32)
    qc = sb0("qc", [128, D], F32)
    qout = sb0("qout", [128, D], BF16)
    hit = sb0("hit", [128, D], BF16)
    lot = sb0("lot", [128, D], BF16)
    qxn_fm = sb0("qxn_fm", [128, NIB_D, T], BF16)
    stat = sb0("stat", [128, 8], F32)
    pb_cm = [nc.psum_tensor(f"p0tp{i}", [128, 1024], BF16) for i in range(2)]
    pbank = [cm.__enter__() for cm in pb_cm]
    p0_ctx.extend(pb_cm)
    bank_free = [0, 0]

    tile_v_done = [0, 0]

    for tt in range(NT + 1):
        buf = ut[tt % 2]
        is_halo = tt == NT
        need_v = tile_v_done[tt % 2]

        @block.sync
        def _(sy, buf=buf, is_halo=is_halo, tt=tt, need_v=need_v):
            sy.wait_ge(s_v.h, need_v)
            src_ = d_uh if is_halo else d_u[tt * 128:(tt + 1) * 128, :]
            dma(sy, s_ld, buf[:], src_)
        ld_done = s_ld.n

        @BV
        def _(v, buf=buf, ld_done=ld_done):
            v.wait_ge(s_ld.h, ld_done)
            v.tensor_reduce(out=stat[:, 0:1], in_=buf[:], axis=AX, op=OP.add)
            v.tensor_scalar(out=stat[:, 0:1], in0=stat[:, 0:1], scalar1=1.0 / D,
                            scalar2=None, op0=OP.mult)
            v.tensor_scalar(out=xct[:], in0=buf[:], scalar1=stat[:, 0:1],
                            scalar2=None, op0=OP.subtract)
            v.tensor_tensor(out=qc[:], in0=xct[:], in1=xct[:], op=OP.mult)
            v.tensor_reduce(out=stat[:, 1:2], in_=qc[:], axis=AX, op=OP.add)
            s_v.inc(v.tensor_scalar(out=stat[:, 1:2], in0=stat[:, 1:2],
                                    scalar1=1.0 / D, scalar2=None, op0=OP.mult))
        tile_v_done[tt % 2] = s_v.n
        var_done = s_v.n

        @BS
        def _(sc, var_done=var_done):
            sc.wait_ge(s_v.h, var_done)
            sc.activation(out=stat[:, 2:3], in_=stat[:, 1:2], func=AF.Ln, bias=ccol(EPS), scale=1.0)
            s_sc.inc(sc.activation(out=stat[:, 2:3], in_=stat[:, 2:3], func=AF.Exp, scale=-0.5))
        rstd_done = s_sc.n

        @BS
        def _(sc, rstd_done=rstd_done):
            sc.wait_ge(s_sc.h, rstd_done)
            s_sc.inc(sc.activation(out=xnt[:], in_=xct[:], func=AF.Copy,
                                   scale=stat[:, 2:3]))
        xnt_done = s_sc.n

        @BV
        def _(v, xnt_done=xnt_done, is_halo=is_halo, tt=tt):
            v.wait_ge(s_sc.h, xnt_done)
            if not is_halo:
                v.tensor_reduce(out=stat[:, 3:4], in_=xnt[:], axis=AX,
                                op=OP.max, apply_absolute_value=True)
                s_v.inc(v.tensor_scalar(out=gx1_cols[:, tt:tt + 1], in0=stat[:, 3:4],
                                        scalar1=1e-5, scalar2=None, op0=OP.max))
            else:
                s_v.inc(v.nop())
        xn_done = s_v.n

        if not is_halo:
            @BS
            def _(sc, xn_done=xn_done):
                sc.wait_ge(s_v.h, xn_done)
                sc.activation(out=stat[:, 4:5], in_=stat[:, 3:4], func=AF.Ln)
                s_sc.inc(sc.activation(out=stat[:, 4:5], in_=stat[:, 4:5], func=AF.Exp,
                                       scale=-1.0, bias=ccol(LN127)))
            sc1_done = s_sc.n

            @BS
            def _(sc, sc1_done=sc1_done):
                sc.wait_ge(s_sc.h, sc1_done)
                sc.activation(out=qc[:], in_=xnt[:], func=AF.Identity,
                              scale=stat[:, 4:5], bias=ccol(0.5))
                s_sc.inc(sc.activation(out=qi[:], in_=qc[:], func=AF.Identity))
            qcast_done = s_sc.n

            @BV
            def _(v, qcast_done=qcast_done):
                v.wait_ge(s_sc.h, qcast_done)
                v.scalar_tensor_tensor(out=xct[:], in0=qi[:], scalar=0.0,
                                       in1=qc[:], op0=OP.add, op1=OP.is_gt)
                s_v.inc(v.tensor_tensor(out=qout[:], in0=xct[:], in1=qi[:], op=OP.subtract))

        @BS
        def _(sc, xnt_done=xnt_done):
            sc.wait_ge(s_sc.h, xnt_done)
            s_sc.inc(sc.activation(out=hit[:], in_=xnt[:], func=AF.Copy))
        hit_done = s_sc.n

        @BV
        def _(v, hit_done=hit_done):
            v.wait_ge(s_sc.h, hit_done)
            s_v.inc(v.tensor_tensor(out=lot[:], in0=xnt[:], in1=hit[:], op=OP.subtract))
        split_done = s_v.n

        for ib in range(NIB_D):
            bi = ib % 2
            pb = pbank[bi]
            need_free = bank_free[bi]

            @block.tensor
            def _(t, ib=ib, pb=pb, need_free=need_free, split_done=split_done,
                  is_halo=is_halo):
                t.wait_ge(s_gp.h, 1)
                t.wait_ge(s_v.h, max(need_free, split_done))
                sl = slice(ib * 128, (ib + 1) * 128)
                if not is_halo:
                    t.transpose(pb[:, 0:128], qout[:, sl], identb[:])
                t.transpose(pb[:, 128:256], hit[:, sl], identb[:])
                s_mm.inc(t.transpose(pb[:, 256:384], lot[:, sl], identb[:]))
            tp_done = s_mm.n

            @BV
            def _(v, ib=ib, pb=pb, tp_done=tp_done, is_halo=is_halo, tt=tt):
                v.wait_ge(s_mm.h, tp_done)
                col = THAL if is_halo else tt * 128
                w = 3 if is_halo else 128
                if not is_halo:
                    v.tensor_copy(out=qxn_fm[:, ib, col:col + 128], in_=pb[:, 0:128])
                v.tensor_copy(out=xnhi[:, ib, col:col + w], in_=pb[:, 128:128 + w])
                s_v.inc(v.tensor_copy(out=xnlo[:, ib, col:col + w], in_=pb[:, 256:256 + w]))
            bank_free[bi] = s_v.n

    @BG
    def _(g, need=s_v.n):
        g.wait_ge(s_v.h, need)
        dma(g, s_st, d_qxn, qxn_fm[:])
    qxn_spilled = s_st.n

    if debug:
        d_xnhi = dbg("dbg_xnhi", [128, NIB_D, T + 4], BF16)
        d_xnlo = dbg("dbg_xnlo", [128, NIB_D, T + 4], BF16)
        d_qxnd = dbg("dbg_qxn", [128, NIB_D, T], BF16)

        @BG
        def _(g, need=s_v.n):
            g.wait_ge(s_v.h, need)
            dma(g, s_st, d_xnhi, xnhi[:])
            dma(g, s_st, d_xnlo, xnlo[:])
            dma(g, s_st, d_qxnd, qxn_fm[:])

    @BG
    def _(g):
        g.wait_ge(s_st.h, s_st.n)
        s_gp.inc(g.nop())
    p0_closed = s_gp.n

    for cm in reversed(p0_ctx):
        cm.__exit__(None, None, None)

    p1_ctx = []

    def sb1(name, shape, dtype):
        cm = nc.sbuf_tensor(name, list(shape), dtype)
        t = cm.__enter__()
        p1_ctx.append(cm)
        return t

    def ps1(name, shape, dtype=F32):
        cm = nc.psum_tensor(name, list(shape), dtype)
        t = cm.__enter__()
        p1_ctx.append(cm)
        return t

    wraw = [sb1(f"wraw{i}", [128, NIB_D * 128], F32) for i in range(2)]
    wh = [sb1(f"wh{i}", [128, NIB_D * 128], BF16) for i in range(2)]
    wl = [sb1(f"wl{i}", [128, NIB_D * 128], BF16) for i in range(2)]
    xraw = sb1("xraw", [128, 1027 + 5], F32)
    cacc = [sb1(f"cacc{i}", [128, T], F32) for i in range(2)]
    xsil = [sb1(f"xsil{i}", [128, T], F32) for i in range(2)]
    xsil_free = [0, 0]
    cacc_free = [0, 0]
    coll = [sb1(f"coll{i}", [128, NT * 128], F32) for i in range(2)]
    pk = [ps1(f"p1b{i}", [128, 512], F32) for i in range(6)]
    ptp_full = [ps1(f"p1t{i}", [128, 512], F32) for i in range(2)]
    ptp = [t[:, 0:128] for t in ptp_full]

    s_w = [S("w0"), S("w1")]
    s_sp = S("sp")
    wraw_free = [0, 0]
    whl_free = [0, 0]
    pset_free = [0, 0]
    tp_free = [0, 0]
    coll_free = [0, 0]

    @BS
    def _(sc):
        sc.preload_act_table(AF.Silu)

    def p1_transpose_spill(par, src_sil, dram_dst, cidx, sil_mark):
        cbuf = coll[cidx % 2]
        need_coll = coll_free[cidx % 2]
        for k in range(NT):
            tb = ptp[k % 2]
            need_tp = tp_free[k % 2]

            @block.tensor
            def _(t, k=k, tb=tb, need_tp=need_tp, need_coll=need_coll,
                  src_sil=src_sil, at_sc=sil_mark):
                t.wait_ge(s_sc.h, at_sc)
                t.wait_ge(s_v.h, need_tp)
                if need_coll:
                    t.wait_ge(s_sp.h, need_coll)
                s_mm.inc(t.transpose(tb, src_sil[:, k * 128:(k + 1) * 128], ident[:]))
            tpd = s_mm.n

            @BV
            def _(v, k=k, tb=tb, tpd=tpd, cbuf=cbuf):
                v.wait_ge(s_mm.h, tpd)
                s_v.inc(v.tensor_copy(out=cbuf[:, k * 128:(k + 1) * 128], in_=tb))
            tp_free[k % 2] = s_v.n
        xsil_free[par] = s_mm.n
        ev_done = s_v.n

        @BG
        def _(g, cbuf=cbuf, ev_done=ev_done, dram_dst=dram_dst):
            g.wait_ge(s_v.h, ev_done)
            dma(g, s_sp, dram_dst.rearrange("tt p c -> p tt c"),
                cbuf.ap().rearrange("p (tt c) -> p tt c", tt=NT))
        coll_free[cidx % 2] = s_sp.n

    def p1_evac_z(zb, par, mm_stop):
        po0, po1 = pk[3 * par], pk[3 * par + 1]
        need_xs = xsil_free[par]

        @BS
        def _(sc, po0=po0, po1=po1, mm_stop=mm_stop, need_xs=need_xs, par=par):
            sc.wait_ge(s_mm.h, max(mm_stop, need_xs))
            sc.activation(out=xsil[par][:, 0:512], in_=po0[:], func=AF.Silu)
            s_sc.inc(sc.activation(out=xsil[par][:, 512:1024], in_=po1[:], func=AF.Silu))
        pset_free[par] = -1
        sil_mark = s_sc.n
        return lambda: p1_transpose_spill(par, xsil[par], d_ztm[zb], zb, sil_mark)

    def p1_evac_x(xcb, par, mm_stop):
        po0, po1, ph = pk[3 * par], pk[3 * par + 1], pk[3 * par + 2]
        need_ca = cacc_free[par]

        @BV
        def _(v, po0=po0, po1=po1, ph=ph, mm_stop=mm_stop, xcb=xcb, par=par,
              need_ca=need_ca):
            v.wait_ge(s_mm.h, mm_stop)
            v.wait_ge(s_sc.h, need_ca)
            v.tensor_copy(out=xraw[:, 0:3], in_=ph[:, 0:3])
            v.tensor_copy(out=xraw[:, 3:515], in_=po0[:])
            v.tensor_copy(out=xraw[:, 515:1027], in_=po1[:])
            v.tensor_scalar(out=cacc[par][:], in0=xraw[:, 0:T],
                            scalar1=cw_sb[:, 4 * xcb:4 * xcb + 1],
                            scalar2=None, op0=OP.mult)
            for k2 in range(1, 3):
                v.scalar_tensor_tensor(out=cacc[par][:], in0=xraw[:, k2:k2 + T],
                                       scalar=cw_sb[:, 4 * xcb + k2:4 * xcb + k2 + 1],
                                       in1=cacc[par][:], op0=OP.mult, op1=OP.add)
            s_v.inc(v.scalar_tensor_tensor(out=cacc[par][:], in0=xraw[:, 3:3 + T],
                                           scalar=cw_sb[:, 4 * xcb + 3:4 * xcb + 4],
                                           in1=cacc[par][:], op0=OP.mult, op1=OP.add))
        conv_done = s_v.n
        pset_free[par] = conv_done

        if xcb < 32:
            need_xs = xsil_free[par]

            @BS
            def _(sc, xcb=xcb, conv_done=conv_done, par=par, need_xs=need_xs):
                sc.wait_ge(s_v.h, conv_done)
                sc.wait_ge(s_mm.h, need_xs)
                s_sc.inc(sc.activation(out=xsil[par][:, 0:T], in_=cacc[par][:],
                                       func=AF.Silu, bias=cb_sb[:, xcb:xcb + 1]))
            cacc_free[par] = s_sc.n
            sil_mark = s_sc.n
            return lambda: p1_transpose_spill(par, xsil[par], d_xtm[xcb], xcb, sil_mark)
        else:
            dst = B_sb if xcb == 32 else C_sb

            @BS
            def _(sc, xcb=xcb, conv_done=conv_done, dst=dst, par=par):
                sc.wait_ge(s_v.h, conv_done)
                s_sc.inc(sc.activation(out=dst[:], in_=cacc[par][:], func=AF.Silu,
                                       bias=cb_sb[:, xcb:xcb + 1]))
            cacc_free[par] = s_sc.n

    def p1_evac_dt(par, mm_stop):
        po0, po1 = pk[3 * par], pk[3 * par + 1]

        @BS
        def _(sc, po0=po0, po1=po1, mm_stop=mm_stop):
            sc.wait_ge(s_mm.h, mm_stop)
            sc.preload_act_table(AF.Exp)
            sc.activation(out=dt_s[:, 0:512], in_=po0[0:NH, :], func=AF.Exp,
                          bias=dtb_sb[:, 0:1])
            sc.activation(out=dt_s[:, 512:1024], in_=po1[0:NH, :], func=AF.Exp,
                          bias=dtb_sb[:, 0:1])
            sc.activation(out=dt_s[:, 0:512], in_=dt_s[:, 0:512], func=AF.Ln,
                          bias=ccol(1.0)[0:NH, :])
            s_sc.inc(sc.activation(out=dt_s[:, 512:1024], in_=dt_s[:, 512:1024],
                                   func=AF.Ln, bias=ccol(1.0)[0:NH, :]))
        dt_done = s_sc.n
        pset_free[par] = -2

        @BV
        def _(v, dt_done=dt_done):
            v.wait_ge(s_sc.h, dt_done)
            v.tensor_scalar(out=bb64[:], in0=dt_s[:], scalar1=negA[:, 0:1],
                            scalar2=None, op0=OP.mult)
            v.memset(cs_ext[:], 0.0)
            v.tensor_tensor_scan(out=cs_ext[:, 1:T + 1], data0=ones64[:],
                                 data1=bb64[:], initial=0.0, op0=OP.mult, op1=OP.add)
            s_v.inc(v.tensor_scalar(out=logdt[:], in0=dt_s[:], scalar1=1e-19,
                                    scalar2=None, op0=OP.max))
        l_done = s_v.n

        @BS
        def _(sc, l_done=l_done):
            sc.wait_ge(s_v.h, l_done)
            s_sc.inc(sc.activation(out=logdt[:], in_=logdt[:], func=AF.Ln))
        ln_done = s_sc.n

        @BV
        def _(v, ln_done=ln_done):
            v.wait_ge(s_sc.h, ln_done)
            s_v.inc(v.tensor_tensor(out=bb64[:], in0=logdt[:], in1=cs_ext[:, 1:T + 1],
                                    op=OP.subtract))
        return s_v.n

    gbuf = [sb1(f"gbuf{i}", [128, 2048], F32) for i in range(2)]
    gtmp = sb1("gtmp", [128, 2], F32)
    s_gm = [S("gm0"), S("gm1")]
    gq_free = [0, 0]

    @BV
    def _(v):
        s_v.inc(v.memset(gam_part[:], 0.0))

    gpieces = []
    for ob_ in range(8):
        gpieces.append((0, d_gg[ob_].rearrange("p ib o -> p (ib o)")))
    for ob_ in range(8):
        gpieces.append((1, d_gv[ob_].rearrange("p ib o -> p (ib o)")))
    for ob_ in range(2):
        flat = d_go[ob_].rearrange("p ib o -> p (ib o)")
        for k_ in range(4):
            gpieces.append((2, flat[:, k_ * 2048:(k_ + 1) * 2048]))

    for gi, (mat, srcap) in enumerate(gpieces):
        gpar = gi % 2

        @BG
        def _(g, gi=gi, gpar=gpar, srcap=srcap):
            g.wait_ge(s_v.h, gq_free[gpar])
            dma(g, s_gm[gpar], gbuf[gpar][:], srcap)
        gld = s_gm[gpar].n

        @BV
        def _(v, gi=gi, gpar=gpar, mat=mat, gld=gld):
            v.wait_ge(s_gm[gpar].h, gld)
            v.tensor_reduce(out=gtmp[:, 0:1], in_=gbuf[gpar][:], axis=AX,
                            op=OP.add, apply_absolute_value=True)
            s_v.inc(v.tensor_tensor(out=gam_part[:, mat:mat + 1],
                                    in0=gam_part[:, mat:mat + 1],
                                    in1=gtmp[:, 0:1], op=OP.add))
        gq_free[gpar] = s_v.n
    gam_done = s_v.n

    xn_ready = s_v.n
    fence = {s: s.n for s in (s_v, s_sc, s_mm, s_st, s_gp)}
    split_marks = {}

    def p1_load_split(ob):
        par = ob % 2

        @block.sync
        def _(sy, ob=ob, par=par):
            if ob == 0:
                for s_, n_ in fence.items():
                    sy.wait_ge(s_.h, n_)
            sy.wait_ge(s_v.h, wraw_free[par])
            dma(sy, s_w[par], wraw[par][:], d_win[ob])
        wld = s_w[par].n

        @BS
        def _(sc, par=par, wld=wld, wf=whl_free[par]):
            sc.wait_ge(s_w[par].h, wld)
            sc.wait_ge(s_mm.h, wf)
            s_sc.inc(sc.copy(out=wh[par][:], in_=wraw[par][:]))
        whd = s_sc.n

        @BV
        def _(v, par=par, wld=wld, whd=whd, wf=whl_free[par]):
            v.wait_ge(s_w[par].h, wld)
            v.wait_ge(s_sc.h, whd)
            v.wait_ge(s_mm.h, wf)
            s_v.inc(v.tensor_tensor(out=wl[par][:], in0=wraw[par][:], in1=wh[par][:],
                                    op=OP.subtract))
        wraw_free[par] = s_v.n
        split_marks[ob] = (whd, s_v.n)

    def p1_mm_evac(ob):
        par = ob % 2
        is_x = 32 <= ob < 66
        need_pset = pset_free[par]
        whd, wld_split = split_marks[ob]

        @block.tensor
        def _(t, ob=ob, par=par, whd=whd, wld_split=wld_split, is_x=is_x,
              need_pset=need_pset, xn_ready=xn_ready):
            t.wait_ge(s_sc.h, whd)
            t.wait_ge(s_v.h, wld_split)
            t.wait_ge(s_v.h, xn_ready)
            if need_pset > 0:
                t.wait_ge(s_v.h, need_pset)
            elif need_pset == -1 or need_pset == -2:
                t.wait_ge(s_sc.h, s_sc.n)
            po0, po1, ph = pk[3 * par], pk[3 * par + 1], pk[3 * par + 2]
            first = True
            last = None
            for ib in range(NIB_D):
                wha = wh[par][:, ib * 128:(ib + 1) * 128]
                wla = wl[par][:, ib * 128:(ib + 1) * 128]
                hi0 = xnhi[:, ib, 0:512]
                hi1 = xnhi[:, ib, 512:1024]
                lo0 = xnlo[:, ib, 0:512]
                lo1 = xnlo[:, ib, 512:1024]
                t.matmul(po0[:], wha, hi0, start=first, stop=False)
                t.matmul(po1[:], wha, hi1, start=first, stop=False)
                t.matmul(po0[:], wha, lo0, start=False, stop=False)
                t.matmul(po1[:], wha, lo1, start=False, stop=False)
                if is_x:
                    t.matmul(ph[:, 0:3], wha, xnhi[:, ib, T:T + 3], start=first, stop=False)
                    t.matmul(ph[:, 0:3], wha, xnlo[:, ib, T:T + 3], start=False, stop=False)
                t.matmul(po0[:], wla, hi0, start=False, stop=(ib == NIB_D - 1))
                last = t.matmul(po1[:], wla, hi1, start=False, stop=(ib == NIB_D - 1))
                if is_x:
                    last = t.matmul(ph[:, 0:3], wla, xnhi[:, ib, T:T + 3], start=False,
                                    stop=(ib == NIB_D - 1))
                first = False
            s_mm.inc(last)
        mm_stop = s_mm.n
        whl_free[par] = s_mm.n

        if ob < 32:
            return ("T", p1_evac_z(ob, par, mm_stop))
        elif is_x:
            return ("T", p1_evac_x(ob - 32, par, mm_stop))
        else:
            return ("dt", p1_evac_dt(par, mm_stop))

    dt_chain_done = None
    pending_T = None

    def run_mm_evac(ob):
        nonlocal dt_chain_done, pending_T
        kind, val = p1_mm_evac(ob)
        prev = pending_T
        pending_T = val if kind == "T" else None
        if kind == "dt":
            dt_chain_done = val
        if prev is not None:
            prev()

    for ob in range(NOB_IN):
        p1_load_split(ob)
        if ob >= 1:
            run_mm_evac(ob - 1)
    run_mm_evac(NOB_IN - 1)
    if pending_T is not None:
        pending_T()

    smallT = sb1("smallT", [NH, 128], F32)
    smallT2 = sb1("smallT2", [128, 128], F32)

    @BV
    def _(v):
        v.wait_ge(s_v.h, dt_chain_done)
        v.memset(smallT[:], 0.0)
        for c in range(NCH):
            v.tensor_scalar(out=smallT[:, c:c + 1],
                            in0=cs_ext[0:NH, (c + 1) * 128:(c + 1) * 128 + 1],
                            scalar1=cs_ext[0:NH, c * 128:c * 128 + 1],
                            scalar2=None, op0=OP.subtract)
            v.tensor_copy(out=smallT[:, NCH + c:NCH + c + 1],
                          in_=cs_ext[0:NH, c * 128:c * 128 + 1])
        s_v.inc(v.memset(smallT[:, 2 * NCH:2 * NCH + 1], 0.0))
    sm_args = s_v.n

    @BS
    def _(sc, sm_args=sm_args):
        sc.wait_ge(s_v.h, sm_args)
        s_sc.inc(sc.activation(out=smallT[:, 0:2 * NCH], in_=smallT[:, 0:2 * NCH], func=AF.Exp))
    sm_exp = s_sc.n

    @block.tensor
    def _(t, sm_exp=sm_exp):
        t.wait_ge(s_sc.h, sm_exp)
        t.wait_ge(s_v.h, tp_free[0])
        s_mm.inc(t.transpose(ptp_full[0][:, 0:NH], smallT[:], ident[0:NH, 0:NH]))
    smT_done = s_mm.n

    @BV
    def _(v, smT_done=smT_done):
        v.wait_ge(s_mm.h, smT_done)
        s_v.inc(v.tensor_copy(out=smallT2[:, 0:NH], in_=ptp_full[0][:, 0:NH]))
    smT2_done = s_v.n

    smflat = sb1("smflat", [1, 2 * NCH * NH], F32)

    @BG
    def _(g, smT2_done=smT2_done):
        g.wait_ge(s_v.h, smT2_done)
        dma(g, s_st, d_small, smallT2[0:2 * NCH, 0:NH])
        g.wait_ge(s_st.h, s_st.n)
        dma(g, s_st, smflat[0:1, :], d_small.rearrange("r c -> (r c)")[None, :])
        g.wait_ge(s_st.h, s_st.n)
        g.partition_broadcast(out_ap=aend_rep[:], in_ap=smflat[0:1, 0:NCH * NH])
        s_gp.inc(g.partition_broadcast(out_ap=rels_rep[:],
                                       in_ap=smflat[0:1, NCH * NH:2 * NCH * NH]))
    reps_done = s_gp.n

    scl64 = sb1("scl64", [NH, 128], F32)
    exr64 = sb1("exr64", [NH, 128], F32)
    for c in range(NCH):
        @BV
        def _(v, c=c):
            v.wait_ge(s_v.h, s_v.n)
            v.tensor_scalar(out=scl64[:], in0=bb64[:, c * 128:(c + 1) * 128],
                            scalar1=cs_ext[0:NH, (c + 1) * 128:(c + 1) * 128 + 1],
                            scalar2=None, op0=OP.add)
            s_v.inc(v.tensor_scalar(out=exr64[:], in0=cs_ext[0:NH, c * 128 + 1:(c + 1) * 128 + 1],
                                    scalar1=cs_ext[0:NH, c * 128:c * 128 + 1],
                                    scalar2=None, op0=OP.subtract))
        args_done = s_v.n

        @BS
        def _(sc, c=c, args_done=args_done):
            sc.wait_ge(s_v.h, args_done)
            sc.activation(out=scl64[:], in_=scl64[:], func=AF.Exp)
            s_sc.inc(sc.activation(out=exr64[:], in_=exr64[:], func=AF.Exp))
        exps_done = s_sc.n

        for srcbuf, dstbuf in ((bb64, bbT), (scl64, sclT), (exr64, exprelT)):
            @block.tensor
            def _(t, c=c, srcbuf=srcbuf, exps_done=exps_done, need_tp=tp_free[1]):
                t.wait_ge(s_sc.h, exps_done)
                t.wait_ge(s_v.h, need_tp)
                src_ap = srcbuf[:, c * 128:(c + 1) * 128] if srcbuf is bb64 else srcbuf[:]
                s_mm.inc(t.transpose(ptp_full[1][:, 0:NH], src_ap, ident[0:NH, 0:NH]))
            td = s_mm.n

            @BV
            def _(v, c=c, dstbuf=dstbuf, td=td):
                v.wait_ge(s_mm.h, td)
                s_v.inc(v.tensor_copy(out=dstbuf[:, c * NH:(c + 1) * NH], in_=ptp_full[1][:, 0:NH]))
            tp_free[1] = s_v.n
    prep_done = s_v.n

    if debug:
        d_dts = dbg("dbg_dts", [NH, T])
        d_cse = dbg("dbg_cse", [NH, T + 8])
        d_Bsb = dbg("dbg_B", [128, T])
        d_Csb = dbg("dbg_C", [128, T])
        d_zt = dbg("dbg_ztm", [32, NT, 128, 128])
        d_xt = dbg("dbg_xtm", [32, NT, 128, 128])

        @BG
        def _(g):
            g.wait_ge(s_v.h, prep_done)
            g.wait_ge(s_sp.h, s_sp.n)
            dma(g, s_st, d_dts, dt_s[:])
            dma(g, s_st, d_cse, cs_ext[:])
            dma(g, s_st, d_Bsb, B_sb[:])
            dma(g, s_st, d_Csb, C_sb[:])
            dma(g, s_st, d_zt, d_ztm)
            dma(g, s_st, d_xt, d_xtm)

    for cm in reversed(p1_ctx):
        cm.__exit__(None, None, None)
    for cm in reversed(xn_cms):
        cm.__exit__(None, None, None)

    p2_ctx = []

    def sb2(name, shape, dtype):
        cm = nc.sbuf_tensor(name, list(shape), dtype)
        t = cm.__enter__()
        p2_ctx.append(cm)
        return t

    def ps2(name, shape, dtype=F32):
        cm = nc.psum_tensor(name, list(shape), dtype)
        t = cm.__enter__()
        p2_ctx.append(cm)
        return t

    init_sb = sbsc("init_sb", [128, DSSM], F32)
    c8rep = sbsc("c8rep", [128, 8 * NH], F32)

    xh_t = [sb2(f"xh_t{i}", [128, DSSM], F32) for i in range(2)]
    xsc = sb2("xsc", [128, DSSM], F32)
    BT_sb = sb2("BT_sb", [128, 128], F32)
    chain = [sb2(f"chain{i}", [128, DSSM], F32) for i in range(2)]
    pst = [ps2(f"p2st{i}", [128, 512], F32) for i in range(2)]
    ptr2_full = ps2("p2tp", [128, 512], F32)
    s_x = S("x2")
    s_ch = S("ch")

    fence2 = {s: s.n for s in (s_v, s_sc, s_mm, s_st, s_gp, s_sp)}

    @BV
    def _(v):
        for s_, n_ in fence2.items():
            v.wait_ge(s_.h, n_)
        s_v.inc(v.memset(chain[0][:], 0.0))
    chain_ready = s_v.n

    xh_free = [0, 0]
    pst_free2 = [0, 0]
    chsp_prev = 0
    xsc_free = 0
    for c in range(NCH):
        par = c % 2

        @block.sync
        def _(sy, c=c, par=par, xhf=xh_free[par], chr_=chain_ready):
            sy.wait_ge(s_v.h, max(xhf, chr_))
            dma(sy, s_x, xh_t[par].ap().rearrange("p (ob c2) -> p ob c2", ob=32),
                d_xtm[:, c].rearrange("ob p c2 -> p ob c2"))
        xld = s_x.n

        @BV
        def _(v, c=c, par=par, xld=xld, xscf=xsc_free):
            v.wait_ge(s_x.h, xld)
            v.wait_ge(s_mm.h, xscf)
            s_v.inc(v.tensor_tensor(
                out=xsc.ap().rearrange("p (h q) -> p h q", h=NH),
                in0=xh_t[par].ap().rearrange("p (h q) -> p h q", h=NH),
                in1=bc64(sclT[:, c * NH:(c + 1) * NH], NH), op=OP.mult))
        xh_free[par] = s_v.n
        xsc_done = s_v.n

        @block.tensor
        def _(t, c=c):
            t.wait_ge(s_v.h, s_v.n)
            s_mm.inc(t.transpose(ptr2_full[:, 0:128], B_sb[:, c * 128:(c + 1) * 128],
                                 ident[:]))
        btd = s_mm.n

        @BV
        def _(v, btd=btd):
            v.wait_ge(s_mm.h, btd)
            s_v.inc(v.tensor_copy(out=BT_sb[:], in_=ptr2_full[:, 0:128]))
        bt_ready = s_v.n

        @BG
        def _(g, c=c, par=par, chain_ready=chain_ready):
            g.wait_ge(s_v.h, chain_ready)
            dma(g, s_ch, d_chain[c], chain[par][:])
        chsp = chsp_prev
        chsp_prev_next = s_ch.n

        for q in range(8):
            pb2 = pst[q % 2]
            at_v = pst_free2[q % 2]

            @block.tensor
            def _(t, q=q, pb2=pb2, xsc_done=xsc_done, bt_ready=bt_ready, at_v=at_v):
                t.wait_ge(s_v.h, max(xsc_done, bt_ready, at_v))
                s_mm.inc(t.matmul(pb2[:], BT_sb[:], xsc[:, q * 512:(q + 1) * 512],
                                  start=True, stop=True))
            mmd = s_mm.n

            @BV
            def _(v, q=q, c=c, par=par, pb2=pb2, mmd=mmd, chsp=chsp):
                v.wait_ge(s_mm.h, mmd)
                v.wait_ge(s_ch.h, chsp)
                sl = slice(q * 512, (q + 1) * 512)
                v.tensor_tensor(
                    out=chain[1 - par].ap()[:, sl].rearrange("p (h q2) -> p h q2", h=8),
                    in0=chain[par].ap()[:, sl].rearrange("p (h q2) -> p h q2", h=8),
                    in1=bc64(aend_rep[:, c * NH + q * 8:c * NH + (q + 1) * 8], 8),
                    op=OP.mult)
                s_v.inc(v.tensor_tensor(out=chain[1 - par][:, sl],
                                        in0=chain[1 - par][:, sl],
                                        in1=pb2[:], op=OP.add))
            pst_free2[q % 2] = s_v.n
        chain_ready = s_v.n
        chsp_prev = chsp_prev_next
        xsc_free = s_mm.n

    final_par = NCH % 2

    gam3 = sb2("gam3", [8, 8], F32)

    @block.tensor
    def _(t):
        t.wait_ge(s_v.h, s_v.n)
        s_mm.inc(t.transpose(ptr2_full[0:4, 0:128], gam_part[:, 0:4], ident[:]))
    g3t = s_mm.n

    @BV
    def _(v, g3t=g3t):
        v.wait_ge(s_mm.h, g3t)
        s_v.inc(v.tensor_reduce(out=gam3[0:4, 0:1], in_=ptr2_full[0:4, 0:128],
                                axis=AX, op=OP.add))
    g3_done = s_v.n

    zrow = sb2("zrow", [1, DSSM], F32)

    @BV
    def _(v):
        s_v.inc(v.memset(zrow[:], 0.0))
    zrow_done = s_v.n

    @BG
    def _(g, g3_done=g3_done, final_par=final_par, zrow_done=zrow_done):
        g.wait_ge(s_v.h, max(g3_done, zrow_done))
        dma(g, s_st, t_ccin.ap()[128:129, :], zrow[0:1, :])
        dma(g, s_st, t_ccin.ap()[0:128, :], chain[final_par][:])
        dma(g, s_st, t_ccin.ap()[128:129, 0:NH].rearrange("one h -> h one"),
            cs_ext[0:NH, T:T + 1])
        dma(g, s_st, t_ccin.ap()[128:129, NH:NH + 3].rearrange("one m -> m one"),
            gam3[0:3, 0:1])
        g.wait_ge(s_st.h, s_st.n)
        s_cc.inc(g.collective_compute(
            "AllGather", OP.bypass,
            replica_groups=[list(range(NCORE))],
            ins=[t_ccin.ap().opt()], outs=[t_ccout.ap().opt()]))
    cc_done = s_cc.n

    ext8T = sb2("ext8T", [80, 8], F32)
    mrow = sb2("mrow", [80, 8], F32)
    c8 = sb2("c8", [NH, 16], F32)

    @block.sync
    def _(sy, cc_done=cc_done):
        sy.wait_ge(s_cc.h, cc_done)
        dma(sy, s_ld, ext8T[0:NH + 3, :],
            t_ccout.ap().rearrange("(j r) q -> j r q", r=129)[:, 128, 0:NH + 3]
            .rearrange("j q -> q j"), allow_slow_non_contiguous=True)
        dma(sy, s_ld, mrow[0:1, :], d_mask.rearrange("j one -> (j one)")[None, :])
    ext_ld = s_ld.n

    @BG
    def _(g, ext_ld=ext_ld):
        g.wait_ge(s_ld.h, ext_ld)
        s_gp.inc(g.partition_broadcast(out_ap=mrow[0:NH, :], in_ap=mrow[0:1, :]))
    mrep_done = s_gp.n

    @BV
    def _(v, mrep_done=mrep_done):
        v.wait_ge(s_gp.h, mrep_done)
        v.tensor_tensor(out=c8[:, 8:16], in0=ext8T[0:NH, :], in1=mrow[0:NH, :],
                        op=OP.mult)
        v.memset(c8[:, 7:8], 0.0)
        for j in range(6, -1, -1):
            v.scalar_tensor_tensor(out=c8[:, j:j + 1], in0=c8[:, j + 1:j + 2],
                                   scalar=1.0, in1=c8[:, 8 + j + 1:8 + j + 2],
                                   op0=OP.mult, op1=OP.add)
        s_v.inc(v.memset(c8[:, 15:16], 0.0))
    sfx_done = s_v.n

    @BS
    def _(sc, sfx_done=sfx_done):
        sc.wait_ge(s_v.h, sfx_done)
        s_sc.inc(sc.activation(out=c8[:, 0:8], in_=c8[:, 0:8], func=AF.Exp))
    c8e_done = s_sc.n

    @BV
    def _(v, c8e_done=c8e_done):
        v.wait_ge(s_sc.h, c8e_done)
        s_v.inc(v.tensor_tensor(out=c8[:, 0:8], in0=c8[:, 0:8], in1=mrow[0:NH, :],
                                op=OP.mult))
    c8_done = s_v.n

    d_c8 = nc.dram_tensor("sp_c8", [NH, 8], F32).ap()
    c8row = sb2("c8row", [1, 8 * NH], F32)

    @BG
    def _(g, c8_done=c8_done):
        g.wait_ge(s_v.h, c8_done)
        dma(g, s_st, d_c8, c8[0:NH, 0:8])
        g.wait_ge(s_st.h, s_st.n)
        dma(g, s_st, c8row.ap().rearrange("one (j h) -> one j h", j=8),
            d_c8.rearrange("h j -> j h")[None, :, :], allow_slow_non_contiguous=True)
        g.wait_ge(s_st.h, s_st.n)
        s_gp.inc(g.partition_broadcast(out_ap=c8rep[:], in_ap=c8row[0:1, :]))
    c8rep_done = s_gp.n

    grow = sb2("grow", [1, 8], F32)

    @BV
    def _(v, mrep_done=mrep_done):
        v.wait_ge(s_gp.h, mrep_done)
        s_v.inc(v.tensor_reduce(out=ext8T[NH:NH + 3, 0:1], in_=ext8T[NH:NH + 3, :],
                                axis=AX, op=OP.add))
    gsum_done = s_v.n

    NW_GATE = float(HID * D)

    @BS
    def _(sc, gsum_done=gsum_done):
        sc.wait_ge(s_v.h, gsum_done)
        s_sc.inc(sc.activation(out=ext8T[NH:NH + 3, 1:2], in_=ext8T[NH:NH + 3, 0:1],
                               func=AF.Ln))
    gln_done = s_sc.n

    LNW = float(np.log(NW_GATE))

    @BV
    def _(v, gln_done=gln_done):
        v.wait_ge(s_sc.h, gln_done)
        s_v.inc(v.tensor_scalar(out=ext8T[NH:NH + 3, 1:2], in0=ext8T[NH:NH + 3, 1:2],
                                scalar1=LNW, scalar2=float(np.log(1e-5)),
                                op0=OP.subtract, op1=OP.max))
    glnc_done = s_v.n

    @BS
    def _(sc, glnc_done=glnc_done):
        sc.wait_ge(s_v.h, glnc_done)
        sc.activation(out=ext8T[NH:NH + 3, 2:3], in_=ext8T[NH:NH + 3, 1:2], func=AF.Exp)
        s_sc.inc(sc.activation(out=ext8T[NH:NH + 3, 3:4], in_=ext8T[NH:NH + 3, 1:2],
                               func=AF.Exp, scale=-1.0))
    gexp_done = s_sc.n

    d_g6 = nc.dram_tensor("sp_g6", [3, 2], F32).ap()

    @BG
    def _(g, gexp_done=gexp_done):
        g.wait_ge(s_sc.h, gexp_done)
        dma(g, s_st, d_g6, ext8T[NH:NH + 3, 2:4])
        g.wait_ge(s_st.h, s_st.n)
        dma(g, s_st, grow.ap()[0:1, 0:6].rearrange("one (k m) -> one k m", k=2),
            d_g6.rearrange("m k -> k m")[None, :, :], allow_slow_non_contiguous=True)
        g.wait_ge(s_st.h, s_st.n)
        s_gp.inc(g.partition_broadcast(out_ap=gsc[:, 0:6],
                                       in_ap=grow[0:1, 0:6]))
    gsc_done = s_gp.n

    stsl = [sb2(f"stsl{i}", [128, DSSM], F32) for i in range(2)]
    s_sl = [S("sl0"), S("sl1")]
    stsl_free = [0, 0]

    @BV
    def _(v):
        s_v.inc(v.memset(init_sb[:], 0.0))
    init_z = s_v.n

    for j in range(NCORE):
        jp = j % 2

        @block.sync
        def _(sy, j=j, jp=jp, cc_done=cc_done, sf=stsl_free[jp]):
            sy.wait_ge(s_cc.h, cc_done)
            sy.wait_ge(s_v.h, sf)
            dma(sy, s_sl[jp], stsl[jp][:], t_ccout.ap()[129 * j:129 * j + 128, :])
        sld = s_sl[jp].n

        @BV
        def _(v, j=j, jp=jp, sld=sld, c8rep_done=c8rep_done, init_z=init_z):
            v.wait_ge(s_sl[jp].h, sld)
            v.wait_ge(s_gp.h, c8rep_done)
            v.wait_ge(s_v.h, init_z)
            v.tensor_tensor(
                out=stsl[jp].ap().rearrange("p (h q) -> p h q", h=NH),
                in0=stsl[jp].ap().rearrange("p (h q) -> p h q", h=NH),
                in1=bc64(c8rep[:, j * NH:(j + 1) * NH], NH), op=OP.mult)
            s_v.inc(v.tensor_tensor(out=init_sb[:], in0=init_sb[:],
                                    in1=stsl[jp][:], op=OP.add))
        stsl_free[jp] = s_v.n
    init_done = s_v.n

    for cm in reversed(p2_ctx):
        cm.__exit__(None, None, None)

    if debug:
        d_chx = dbg("dbg_chain", [NCH, 128, DSSM])
        d_init = dbg("dbg_init", [128, DSSM])
        d_c8d = dbg("dbg_c8", [NH, 16])

        @BG
        def _(g, init_done=init_done, final_par=final_par):
            g.wait_ge(s_v.h, init_done)
            dma(g, s_st, d_chx, d_chain)
            dma(g, s_st, d_init, init_sb[:])
            dma(g, s_st, d_c8d, c8[0:NH, :])

    p3_ctx = []

    def sb3(name, shape, dtype):
        cm = nc.sbuf_tensor(name, list(shape), dtype)
        t = cm.__enter__()
        p3_ctx.append(cm)
        return t

    def ps3(name, shape, dtype=F32):
        cm = nc.psum_tensor(name, list(shape), dtype)
        t = cm.__enter__()
        p3_ctx.append(cm)
        return t

    xh3 = [sb3(f"xh3_{i}", [128, DSSM], F32) for i in range(2)]
    zch = [sb3("zch0", [128, DSSM], F32)]
    stcor = sb3("stcor", [128, DSSM], F32)
    Ebuf = [sb3(f"Ebuf{i}", [128, 16 * 128], F32) for i in range(2)]
    bbrow = [sb3(f"bbrow{i}", [1, 16 * 128], F32) for i in range(2)]
    csrow = [sb3(f"csrow{i}", [1, 16 * 128], F32) for i in range(2)]
    ones2k = sb3("ones2k", [1, 128], F32)
    ychunk = sb3("ychunk", [128, DSSM], F32)
    ynspill = sb3("ynspill", [128, 2 * 32 * 128], BF16)
    rcol = sb3("rcol", [128, 12], F32)
    CB_sb = sb3("CB_sb", [128, 128], F32)
    pcb = ps3("p3cb", [128, 512], F32)
    prb = ps3("p3rb", [128, 512], F32)
    py4 = [ps3(f"p3y{i}", [128, 512], F32) for i in range(4)]
    ptp3 = [ps3(f"p3tp{i}", [128, 512], F32) for i in range(2)]

    @BV
    def _(v):
        v.wait_ge(s_ch.h, s_ch.n)
        v.wait_ge(s_st.h, s_st.n)
        v.wait_ge(s_gp.h, s_gp.n)
        s_v.inc(v.memset(ones2k[:, 0:128], 1.0))
    art_ones = s_v.n

    s_x3 = S("x3")
    s_z3 = S("z3")
    s_c3 = S("c3")
    s_r3 = S("r3")
    s_yn = S("yn")

    fence3 = {s: s.n for s in (s_v, s_sc, s_mm, s_st, s_gp, s_sp, s_ch, s_ld)}

    d_csr = nc.dram_tensor("sp_csr", [NH, T], F32).ap()
    d_bbr = nc.dram_tensor("sp_bbr", [NH, T], F32).ap()

    @BG
    def _(g):
        for s_, n_ in fence3.items():
            g.wait_ge(s_.h, n_)
        dma(g, s_st, d_csr, cs_ext[0:NH, 1:T + 1])
        dma(g, s_st, d_bbr, bb64[0:NH, :])
        g.wait_ge(s_st.h, s_st.n)
        s_gp.inc(g.nop())
    nwtm_done = s_gp.n

    P3 = dict(
        xh3_free=[init_done, init_done], zch_free=init_done, yn_free=0,
        art_free=[0, 0],
        cbrep_free=init_done,
        tp3_free=[0, 0],
        ebuf_free=[0, 0], cb_free=init_done,
        pcb_free=(s_v, init_done), prb_free=(s_v, init_done),
        pool_free=[init_done, init_done],
        ya_gate=init_done,
        stcor_free=init_done,
        x3ld={}, z3ld={}, c3ld={}, intra={}, ydone={},
    )

    def p3_partA(c):
        par = c % 2

        @block.sync
        def _(sy, c=c, par=par, xf=P3["xh3_free"][par]):
            sy.wait_ge(s_v.h, xf)
            dma(sy, s_x3, xh3[par].ap().rearrange("p (ob c2) -> p ob c2", ob=32),
                d_xtm[:, c].rearrange("ob p c2 -> p ob c2"))
        P3["x3ld"][c] = s_x3.n

        @block.tensor
        def _(t, c=c, cbf=P3["cb_free"], pcbf=P3["pcb_free"]):
            t.wait_ge(s_v.h, cbf)
            t.wait_ge(pcbf[0].h, pcbf[1])
            s_mm.inc(t.matmul(pcb[:, 0:128], B_sb[:, c * 128:(c + 1) * 128],
                              C_sb[:, c * 128:(c + 1) * 128], start=True, stop=True))
        cbmm = s_mm.n

        @BV
        def _(v, cbmm=cbmm, crf=P3["cbrep_free"]):
            v.wait_ge(s_mm.h, cbmm)
            v.wait_ge(s_v.h, crf)
            s_v.inc(v.tensor_tensor(out=CB_sb[:], in0=pcb[:, 0:128], in1=mask01[:],
                                    op=OP.mult))
        cb_done = s_v.n
        P3["cb_free"] = cb_done
        P3["pcb_free"] = (s_v, cb_done)

        pend_intra = None
        pend_evac = None
        for grp in range(4):
            epar = grp % 2
            apar = grp % 2
            h0 = grp * 16

            @block.sync
            def _(sy, c=c, grp=grp, artf=P3["art_free"][apar]):
                sy.wait_ge(s_mm.h, artf)
                sy.wait_ge(s_gp.h, nwtm_done)
                dma(sy, s_r3,
                    bbrow[apar].ap().rearrange("one (h s) -> one h s", h=16),
                    d_bbr[grp * 16:(grp + 1) * 16,
                          c * 128:(c + 1) * 128][None, :, :])
                dma(sy, s_r3,
                    csrow[apar].ap().rearrange("one (h s) -> one h s", h=16),
                    d_csr[grp * 16:(grp + 1) * 16,
                          c * 128:(c + 1) * 128][None, :, :])
            artld = s_r3.n

            last_amm = None
            for piece in range(4):
                bank, bkey = ((prb, "prb_free") if piece % 2 == 0
                              else (pcb, "pcb_free"))
                bf = P3[bkey]

                @block.tensor
                def _(t, apar=apar, piece=piece, bank=bank, bf=bf, artld=artld):
                    t.wait_ge(s_r3.h, artld)
                    t.wait_ge(s_v.h, art_ones)
                    t.wait_ge(bf[0].h, bf[1])
                    last = None
                    for hh4 in range(4):
                        hh = piece * 4 + hh4
                        reg = bank[:, hh4 * 128:(hh4 + 1) * 128]
                        t.matmul(reg, bbrow[apar][:, hh * 128:(hh + 1) * 128],
                                 ones2k[:, 0:128],
                                 start=(hh4 == 0), stop=False)
                        last = t.matmul(reg, ones2k[:, 0:128],
                                        csrow[apar][:, hh * 128:(hh + 1) * 128],
                                        start=False, stop=(hh4 == 3))
                    s_mm.inc(last)
                amm = s_mm.n
                last_amm = amm

                @BS
                def _(sc, piece=piece, epar=epar, amm=amm, bank=bank,
                      ef=P3["ebuf_free"][epar]):
                    sc.wait_ge(s_mm.h, amm)
                    if piece == 0:
                        sc.wait_ge(s_mm.h, ef)
                    dst = Ebuf[epar][:, piece * 512:(piece + 1) * 512]
                    sc.activation(out=dst, in_=bank[:], func=AF.Relu,
                                  scale=-1.0, bias=ccol(30.0))
                    s_sc.inc(sc.activation(out=dst, in_=dst, func=AF.Exp,
                                           scale=-1.0, bias=ccol(30.0)))
                P3[bkey] = (s_sc, s_sc.n)
            exp_done = s_sc.n
            P3["art_free"][apar] = last_amm

            @BV
            def _(v, epar=epar, exp_done=exp_done, cb_done=cb_done):
                v.wait_ge(s_sc.h, exp_done)
                v.wait_ge(s_v.h, cb_done)
                s_v.inc(v.tensor_tensor(
                    out=Ebuf[epar].ap().rearrange("p (h t2) -> p h t2", h=16),
                    in0=Ebuf[epar].ap().rearrange("p (h t2) -> p h t2", h=16),
                    in1=CB_sb[:, None, :].broadcast_to([128, 16, 128]),
                    op=OP.mult))
            m_done = s_v.n

            def mk_intra(c=c, par=par, h0=h0, epar=epar, m_done=m_done,
                         ppi=grp % 2):
                pp = [py4[2 * ppi], py4[2 * ppi + 1]]

                def emit():
                    @block.tensor
                    def _(t, x3ld=P3["x3ld"][c], pf=P3["pool_free"][ppi]):
                        t.wait_ge(s_x3.h, x3ld)
                        t.wait_ge(s_v.h, max(m_done, pf))
                        last = None
                        for hh in range(16):
                            h = h0 + hh
                            bk = pp[hh // 8]
                            last = t.matmul(bk[:, (hh % 8) * HD:((hh % 8) + 1) * HD],
                                            Ebuf[epar][:, hh * 128:(hh + 1) * 128],
                                            xh3[par][:, h * HD:(h + 1) * HD],
                                            start=(hh % 8 == 0), stop=(hh % 8 == 7))
                        s_mm.inc(last)
                    imm = s_mm.n
                    P3["ebuf_free"][epar] = imm

                    def emit_evac():
                        @BV
                        def _(v, gate=P3["ya_gate"]):
                            v.wait_ge(s_mm.h, imm)
                            v.wait_ge(s_v.h, gate)
                            ins = None
                            for bk_i in range(2):
                                hh0 = h0 + bk_i * 8
                                sl = slice(hh0 * HD, (hh0 + 8) * HD)
                                v.tensor_tensor(
                                    out=ychunk.ap()[:, sl].rearrange("p (h q) -> p h q", h=8),
                                    in0=xh3[par].ap()[:, sl].rearrange("p (h q) -> p h q", h=8),
                                    in1=bc64(drep[:, hh0:hh0 + 8], 8), op=OP.mult)
                                ins = v.tensor_tensor(out=ychunk[:, sl], in0=ychunk[:, sl],
                                                      in1=pp[bk_i][:], op=OP.add)
                            s_v.inc(ins)
                        P3["pool_free"][ppi] = s_v.n
                    return emit_evac
                return emit

            this_intra = mk_intra()
            if pend_intra is not None:
                new_evac = pend_intra()
                if pend_evac is not None:
                    pend_evac()
                pend_evac = new_evac
            pend_intra = this_intra
        P3["cbrep_free"] = m_done
        new_evac = pend_intra()
        if pend_evac is not None:
            pend_evac()
        new_evac()
        P3["intra"][c] = s_v.n

    def p3_partB(c, init_done):
        par = c % 2

        @block.sync
        def _(sy, c=c, scf=P3["stcor_free"]):
            sy.wait_ge(s_v.h, scf)
            dma(sy, s_c3, stcor[:], d_chain[c])
        c3ld = s_c3.n

        @BV
        def _(v, c=c, c3ld=c3ld, init_done=init_done, zf=P3["zch_free"]):
            v.wait_ge(s_c3.h, c3ld)
            v.wait_ge(s_v.h, max(init_done, zf))
            v.tensor_tensor(
                out=zch[0].ap().rearrange("p (h q) -> p h q", h=NH),
                in0=init_sb.ap().rearrange("p (h q) -> p h q", h=NH),
                in1=bc64(rels_rep[:, c * NH:(c + 1) * NH], NH), op=OP.mult)
            s_v.inc(v.tensor_tensor(out=stcor[:], in0=stcor[:], in1=zch[0][:],
                                    op=OP.add))
        stcor_done = s_v.n

        @block.sync
        def _(sy, c=c, scd=stcor_done):
            sy.wait_ge(s_v.h, scd)
            dma(sy, s_z3, zch[0].ap().rearrange("p (ob c2) -> p ob c2", ob=32),
                d_ztm[:, c].rearrange("ob p c2 -> p ob c2"))
        z3ld = s_z3.n

        intra_done = P3["intra"][c]
        for grp in range(4):
            h0 = grp * 16
            pp = [py4[2 * (grp % 2)], py4[2 * (grp % 2) + 1]]

            @block.tensor
            def _(t, c=c, h0=h0, stcor_done=stcor_done, pp=pp,
                  pf=P3["pool_free"][grp % 2]):
                t.wait_ge(s_v.h, max(stcor_done, pf))
                last = None
                for hh in range(16):
                    h = h0 + hh
                    bk = pp[hh // 8]
                    last = t.matmul(bk[:, (hh % 8) * HD:((hh % 8) + 1) * HD],
                                    C_sb[:, c * 128:(c + 1) * 128],
                                    stcor[:, h * HD:(h + 1) * HD],
                                    start=(hh % 8 == 0), stop=(hh % 8 == 7))
                s_mm.inc(last)
            bmm = s_mm.n

            @BV
            def _(v, c=c, h0=h0, bmm=bmm, intra_done=intra_done, pp=pp):
                v.wait_ge(s_mm.h, bmm)
                v.wait_ge(s_v.h, intra_done)
                ins = None
                for bk_i in range(2):
                    hh0 = h0 + bk_i * 8
                    sl = slice(hh0 * HD, (hh0 + 8) * HD)
                    v.tensor_tensor(
                        out=pp[bk_i].ap().rearrange("p (h q) -> p h q", h=8),
                        in0=pp[bk_i].ap().rearrange("p (h q) -> p h q", h=8),
                        in1=bc64(exprelT[:, c * NH + hh0:c * NH + hh0 + 8], 8),
                        op=OP.mult)
                    ins = v.tensor_tensor(out=ychunk[:, sl], in0=ychunk[:, sl],
                                          in1=pp[bk_i][:], op=OP.add)
                s_v.inc(ins)
            P3["pool_free"][grp % 2] = s_v.n
        P3["stcor_free"] = s_v.n
        y_done = s_v.n

        @BV
        def _(v, c=c, y_done=y_done, z3ld=z3ld):
            v.wait_ge(s_z3.h, z3ld)
            s_v.inc(v.tensor_tensor(out=ychunk[:], in0=ychunk[:], in1=zch[0][:],
                                    op=OP.mult))
        yg_done = s_v.n
        P3["zch_free"] = yg_done

        @BS
        def _(sc, c=c, yg_done=yg_done):
            sc.wait_ge(s_v.h, yg_done)
            ins = None
            for pp in range(8):
                ins = sc.activation(out=ptp3[0][:, 0:512],
                                    in_=ychunk[:, pp * 512:(pp + 1) * 512],
                                    func=AF.Square, accum_out=rcol[:, pp:pp + 1])
            s_sc.inc(ins)
        sq_done = s_sc.n

        @BV
        def _(v, sq_done=sq_done):
            v.wait_ge(s_sc.h, sq_done)
            s_v.inc(v.tensor_reduce(out=rcol[:, 8:9], in_=rcol[:, 0:8], axis=AX,
                                    op=OP.add))
        rsum_done = s_v.n

        @BS
        def _(sc, rsum_done=rsum_done):
            sc.wait_ge(s_v.h, rsum_done)
            sc.activation(out=rcol[:, 9:10], in_=rcol[:, 8:9], func=AF.Ln,
                          bias=ccol(EPS), scale=1.0 / DSSM)
            s_sc.inc(sc.activation(out=rcol[:, 2:3], in_=rcol[:, 9:10], func=AF.Exp,
                                   scale=-0.5))
        rstd_done = s_sc.n

        @BV
        def _(v, c=c, par=par, rstd_done=rstd_done):
            v.wait_ge(s_sc.h, rstd_done)
            s_v.inc(v.tensor_scalar(out=xh3[par][:], in0=ychunk[:],
                                    scalar1=rcol[:, 2:3], scalar2=None, op0=OP.mult))
        yn_done = s_v.n
        P3["ya_gate"] = yn_done

        def p3_tp(c=c, par=par, yn_done=yn_done):
            for ib in range(32):
                tb3 = ptp3[ib % 2]
                at_v = P3["tp3_free"][ib % 2]

                @block.tensor
                def _(t, c=c, par=par, ib=ib, tb3=tb3, yn_done=yn_done, at_v=at_v):
                    t.wait_ge(s_v.h, max(yn_done, at_v))
                    s_mm.inc(t.transpose(tb3[:, 0:128],
                                         xh3[par][:, ib * 128:(ib + 1) * 128],
                                         ident[:]))
                tpd = s_mm.n

                @BS
                def _(sc, ib=ib, tb3=tb3, tpd=tpd, ynf=P3["yn_free"]):
                    sc.wait_ge(s_mm.h, tpd)
                    if ib == 0:
                        sc.wait_ge(s_yn.h, ynf)
                    s_sc.inc(sc.activation(out=ynspill[:, ib * 128:(ib + 1) * 128],
                                           in_=tb3[:, 0:128], func=AF.Copy,
                                           scale=nw_sb[:, ib:ib + 1]))
                hid = s_sc.n

                @BV
                def _(v, ib=ib, tb3=tb3, hid=hid):
                    v.wait_ge(s_sc.h, hid)
                    s_v.inc(v.scalar_tensor_tensor(
                        out=ynspill[:, 4096 + ib * 128:4096 + (ib + 1) * 128],
                        in0=tb3[:, 0:128], scalar=nw_sb[:, ib:ib + 1],
                        in1=ynspill[:, ib * 128:(ib + 1) * 128],
                        op0=OP.mult, op1=OP.subtract))
                P3["tp3_free"][ib % 2] = s_v.n
            sp_ready = s_v.n

            @BG
            def _(g, c=c, sp_ready=sp_ready):
                g.wait_ge(s_v.h, sp_ready)
                dma(g, s_yn, d_ynsp[c].rearrange("p two ib c2 -> p (two ib c2)"),
                    ynspill[:])
            P3["yn_free"] = s_yn.n
            P3["xh3_free"][par] = sp_ready
        return p3_tp

    @block.sync
    def _(sy):
        for s_, n_ in fence3.items():
            sy.wait_ge(s_.h, n_)
        sy.wait_ge(s_gp.h, s_gp.n)

    tp_thunk = None
    for c in range(NCH):
        p3_partA(c)
        if tp_thunk is not None:
            tp_thunk()
        tp_thunk = p3_partB(c, init_done)
    tp_thunk()

    for cm in reversed(p3_ctx):
        cm.__exit__(None, None, None)
    for cm in reversed(scan_cms):
        cm.__exit__(None, None, None)

    if debug:
        d_ynd = dbg("dbg_yn", [NCH, 128, 2, 32, 128], BF16)

        @BG
        def _(g):
            g.wait_ge(s_yn.h, s_yn.n)
            dma(g, s_st, d_ynd, d_ynsp)

    p4_ctx = []

    def sb4(name, shape, dtype):
        cm = nc.sbuf_tensor(name, list(shape), dtype)
        t = cm.__enter__()
        p4_ctx.append(cm)
        return t

    def ps4(name, shape, dtype=F32):
        cm = nc.psum_tensor(name, list(shape), dtype)
        t = cm.__enter__()
        p4_ctx.append(cm)
        return t

    ynres = sb4("ynres", [128, 2, 32, T], BF16)
    wraw4 = [sb4(f"wraw4_{i}", [128, 32 * 128], F32) for i in range(2)]
    wh4 = [sb4(f"wh4_{i}", [128, 32 * 128], BF16) for i in range(2)]
    wl4 = [sb4(f"wl4_{i}", [128, 32 * 128], BF16) for i in range(2)]
    sbuf4 = [sb4(f"sbuf4_{i}", [128, T], F32) for i in range(2)]
    pk4 = [ps4(f"p4b{i}", [128, 512], F32) for i in range(4)]

    s_w4 = S("w4")
    s_s4 = S("s4")

    fence4 = {s: s.n for s in (s_v, s_sc, s_mm, s_st, s_gp, s_sp, s_ch, s_yn, s_ld)}

    s_y4 = [S("y4a"), S("y4b")]

    @block.sync
    def _(sy):
        for s_, n_ in fence4.items():
            sy.wait_ge(s_.h, n_)
        for c in range(NCH):
            dma(sy, s_y4[c % 2], ynres[:, :, :, c * 128:(c + 1) * 128], d_ynsp[c])

    yn_ld4 = (s_y4[0].n, s_y4[1].n)

    wraw4_free = [0, 0]
    whl4_free = [0, 0]
    pset4_free = [0, 0]
    sbuf4_free = [0, 0]
    split4 = {}

    def p4_load_split(ob):
        par = ob % 2

        @block.sync
        def _(sy, ob=ob, par=par, wfree=wraw4_free[par]):
            sy.wait_ge(s_v.h, wfree)
            dma(sy, s_w4, wraw4[par][:], d_wout[ob].rearrange("p ib o -> p (ib o)"))
        wld = s_w4.n

        @BS
        def _(sc, par=par, wld=wld, wf=whl4_free[par]):
            sc.wait_ge(s_w4.h, wld)
            sc.wait_ge(s_mm.h, wf)
            s_sc.inc(sc.copy(out=wh4[par][:], in_=wraw4[par][:]))
        whd = s_sc.n

        @BV
        def _(v, par=par, wld=wld, whd=whd, wf=whl4_free[par]):
            v.wait_ge(s_w4.h, wld)
            v.wait_ge(s_sc.h, whd)
            v.wait_ge(s_mm.h, wf)
            s_v.inc(v.tensor_tensor(out=wl4[par][:], in0=wraw4[par][:], in1=wh4[par][:],
                                    op=OP.subtract))
        wraw4_free[par] = s_v.n
        split4[ob] = (whd, s_v.n)

    def p4_mm_evac(ob):
        par = ob % 2
        need_ps = pset4_free[par]
        whd, wsplit = split4[ob]

        @block.tensor
        def _(t, ob=ob, par=par, whd=whd, wsplit=wsplit, need_ps=need_ps, yn_ld4=yn_ld4):
            t.wait_ge(s_y4[0].h, yn_ld4[0])
            t.wait_ge(s_y4[1].h, yn_ld4[1])
            t.wait_ge(s_sc.h, whd)
            t.wait_ge(s_v.h, max(wsplit, need_ps))
            p0_, p1_ = pk4[2 * par], pk4[2 * par + 1]
            last = None
            for ib in range(32):
                wha = wh4[par][:, ib * 128:(ib + 1) * 128]
                wla = wl4[par][:, ib * 128:(ib + 1) * 128]
                hi0 = ynres[:, 0, ib, 0:512]
                hi1 = ynres[:, 0, ib, 512:1024]
                lo0 = ynres[:, 1, ib, 0:512]
                lo1 = ynres[:, 1, ib, 512:1024]
                t.matmul(p0_[:], wha, hi0, start=(ib == 0), stop=False)
                t.matmul(p1_[:], wha, hi1, start=(ib == 0), stop=False)
                t.matmul(p0_[:], wha, lo0, start=False, stop=False)
                t.matmul(p1_[:], wha, lo1, start=False, stop=False)
                t.matmul(p0_[:], wla, hi0, start=False, stop=(ib == 31))
                last = t.matmul(p1_[:], wla, hi1, start=False, stop=(ib == 31))
            s_mm.inc(last)
        mm4 = s_mm.n
        whl4_free[par] = s_mm.n

        @BV
        def _(v, ob=ob, par=par, mm4=mm4, sfree=sbuf4_free[par]):
            v.wait_ge(s_mm.h, mm4)
            v.wait_ge(s_s4.h, sfree)
            v.tensor_copy(out=sbuf4[par][:, 0:512], in_=pk4[2 * par][:])
            s_v.inc(v.tensor_copy(out=sbuf4[par][:, 512:1024], in_=pk4[2 * par + 1][:]))
        ev4 = s_v.n
        pset4_free[par] = ev4

        @BG
        def _(g, ob=ob, par=par, ev4=ev4):
            g.wait_ge(s_v.h, ev4)
            dma(g, s_s4, d_ssp[ob], sbuf4[par][:])
        sbuf4_free[par] = s_s4.n

    for ob in range(16):
        p4_load_split(ob)
        if ob >= 1:
            p4_mm_evac(ob - 1)
    p4_mm_evac(15)

    for cm in reversed(p4_ctx):
        cm.__exit__(None, None, None)

    if debug:
        d_sd = dbg("dbg_s", [16, 128, T])

        @BG
        def _(g):
            g.wait_ge(s_s4.h, s_s4.n)
            dma(g, s_st, d_sd, d_ssp)

    p5o_ctx = []

    def sb5o(name, shape, dtype):
        cm = nc.sbuf_tensor(name, list(shape), dtype)
        t = cm.__enter__()
        p5o_ctx.append(cm)
        return t

    s2_b = sb5o("s2_b", [128, T], F32)
    o3_b = sb5o("o3_b", [128, T], F32)
    gneg = sb5o("gneg", [128, 4], F32)
    d_row = nc.dram_tensor("sp_row", [128, 8], F32).ap()

    @BV
    def _(v, gsc_done=gsc_done):
        v.wait_ge(s_gp.h, gsc_done)
        s_v.inc(v.tensor_scalar(out=gneg[:, 0:3], in0=gsc[:, 3:6], scalar1=-1.0,
                                scalar2=None, op0=OP.mult))
    gneg_done = s_v.n

    p5a_ctx = []

    def sb5a(name, shape, dtype):
        cm = nc.sbuf_tensor(name, list(shape), dtype)
        t = cm.__enter__()
        p5a_ctx.append(cm)
        return t

    p5a_ps_ctx = []

    def ps5a(name, shape, dtype=F32):
        cm = nc.psum_tensor(name, list(shape), dtype)
        t = cm.__enter__()
        p5a_ps_ctx.append(cm)
        return t

    gxg_b = sb5a("gxg_b", [128, T], F32)
    gxv_b = sb5a("gxv_b", [128, T], F32)
    rowbuf = sb5a("rowbuf", [1, T], F32)
    gx2c = sb5a("gx2c", [128, 8], F32)
    qxn5 = sb5a("qxn5", [128, NIB_D, T], BF16)
    wgr = [sb5a(f"wgr{i}", [128, 2048], F32) for i in range(2)]
    wvr = [sb5a(f"wvr{i}", [128, 2048], F32) for i in range(2)]
    tg = [sb5a(f"tg{i}", [128, 2048], BF16) for i in range(2)]
    tv = [sb5a(f"tv{i}", [128, 2048], BF16) for i in range(2)]
    tscr5 = sb5a("tscr5", [128, 2048], F32)
    pregate = sb5a("pregate", [128, T], F32)
    gatebuf = sb5a("gatebuf", [128, T], F32)
    vbuf = [sb5a(f"vbuf{i}", [128, T], F32) for i in range(2)]
    rmax = sb5a("rmax", [128, T], F32)
    rmin = sb5a("rmin", [128, T], F32)
    pg5 = [ps5a(f"p5g{i}", [128, 512], F32) for i in range(4)]
    pv5 = [ps5a(f"p5v{i}", [128, 512], F32) for i in range(4)]

    s_wg = S("wg5")
    s_wv = S("wv5")
    s_q5 = S("q5")
    s_v5 = S("v5")

    fence5 = {s: s.n for s in (s_v, s_sc, s_mm, s_st, s_gp, s_sp, s_s4)}

    @block.sync
    def _(sy):
        for s_, n_ in fence5.items():
            sy.wait_ge(s_.h, n_)
        dma(sy, s_q5, qxn5[:], d_qxn)
    q5ld = s_q5.n

    @BG
    def _(g, q5ld=q5ld):
        g.wait_ge(s_q5.h, q5ld)
        dma(g, s_st, d_row, gx1_cols[:, 0:NT], allow_slow_non_contiguous=True)
        g.wait_ge(s_st.h, s_st.n)
        dma(g, s_st, rowbuf.ap().rearrange("one (tt p) -> one tt p", tt=NT),
            d_row.rearrange("p tt -> tt p")[None, :, :], allow_slow_non_contiguous=True)
        g.wait_ge(s_st.h, s_st.n)
        s_gp.inc(g.nop())
    gx1row_done = s_gp.n

    @BV
    def _(v, gx1row_done=gx1row_done):
        v.wait_ge(s_gp.h, gx1row_done)
        v.tensor_scalar(out=gxg_b[0:1, :], in0=rowbuf[0:1, :],
                        scalar1=gsc[0:1, 0:1], scalar2=1.0 / 127.0,
                        op0=OP.mult, op1=OP.mult)
        s_v.inc(v.tensor_scalar(out=gxv_b[0:1, :], in0=rowbuf[0:1, :],
                                scalar1=gsc[0:1, 1:2], scalar2=1.0 / 127.0,
                                op0=OP.mult, op1=OP.mult))
    rows_done = s_v.n

    @BG
    def _(g, rows_done=rows_done):
        g.wait_ge(s_v.h, rows_done)
        g.partition_broadcast(out_ap=gxg_b[:], in_ap=gxg_b[0:1, :])
        s_gp.inc(g.partition_broadcast(out_ap=gxv_b[:], in_ap=gxv_b[0:1, :]))
    gx_b_done = s_gp.n

    @BV
    def _(v):
        v.memset(rmax[:], -1e30)
        s_v.inc(v.memset(rmin[:], 1e30))
    rinit = s_v.n

    wg_free = [0, 0]
    tern_free = [0, 0]
    ps5_free = [0, 0]
    vb_free = [0, 0]
    tern5 = {}

    def p5a_load_tern(hb):
        par = hb % 2

        @block.sync
        def _(sy, hb=hb, par=par, wf=wg_free[par]):
            sy.wait_ge(s_v.h, wf)
            dma(sy, s_wg, wgr[par][:], d_wg[hb].rearrange("p ib o -> p (ib o)"))
            dma(sy, s_wv, wvr[par][:], d_wv[hb].rearrange("p ib o -> p (ib o)"))
        wgld, wvld = s_wg.n, s_wv.n

        @BV
        def _(v, hb=hb, par=par, wgld=wgld, wvld=wvld, tf=tern_free[par],
              gneg_done=gneg_done):
            v.wait_ge(s_wg.h, wgld)
            v.wait_ge(s_wv.h, wvld)
            v.wait_ge(s_mm.h, tf)
            v.wait_ge(s_v.h, gneg_done)
            v.tensor_scalar(out=tscr5[:], in0=wgr[par][:], scalar1=gneg[:, 0:1],
                            scalar2=0.5, op0=OP.mult, op1=OP.is_ge)
            v.tensor_scalar(out=wgr[par][:], in0=wgr[par][:], scalar1=gsc[:, 3:4],
                            scalar2=0.5, op0=OP.mult, op1=OP.is_ge)
            v.tensor_tensor(out=tg[par][:], in0=tscr5[:], in1=wgr[par][:], op=OP.subtract)
            v.tensor_scalar(out=tscr5[:], in0=wvr[par][:], scalar1=gneg[:, 1:2],
                            scalar2=0.5, op0=OP.mult, op1=OP.is_ge)
            v.tensor_scalar(out=wvr[par][:], in0=wvr[par][:], scalar1=gsc[:, 4:5],
                            scalar2=0.5, op0=OP.mult, op1=OP.is_ge)
            s_v.inc(v.tensor_tensor(out=tv[par][:], in0=tscr5[:], in1=wvr[par][:],
                                    op=OP.subtract))
        wg_free[par] = s_v.n
        tern5[hb] = s_v.n

    def p5a_mm_evac(hb):
        par = hb % 2
        tern_done = tern5[hb]
        pgA, pgB = pg5[2 * par], pg5[2 * par + 1]
        pvA, pvB = pv5[2 * par], pv5[2 * par + 1]
        need_ps5 = ps5_free[par]

        @block.tensor
        def _(t, hb=hb, par=par, tern_done=tern_done, q5ld=q5ld, need_ps5=need_ps5,
              pgA=pgA, pgB=pgB, pvA=pvA, pvB=pvB):
            t.wait_ge(s_q5.h, q5ld)
            t.wait_ge(s_v.h, max(tern_done, need_ps5))
            last = None
            for ib in range(NIB_D):
                tga = tg[par][:, ib * 128:(ib + 1) * 128]
                tva = tv[par][:, ib * 128:(ib + 1) * 128]
                q0 = qxn5[:, ib, 0:512]
                q1 = qxn5[:, ib, 512:1024]
                t.matmul(pgA[:], tga, q0, start=(ib == 0), stop=(ib == 15))
                t.matmul(pgB[:], tga, q1, start=(ib == 0), stop=(ib == 15))
                t.matmul(pvA[:], tva, q0, start=(ib == 0), stop=(ib == 15))
                last = t.matmul(pvB[:], tva, q1, start=(ib == 0), stop=(ib == 15))
            s_mm.inc(last)
        mm5 = s_mm.n
        tern_free[par] = mm5

        @BV
        def _(v, hb=hb, mm5=mm5, gx_b_done=gx_b_done, pgA=pgA, pgB=pgB):
            v.wait_ge(s_mm.h, mm5)
            v.wait_ge(s_gp.h, gx_b_done)
            v.tensor_tensor(out=pregate[:, 0:512], in0=pgA[:], in1=gxg_b[:, 0:512],
                            op=OP.mult)
            s_v.inc(v.tensor_tensor(out=pregate[:, 512:1024], in0=pgB[:],
                                    in1=gxg_b[:, 512:1024], op=OP.mult))
        pre_done = s_v.n

        @BS
        def _(sc, hb=hb, pre_done=pre_done):
            sc.wait_ge(s_v.h, pre_done)
            s_sc.inc(sc.activation(out=gatebuf[:], in_=pregate[:], func=AF.Silu,
                                   bias=gb_sb[:, hb:hb + 1]))
        gate_done = s_sc.n

        @BV
        def _(v, hb=hb, par=par, gate_done=gate_done, vfree=vb_free[par],
              pvA=pvA, pvB=pvB):
            v.wait_ge(s_sc.h, gate_done)
            v.wait_ge(s_v5.h, vfree)
            v.tensor_tensor(out=pregate[:, 0:512], in0=pvA[:], in1=gxv_b[:, 0:512],
                            op=OP.mult)
            v.tensor_tensor(out=pregate[:, 512:1024], in0=pvB[:],
                            in1=gxv_b[:, 512:1024], op=OP.mult)
            v.scalar_tensor_tensor(out=vbuf[par][:], in0=pregate[:],
                                   scalar=vb_sb[:, hb:hb + 1], in1=gatebuf[:],
                                   op0=OP.add, op1=OP.mult)
            v.tensor_tensor(out=rmax[:], in0=rmax[:], in1=vbuf[par][:], op=OP.max)
            s_v.inc(v.tensor_tensor(out=rmin[:], in0=rmin[:], in1=vbuf[par][:], op=OP.min))
        v_done = s_v.n
        ps5_free[par] = v_done

        @BG
        def _(g, hb=hb, par=par, v_done=v_done):
            g.wait_ge(s_v.h, v_done)
            dma(g, s_v5, d_vsp[hb], vbuf[par][:])
        vb_free[par] = s_v5.n

    for hb in range(64):
        p5a_load_tern(hb)
        if hb >= 1:
            p5a_mm_evac(hb - 1)
    p5a_mm_evac(63)
    va_done = s_v.n

    @BV
    def _(v, va_done=va_done):
        v.wait_ge(s_v.h, va_done)
        v.tensor_scalar(out=rmin[:], in0=rmin[:], scalar1=-1.0, scalar2=None,
                        op0=OP.mult)
        s_v.inc(v.tensor_tensor(out=rmax[:], in0=rmax[:], in1=rmin[:], op=OP.max))
    gmx_done = s_v.n

    for tt in range(NT):
        at_v5 = s_v.n

        @block.tensor
        def _(t, tt=tt, gmx_done=gmx_done, at_v5=at_v5):
            t.wait_ge(s_v.h, max(gmx_done, at_v5))
            s_mm.inc(t.transpose(pg5[0][:, 0:128], rmax[:, tt * 128:(tt + 1) * 128],
                                 ident[:]))
        gt_ = s_mm.n

        @BV
        def _(v, tt=tt, gt_=gt_):
            v.wait_ge(s_mm.h, gt_)
            v.tensor_reduce(out=gx2c[:, tt:tt + 1], in_=pg5[0][:, 0:128], axis=AX,
                            op=OP.max)
            s_v.inc(v.tensor_scalar(out=gx2c[:, tt:tt + 1], in0=gx2c[:, tt:tt + 1],
                                    scalar1=1e-5, scalar2=None, op0=OP.max))
    gx2_done = s_v.n

    @BG
    def _(g, gx2_done=gx2_done):
        g.wait_ge(s_v.h, gx2_done)
        dma(g, s_st, d_row, gx2c[:, 0:8], allow_slow_non_contiguous=True)
        g.wait_ge(s_st.h, s_st.n)
        dma(g, s_st, rowbuf.ap().rearrange("one (tt p) -> one tt p", tt=NT),
            d_row.rearrange("p tt -> tt p")[None, :, :], allow_slow_non_contiguous=True)
        g.wait_ge(s_st.h, s_st.n)
        s_gp.inc(g.nop())
    gx2row_done = s_gp.n

    @BS
    def _(sc, gx2row_done=gx2row_done):
        sc.wait_ge(s_gp.h, gx2row_done)
        s_sc.inc(sc.activation(out=s2_b[0:1, :], in_=rowbuf[0:1, :], func=AF.Ln))
    lngx2 = s_sc.n

    @BS
    def _(sc, lngx2=lngx2):
        sc.wait_ge(s_sc.h, lngx2)
        s_sc.inc(sc.activation(out=s2_b[0:1, :], in_=s2_b[0:1, :], func=AF.Exp,
                               scale=-1.0, bias=ccol(LN127)[0:1, :]))
    s2row = s_sc.n

    @BV
    def _(v, s2row=s2row):
        v.wait_ge(s_sc.h, s2row)
        s_v.inc(v.tensor_scalar(out=o3_b[0:1, :], in0=rowbuf[0:1, :],
                                scalar1=gsc[0:1, 2:3], scalar2=1.0 / 127.0,
                                op0=OP.mult, op1=OP.mult))
    o3row = s_v.n

    @BG
    def _(g, o3row=o3row):
        g.wait_ge(s_v.h, o3row)
        g.partition_broadcast(out_ap=s2_b[:], in_ap=s2_b[0:1, :])
        s_gp.inc(g.partition_broadcast(out_ap=o3_b[:], in_ap=o3_b[0:1, :]))
    s2o3_done = s_gp.n

    for cm in reversed(p5a_ps_ctx):
        cm.__exit__(None, None, None)
    for cm in reversed(p5a_ctx):
        cm.__exit__(None, None, None)

    p5b_ctx = []

    def sb5b(name, shape, dtype):
        cm = nc.sbuf_tensor(name, list(shape), dtype)
        t = cm.__enter__()
        p5b_ctx.append(cm)
        return t

    def ps5b(name, shape, dtype=F32):
        cm = nc.psum_tensor(name, list(shape), dtype)
        t = cm.__enter__()
        p5b_ctx.append(cm)
        return t

    vqres = sb5b("vqres", [128, 64, T], BF16)
    vq_cms = [nc.sbuf_tensor(f"vrel{i}", [128, T], F32) for i in range(4)] + [
              nc.sbuf_tensor("qtmp0", [128, T], F32),
              nc.sbuf_tensor("qtmp1", [128, T], F32),
              nc.sbuf_tensor("qint0", [128, T], I32),
              nc.sbuf_tensor("qint1", [128, T], I32)]
    vrel = [cm.__enter__() for cm in vq_cms[:4]]
    qtmp = [vq_cms[4].__enter__(), vq_cms[5].__enter__()]
    qint = [vq_cms[6].__enter__(), vq_cms[7].__enter__()]
    s_vr = S("vr5")

    fence5b = {s: s.n for s in (s_v, s_sc, s_mm, s_st, s_gp, s_v5)}
    vrel_free = [0, 0, 0, 0]
    q_free = [0, 0]
    pend_corr = None

    for hb in range(64):
        par = hb % 4
        qpar = hb % 2

        @block.sync
        def _(sy, hb=hb, par=par, vfree=vrel_free[par]):
            if hb == 0:
                for s_, n_ in fence5b.items():
                    sy.wait_ge(s_.h, n_)
            sy.wait_ge(s_v.h, vfree)
            dma(sy, s_vr, vrel[par][:], d_vsp[hb])
        vld = s_vr.n

        @BV
        def _(v, hb=hb, par=par, qpar=qpar, vld=vld, s2o3_done=s2o3_done,
              qf=q_free[qpar]):
            v.wait_ge(s_vr.h, vld)
            v.wait_ge(s_gp.h, s2o3_done)
            v.wait_ge(s_v.h, qf)
            s_v.inc(v.tensor_tensor(out=qtmp[qpar][:], in0=vrel[par][:], in1=s2_b[:],
                                    op=OP.mult))
        qm_done = s_v.n

        @BS
        def _(sc, hb=hb, qpar=qpar, qm_done=qm_done):
            sc.wait_ge(s_v.h, qm_done)
            s_sc.inc(sc.activation(out=qint[qpar][:], in_=qtmp[qpar][:],
                                   func=AF.Identity, bias=ccol(0.5), scale=1.0))
        qi_done = s_sc.n

        def mk_corr(hb=hb, par=par, qpar=qpar, qi_done=qi_done):
            def emit():
                @BV
                def _(v):
                    v.wait_ge(s_sc.h, qi_done)
                    v.scalar_tensor_tensor(out=vrel[par][:], in0=qint[qpar][:],
                                           scalar=-0.5, in1=qtmp[qpar][:],
                                           op0=OP.add, op1=OP.is_gt)
                    s_v.inc(v.tensor_tensor(out=vqres[:, hb, :], in0=vrel[par][:],
                                            in1=qint[qpar][:], op=OP.subtract))
                vrel_free[par] = s_v.n
                q_free[qpar] = s_v.n
            return emit

        if pend_corr is not None:
            pend_corr()
        pend_corr = mk_corr()
    pend_corr()
    vq_done = s_v.n

    for cm in reversed(vq_cms):
        cm.__exit__(None, None, None)

    wor = [sb5b(f"wor{i}", [128, 2048], F32) for i in range(2)]
    to3 = [sb5b(f"to3h{i}", [128, 4096], BF16) for i in range(2)]
    tscr6 = sb5b("tscr6", [128, 2048], F32)
    gb5 = [sb5b(f"gb5_{i}", [128, T], F32) for i in range(2)]
    preo = sb5b("preo", [128, T], F32)
    sres = [sb5b("sres0", [128, T], F32)]
    sres = [sres[0], sres[0]]
    ub2 = [sb5b(f"ub2_{i}", [128, NT * 128], F32) for i in range(2)]
    po5 = [ps5b(f"p5o{i}", [128, 512], F32) for i in range(4)]
    pt5 = [ps5b(f"p5t{i}", [128, 512], F32) for i in range(2)]

    s_wo = S("wo5")
    s_sr = S("sr5")
    s_ub = S("ub5")
    s_out = S("out5")

    wor_free = [vq_done, vq_done]
    to3_free = [0, 0]
    po5_free = [vq_done, vq_done]
    sres_free = [vq_done, vq_done]
    ub_free = [vq_done, vq_done]
    tp5_free = [vq_done, vq_done]

    def p5b_main(ob):
        par = ob % 2

        @block.sync
        def _(sy, ob=ob, par=par, sfree=sres_free[par], ubf=ub_free[par]):
            sy.wait_ge(s_v.h, sfree)
            dma(sy, s_sr, sres[par][:], d_ssp[ob])
            if ob < 2:
                sy.wait_ge(s_v.h, ubf)
            else:
                sy.wait_ge(s_out.h, ubf)
            dma(sy, s_ub, ub2[par].ap().rearrange("p (tt o) -> p tt o", tt=NT),
                d_u.rearrange("(tt p) dd -> p tt dd", p=128)[:, :, ob * 128:(ob + 1) * 128])
        srld, ubld = s_sr.n, s_ub.n

        poA, poB = po5[2 * par], po5[2 * par + 1]
        need_po = po5_free[par]

        for half in range(2):
            hpar = half
            tern6_done = None
            for qq in range(2):
                wpar = (half * 2 + qq) % 2

                @block.sync
                def _(sy, ob=ob, half=half, qq=qq, wpar=wpar, wfree=wor_free[wpar]):
                    sy.wait_ge(s_v.h, wfree)
                    dma(sy, s_wo, wor[wpar][:],
                        d_wo[ob].rearrange("p ib o -> p (ib o)")
                        [:, half * 4096 + qq * 2048:half * 4096 + (qq + 1) * 2048])
                wold = s_wo.n

                @BV
                def _(v, qq=qq, hpar=hpar, wpar=wpar, wold=wold,
                      tfree=to3_free[hpar], gneg_done=gneg_done):
                    v.wait_ge(s_wo.h, wold)
                    v.wait_ge(s_v.h, gneg_done)
                    if qq == 0:
                        v.wait_ge(s_mm.h, tfree)
                    sl = slice(qq * 2048, (qq + 1) * 2048)
                    v.tensor_scalar(out=tscr6[:], in0=wor[wpar][:], scalar1=gneg[:, 2:3],
                                    scalar2=0.5, op0=OP.mult, op1=OP.is_ge)
                    v.tensor_scalar(out=wor[wpar][:], in0=wor[wpar][:], scalar1=gsc[:, 5:6],
                                    scalar2=0.5, op0=OP.mult, op1=OP.is_ge)
                    s_v.inc(v.tensor_tensor(out=to3[hpar][:, sl], in0=tscr6[:],
                                            in1=wor[wpar][:], op=OP.subtract))
                wor_free[wpar] = s_v.n
            tern6_done = s_v.n

            @block.tensor
            def _(t, ob=ob, par=par, half=half, hpar=hpar, tern6_done=tern6_done,
                  vq_done=vq_done, need_po=need_po, poA=poA, poB=poB):
                t.wait_ge(s_v.h, max(tern6_done, vq_done, need_po))
                last = None
                for ib2 in range(32):
                    ib = half * 32 + ib2
                    ta = to3[hpar][:, ib2 * 128:(ib2 + 1) * 128]
                    t.matmul(poA[:], ta, vqres[:, ib, 0:512],
                             start=(ib == 0), stop=(ib == 63))
                    last = t.matmul(poB[:], ta, vqres[:, ib, 512:1024],
                                    start=(ib == 0), stop=(ib == 63))
                s_mm.inc(last)
            to3_free[hpar] = s_mm.n
        mmo = s_mm.n

        @BV
        def _(v, ob=ob, mmo=mmo, s2o3_done=s2o3_done, poA=poA, poB=poB):
            v.wait_ge(s_mm.h, mmo)
            v.wait_ge(s_gp.h, s2o3_done)
            v.tensor_tensor(out=preo[:, 0:512], in0=poA[:], in1=o3_b[:, 0:512],
                            op=OP.mult)
            s_v.inc(v.tensor_tensor(out=preo[:, 512:1024], in0=poB[:],
                                    in1=o3_b[:, 512:1024], op=OP.mult))
        preo_done = s_v.n
        po5_free[par] = preo_done

        @BS
        def _(sc, ob=ob, par=par, preo_done=preo_done):
            sc.wait_ge(s_v.h, preo_done)
            s_sc.inc(sc.activation(out=gb5[par][:], in_=preo[:], func=AF.Silu,
                                   bias=obias_sb[:, ob:ob + 1]))
        g_done = s_sc.n

        @BV
        def _(v, ob=ob, par=par, g_done=g_done, srld=srld):
            v.wait_ge(s_sc.h, g_done)
            v.wait_ge(s_sr.h, srld)
            s_v.inc(v.tensor_tensor(out=gb5[par][:], in0=gb5[par][:], in1=sres[par][:],
                                    op=OP.mult))
        r_done = s_v.n
        sres_free[0] = sres_free[1] = r_done

        def p5_tp(ob=ob, par=par, r_done=r_done, ubld=ubld):
            for tt in range(NT):
                tb5 = pt5[tt % 2]
                need_tp5 = tp5_free[tt % 2]

                @block.tensor
                def _(t, ob=ob, tt=tt, tb5=tb5, r_done=r_done, need_tp5=need_tp5):
                    t.wait_ge(s_v.h, max(r_done, need_tp5))
                    s_mm.inc(t.transpose(tb5[:, 0:128],
                                         gb5[par][:, tt * 128:(tt + 1) * 128],
                                         ident[:]))
                tpd5 = s_mm.n

                @BV
                def _(v, ob=ob, par=par, tt=tt, tb5=tb5, tpd5=tpd5, ubld=ubld):
                    v.wait_ge(s_mm.h, tpd5)
                    v.wait_ge(s_ub.h, ubld)
                    s_v.inc(v.tensor_tensor(out=ub2[par][:, tt * 128:(tt + 1) * 128],
                                            in0=tb5[:, 0:128],
                                            in1=ub2[par][:, tt * 128:(tt + 1) * 128],
                                            op=OP.add))
                tp5_free[tt % 2] = s_v.n
            ob_ready = s_v.n

            @BG
            def _(g, ob=ob, par=par, ob_ready=ob_ready):
                g.wait_ge(s_v.h, ob_ready)
                dma(g, s_out,
                    d_out.rearrange("(tt p) dd -> p tt dd", p=128)[:, :, ob * 128:(ob + 1) * 128],
                    ub2[par].ap().rearrange("p (tt o) -> p tt o", tt=NT))
            ub_free[par] = s_out.n
        return p5_tp

    tp5_thunk = None
    for ob in range(16):
        th = p5b_main(ob)
        if tp5_thunk is not None:
            tp5_thunk()
        tp5_thunk = th
    tp5_thunk()

    for cm in reversed(p5b_ctx):
        cm.__exit__(None, None, None)
    for cm in reversed(p5o_ctx):
        cm.__exit__(None, None, None)

    @BG
    def _(g):
        g.wait_ge(s_out.h, s_out.n)
        g.wait_ge(s_st.h, s_st.n)
        g.nop()

    blk_cm.__exit__(None, None, None)
    for cm, _t in reversed(ctx):
        cm.__exit__(None, None, None)
    nc.finalize()
    return nc, ["out"] + dbg_outs
